# version 28
# baseline (speedup 1.0000x reference)
import numpy as np
import ml_dtypes
import concourse.bass as bass
import concourse.mybir as mybir
from concourse.bass_utils import run_bass_kernel_spmd
from contextlib import ExitStack

F32 = mybir.dt.float32
BF16 = mybir.dt.bfloat16
Alu = mybir.AluOpType
Act = mybir.ActivationFunctionType
AxX = mybir.AxisListType.X

import concourse.dve_ops as _dvo
from concourse.dve_op import DveOp
from concourse.dve_spec import (Spec, lower as _dve_lower, Src0, Src1, C0, C1, C2, C3,
                                Zero, maxx, _spill_c3_to_src1)
from concourse.dve_table_gen import dve_ver_for

_VER = dve_ver_for("TRN2")


def _reg_op(name, spec, reference, row):
    if name in _dvo._SUB_OPCODE_FOR_NAME:
        for op in _dvo.OPS:
            if op.name == name:
                return op
    compiled = _dve_lower(spec, ver=_VER, perf_en=False, name=name)
    op = DveOp(name, spec, subdim=False, uops_sha={_VER: compiled.sha(_VER)},
               reference=reference)
    _dvo.OPS.append(op)
    _dvo._SUB_OPCODE_FOR_NAME[name] = row
    _dvo.CUSTOM_DVE_SPECS[name] = op.spec
    _dvo.CUSTOM_DVE_REFS[name] = op.reference
    return op


FP4U = _reg_op("FP4U", Spec(body=maxx(Src0, Zero - Src0) * Src1),
               lambda in0, in1, s0, s1, imm2: np.abs(in0) * in1, 29)
_a1 = Src0 > C0
_a2 = Src0 > C1
_a3 = Src0 > C2
_a4 = Src0 > C3
FP4A = _reg_op("FP4A", Spec(body=_spill_c3_to_src1((_a1 + _a2) + (_a3 + _a4))),
               lambda in0, in1, s0, s1, imm2: (
                   (in0 > s0).astype(in0.dtype) + (in0 > s1).astype(in0.dtype)
                   + (in0 > imm2).astype(in0.dtype) + (in0 > in1[:, :1]).astype(in0.dtype)),
               30)
_b5 = Src0 > C0
_b6 = Src0 > C1
_b7 = Src0 > C2
_bt = (_b5 + _b6) + (_b7 + _b7)
FP4B = _reg_op("FP4B", Spec(body=Src1 + (_bt + _bt)),
               lambda in0, in1, s0, s1, imm2: (
                   in1 + 2.0 * ((in0 > s0).astype(in0.dtype) + (in0 > s1).astype(in0.dtype)
                                + 2.0 * (in0 > imm2).astype(in0.dtype))),
               31)

T, D, NT = 2048, 1024, 16
I_FFN = 2816
FP32_EPS = 1.1920929e-07

OFF = dict(
    cL=0, csel=256, cident=768, crow=1024,
    ssq_b=5120, lsc_b=5152, inv_b=5184, ninv_b=5216,
    sigtmp=5248,
    cpfx=9216,
    c_i1024=9280, c_rmse=9312, c_lneps=9344, c_nhalf=9376, c_none=9408,
    c_one=9440, c_c3a=9472,
    st_sum=9504, st_sq=9568, st_mean=9632, st_var=9696, st_rstd=9760,
    ssq_s=9824, lsc_s=9856, inv_n=9888, ninv_n=9920, c_eps6=9984,
    q_bmax=10048, q_s=10752, q_r=11456, q_sh=12160,
    q_u=12864, q_p4=24128, q_g2=29760, q_sgn=35392, q_m=41024,
    aux1k=12864,
    pf8=12864,
    e1s0=21056, e1s1=25152,
    lnscr=29248,
    ebufA=33344, ebufB=37440,
    th=41536, tl=42560,
    y2c=43584,
    wslab=58368, wdslab=69632, qbf=80896, tstage=86528,
    wbuf=47680,
    kvbuf=80448,
    xnT=113216,
    win_kv=145984,
    y2T=145984,
    xc=178752,
    xnc=186944,
    win_q=178752,
    win_sw=80448,
    zc=43584, gs_a6=41536, zT=47680,
    win_o=12864, xc2=29248, scrA7=168512, x1buf=80448,
    xqc=47680, xqs0=51776, xqs1=53824, wg_win=55872, wu_win=145984,
    xqw=178752, hh=168512, gs2=174144, hqs=182848, hqt=185664,
    q_g2b=188480, q_shb=194112, q_mb=41536, gs2b=47680,
    wd_win=145984, hqw=47680, outb=58944,
)
ARENA = 196608


class Counts:
    def __init__(self):
        self.c = {}

    def add(self, k, by):
        self.c[k] = self.c.get(k, 0) + by
        return self.c[k]

    def val(self, k):
        return self.c.get(k, 0)


def build_program():
    nc = bass.Bass("TRN2", target_bir_lowering=False, debug=False,
                   detect_race_conditions=False)

    x_d = nc.dram_tensor("x", [T, D], F32, kind="ExternalInput").ap()
    wqkvT_d = nc.dram_tensor("wqkvT", [D, 3 * D], F32, kind="ExternalInput").ap()
    wswT_d = nc.dram_tensor("wswT", [D, 2 * D], F32, kind="ExternalInput").ap()
    woutT_d = nc.dram_tensor("woutT", [D, D], F32, kind="ExternalInput").ap()
    wg_sl_d = nc.dram_tensor("wg_slab", [128, I_FFN], F32, kind="ExternalInput").ap()
    wu_sl_d = nc.dram_tensor("wu_slab", [128, I_FFN], F32, kind="ExternalInput").ap()
    wd_sl_d = nc.dram_tensor("wd_slab", [128, I_FFN], F32, kind="ExternalInput").ap()
    cL_d = nc.dram_tensor("cL128", [128, 128], BF16, kind="ExternalInput").ap()
    csel_d = nc.dram_tensor("ccolsel", [128, 16, 16], BF16, kind="ExternalInput").ap()
    crow_d = nc.dram_tensor("crowsel", [16, 16, 128], BF16, kind="ExternalInput").ap()
    cpfx_d = nc.dram_tensor("cpfx17", [16, 16], BF16, kind="ExternalInput").ap()
    cid_d = nc.dram_tensor("cident", [128, 128], BF16, kind="ExternalInput").ap()
    out_d = nc.dram_tensor("out", [T, D], F32, kind="ExternalOutput").ap()

    ag_in_g = nc.dram_tensor("ag_in_g", [128, I_FFN], BF16).ap()
    ag_in_u = nc.dram_tensor("ag_in_u", [128, I_FFN], BF16).ap()
    ag_in_d = nc.dram_tensor("ag_in_d", [I_FFN, 128], BF16).ap()
    ag_g = nc.dram_tensor("ag_g", [D, I_FFN], BF16).ap()
    ag_u = nc.dram_tensor("ag_u", [D, I_FFN], BF16).ap()
    ag_d = nc.dram_tensor("ag_d", [8 * I_FFN, 128], BF16).ap()
    xqT_dr = nc.dram_tensor("xqT_dr", [NT, D, 128], BF16).ap()
    hqT_dr = nc.dram_tensor("hqT_dr", [NT, I_FFN, 128], BF16).ap()
    ag_d2 = nc.dram_tensor("ag_d2", [I_FFN, 8 * 128], BF16).ap()

    cnt = Counts()
    SY, GQ, MM, DV, AC, GP, CC = "sy", "gq", "mm", "dv", "ac", "gp", "cc"

    with ExitStack() as top:
        sem = {s: top.enter_context(nc.semaphore("sem_" + s))
               for s in (SY, GQ, MM, DV, AC, GP, CC)}
        arena = top.enter_context(nc.sbuf_tensor("arena", [128, ARENA], mybir.dt.uint8))
        base = nc.lookup_mloc(arena).addr

        def place(key, shape, dtype):
            h = nc.virtual_sbuf_tensor("t_" + key, shape, dtype)
            nc._tensor(h.name, h.shape, h.dtype, type="SB")
            mloc = nc.lookup_mloc(h)
            mloc.allocated = True
            mloc.addr = base + OFF[key]
            return h

        cL = place("cL", [128, 128], BF16)
        csel = place("csel", [128, 16, 16], BF16)
        cident = place("cident", [128, 128], BF16)
        crow = place("crow", [16, 16, 128], BF16)
        cpfx = place("cpfx", [16, 16], BF16)
        c_i1024 = place("c_i1024", [128, 1], F32)
        c_rmse = place("c_rmse", [128, 1], F32)
        c_lneps = place("c_lneps", [128, 1], F32)
        c_nhalf = place("c_nhalf", [128, 1], F32)
        c_none = place("c_none", [128, 1], F32)
        c_one = place("c_one", [128, 1], F32)
        c_c3a = place("c_c3a", [128, 1], F32)
        c_eps6 = place("c_eps6", [128, 1], F32)
        st_sum = place("st_sum", [128, 16], F32)
        st_sq = place("st_sq", [128, 16], F32)
        st_mean = place("st_mean", [128, 16], F32)
        st_var = place("st_var", [128, 16], F32)
        st_rstd = place("st_rstd", [128, 16], F32)
        ssq_s = place("ssq_s", [128, 1], F32)
        lsc_s = place("lsc_s", [128, 1], F32)
        inv_n = place("inv_n", [128, 1], F32)
        ninv_n = place("ninv_n", [128, 1], F32)
        ssq_b = place("ssq_b", [128, 1], F32)
        lsc_b = place("lsc_b", [128, 1], F32)
        inv_b = place("inv_b", [128, 1], F32)
        ninv_b = place("ninv_b", [128, 1], F32)
        q_bmax = place("q_bmax", [128, 176], F32)
        q_s = place("q_s", [128, 176], F32)
        q_r = place("q_r", [128, 176], F32)
        q_sh = place("q_sh", [128, 176], F32)
        q_u = place("q_u", [128, 2816], F32)
        q_p4 = place("q_p4", [128, 2816], BF16)
        q_g2 = place("q_g2", [128, 2816], BF16)
        q_sgn = place("q_sgn", [128, 2816], BF16)
        q_m = place("q_m", [128, 2816], BF16)
        q_g2b = place("q_g2b", [128, 2816], BF16)
        q_mb = place("q_mb", [128, 2816], BF16)
        q_shb = place("q_shb", [128, 176], F32)
        aux1k = place("aux1k", [128, 1024], F32)

        block = top.enter_context(nc.Block())

        @block.sync
        def _(sync):
            sync.dma_start(out=cL[:], in_=cL_d).then_inc(sem[SY], 16)
            sync.dma_start(out=csel[:], in_=csel_d).then_inc(sem[SY], 16)
            sync.dma_start(out=crow[:], in_=crow_d).then_inc(sem[SY], 16)
            sync.dma_start(out=cpfx[:], in_=cpfx_d).then_inc(sem[SY], 16)
            sync.dma_start(out=cident[:], in_=cid_d).then_inc(sem[SY], 16)
            cnt.add(SY, 80)

        @block.vector
        def _(vec):
            for buf, v in ((c_i1024, 1.0 / 1024.0), (c_rmse, FP32_EPS),
                           (c_lneps, 1e-5), (c_nhalf, -0.5), (c_none, -1.0),
                           (c_one, 1.0), (c_c3a, 1.75), (c_eps6, 1e-6)):
                vec.memset(buf[:], v)
            vec.memset(inv_n[:], 1.0).then_inc(sem[DV], 1)
        consts_dv = cnt.add(DV, 1)

        xc = place("xc", [128, 2, 1024], F32)
        win_kv = place("win_kv", [128, 8, 2048], BF16)

        @block.gpsimd
        def _(gp):
            for c in range(2):
                gp.dma_start(out=xc[:, c, :], in_=x_d[c * 128:(c + 1) * 128, :]
                             ).then_inc(sem[GQ], 16)
                cnt.add(GQ, 16)
            for dc in range(8):
                gp.dma_start(
                    out=win_kv[:, dc, :],
                    in_=wqkvT_d.rearrange("(dc p) e -> p dc e", p=128)[:, dc, 1024:3072]
                ).then_inc(sem[GQ], 16)
                cnt.add(GQ, 16)
        xgq = [16, 32] + [0] * 16
        winkv_gq = cnt.val(GQ)

        last_m = [(DV, 0)]

        last_gpset = {0: 0, 1: 0}

        def emit_quant(X_ap, X3_ap, out_ap, nb, F, dv_wait=(), ac_wait=(), gp_wait=(),
                       out_on_dve=False, m_on_gp=False, gset=0):
            g2t, mt, sht = ((q_g2, q_m, q_sh), (q_g2b, q_mb, q_shb))[gset]
            prev_gp = last_gpset[gset]
            prev_mk, prev_m = last_m[0]
            r_b = q_r[0:128, 0:nb].unsqueeze(2).broadcast_to([128, nb, 16])
            sh_b = sht[0:128, 0:nb].unsqueeze(2).broadcast_to([128, nb, 16])
            u_f = q_u[0:128, 0:F]
            u_3 = q_u[0:128, 0:F].rearrange("p (b k) -> p b k", k=16)

            @block.scalar
            def _(scal):
                for s, v in ac_wait:
                    scal.wait_ge(sem[s], v)
                if prev_m:
                    scal.wait_ge(sem[prev_mk], prev_m)
                scal.activation(q_sgn[0:128, 0:F], X_ap, Act.Sign).then_inc(sem[AC], 1)
            acv = cnt.add(AC, 1)

            @block.vector
            def _(vec):
                for s, v in dv_wait:
                    vec.wait_ge(sem[s], v)
                vec.tensor_reduce(q_bmax[0:128, 0:nb], X3_ap, AxX, Alu.max,
                                  apply_absolute_value=True).then_inc(sem[DV], 1)
                vec.wait_ge(sem[DV], cnt.add(DV, 1))
                vec.tensor_scalar(q_s[0:128, 0:nb], q_bmax[0:128, 0:nb],
                                  1.0 / 6.0, 1e-12, Alu.mult, Alu.max
                                  ).then_inc(sem[DV], 1)
                vec.wait_ge(sem[DV], cnt.add(DV, 1))
                vec.reciprocal(q_r[0:128, 0:nb], q_s[0:128, 0:nb]).then_inc(sem[DV], 1)
                vec.wait_ge(sem[DV], cnt.add(DV, 1))
                vec._custom_dve(FP4U, out=u_3, in0=X3_ap, in1=r_b)
                vec._custom_dve(FP4A, out=q_p4[0:128, 0:F], in0=u_f, in1=c_c3a[:],
                                s0=0.25, s1=0.75, imm2=1.25)
                if prev_gp:
                    vec.wait_ge(sem[GP], prev_gp)
                vec.tensor_scalar(sht[0:128, 0:nb], q_s[0:128, 0:nb],
                                  0.5, None, Alu.mult)
                m3 = mt[0:128, 0:F].rearrange("p (b k) -> p b k", k=16)
                s3 = q_sgn[0:128, 0:F].rearrange("p (b k) -> p b k", k=16)
                if m_on_gp:
                    vec._custom_dve(FP4B, out=g2t[0:128, 0:F], in0=u_f,
                                    in1=q_p4[0:128, 0:F],
                                    s0=2.5, s1=3.5, imm2=5.0).then_inc(sem[DV], 1)
                elif out_on_dve:
                    vec._custom_dve(FP4B, out=g2t[0:128, 0:F], in0=u_f,
                                    in1=q_p4[0:128, 0:F],
                                    s0=2.5, s1=3.5, imm2=5.0)
                    vec.wait_ge(sem[AC], acv)
                    vec.tensor_tensor(m3, s3, sh_b, Alu.mult)
                    for s, v in gp_wait:
                        vec.wait_ge(sem[s], v)
                    vec.tensor_tensor(out_ap, g2t[0:128, 0:F], mt[0:128, 0:F],
                                      Alu.mult).then_inc(sem[DV], 1)
                else:
                    vec._custom_dve(FP4B, out=g2t[0:128, 0:F], in0=u_f,
                                    in1=q_p4[0:128, 0:F],
                                    s0=2.5, s1=3.5, imm2=5.0)
                    vec.wait_ge(sem[AC], acv)
                    vec.tensor_tensor(m3, s3, sh_b, Alu.mult).then_inc(sem[DV], 1)
            dvv = cnt.add(DV, 1)
            if out_on_dve:
                last_m[0] = (DV, dvv)
                return (DV, dvv)

            if m_on_gp:
                @block.gpsimd
                def _(gp):
                    for s, v in gp_wait:
                        gp.wait_ge(sem[s], v)
                    gp.wait_ge(sem[DV], dvv)
                    gp.wait_ge(sem[AC], acv)
                    gp.tensor_tensor(
                        mt[0:128, 0:F].rearrange("p (b k) -> p b k", k=16),
                        q_sgn[0:128, 0:F].rearrange("p (b k) -> p b k", k=16),
                        sh_b, Alu.mult)
                    gp.tensor_tensor(out_ap, g2t[0:128, 0:F], mt[0:128, 0:F],
                                     Alu.mult).then_inc(sem[GP], 1)
                gpv = cnt.add(GP, 1)
                last_m[0] = (GP, gpv)
                last_gpset[gset] = gpv
                return (GP, gpv)

            last_m[0] = (DV, dvv)

            @block.gpsimd
            def _(gp):
                for s, v in gp_wait:
                    gp.wait_ge(sem[s], v)
                gp.wait_ge(sem[DV], dvv)
                gp.tensor_tensor(out_ap, g2t[0:128, 0:F], mt[0:128, 0:F],
                                 Alu.mult).then_inc(sem[GP], 1)
            gpv2 = cnt.add(GP, 1)
            last_gpset[gset] = gpv2
            return (GP, gpv2)

        wslab = place("wslab", [128, 22, 128], F32)
        wdslab = place("wdslab", [128, 22, 128], F32)
        qbf = place("qbf", [128, 2816], BF16)
        tstage = place("tstage", [128, 22, 128], BF16)

        @block.sync
        def _(sync):
            sync.dma_start(out=wslab[:],
                           in_=wg_sl_d.rearrange("p (it h) -> p it h", it=22)
                           ).then_inc(sem[SY], 16)
            cnt.add(SY, 16)
            sync.dma_start(out=wdslab[:],
                           in_=wu_sl_d.rearrange("p (it h) -> p it h", it=22)
                           ).then_inc(sem[SY], 16)
            cnt.add(SY, 16)
        wu_ld = cnt.val(SY)
        wg_ld = wu_ld - 16

        wsl_f = wslab[:].rearrange("p it h -> p (it h)")
        wsl_3 = wslab[:].rearrange("p it (b k) -> p (it b) k", k=16)
        wdl_f = wdslab[:].rearrange("p it i -> p (it i)")
        wdl_3 = wdslab[:].rearrange("p it (b k) -> p (it b) k", k=16)

        kg, vg = emit_quant(wsl_f, wsl_3, qbf[:], 176, 2816,
                            dv_wait=[(SY, wg_ld), (DV, consts_dv)],
                            ac_wait=[(SY, wg_ld)], out_on_dve=True)

        @block.sync
        def _(sync):
            sync.wait_ge(sem[kg], vg)
            sync.dma_start_transpose(
                out=tstage[:], in_=qbf[:].rearrange("p (it h) -> p it h", it=22)
            ).then_inc(sem[SY], 16)
            cnt.add(SY, 16)
            sync.wait_ge(sem[SY], cnt.val(SY))
            sync.dma_start(out=ag_in_g,
                           in_=tstage[:].rearrange("p it h -> p (it h)")
                           ).then_inc(sem[SY], 16)
            cnt.add(SY, 16)
            sync.dma_start(out=wslab[:],
                           in_=wd_sl_d.rearrange("p (it i) -> p it i", it=22)
                           ).then_inc(sem[SY], 16)
            cnt.add(SY, 16)
        wd_ld = cnt.val(SY)
        g_stored = wd_ld - 16
        gtr_sy = wd_ld - 32

        ku, vu = emit_quant(wdl_f, wdl_3, qbf[:], 176, 2816,
                            dv_wait=[(SY, wu_ld)], ac_wait=[(SY, wu_ld)],
                            gp_wait=[(SY, gtr_sy)], out_on_dve=True)

        @block.sync
        def _(sync):
            sync.wait_ge(sem[ku], vu)
            sync.dma_start_transpose(
                out=tstage[:], in_=qbf[:].rearrange("p (it h) -> p it h", it=22)
            ).then_inc(sem[SY], 16)
            cnt.add(SY, 16)
            sync.wait_ge(sem[SY], cnt.val(SY))
            sync.dma_start(out=ag_in_u,
                           in_=tstage[:].rearrange("p it h -> p (it h)")
                           ).then_inc(sem[SY], 16)
            cnt.add(SY, 16)
        u_stored = cnt.val(SY)
        utr_sy = u_stored - 16

        kd, vd = emit_quant(wsl_f, wsl_3, qbf[:], 176, 2816,
                            dv_wait=[(SY, wd_ld)], ac_wait=[(SY, wd_ld)],
                            gp_wait=[(SY, utr_sy)], out_on_dve=True)

        @block.sync
        def _(sync):
            sync.wait_ge(sem[kd], vd)
            sync.dma_start_transpose(
                out=tstage[:], in_=qbf[:].rearrange("p (it i) -> p it i", it=22)
            ).then_inc(sem[SY], 16)
            cnt.add(SY, 16)
            sync.wait_ge(sem[SY], cnt.val(SY))
            sync.dma_start(out=ag_in_d.rearrange("(it q) h -> q it h", q=128),
                           in_=tstage[:]).then_inc(sem[SY], 16)
            cnt.add(SY, 16)
        d_stored = cnt.val(SY)

        xnc = place("xnc", [128, 2, 1024], BF16)
        xnT = place("xnT", [128, 8, 2048], BF16)
        wbuf = place("wbuf", [128, 16, 1024], BF16)
        kvbuf = place("kvbuf", [128, 16, 1024], BF16)

        a2_ac = [0] * (NT + 1)
        a2ts = [0] * (NT + 1)
        a2Tsy = [0] * (NT + 1)
        xsy2 = [0] * (NT + 2)
        a4_ac = [0] * (NT + 1)
        kvtt = [0] * (NT + 1)

        def emit_a2_chunk(c):
            @block.vector
            def _(vec, c=c, pa=a2_ac[c - 1] if c >= 1 else 0, px=xsy2[c]):
                if c < 2:
                    vec.wait_ge(sem[GQ], xgq[c])
                else:
                    vec.wait_ge(sem[SY], px)
                if c == 0:
                    vec.wait_ge(sem[DV], consts_dv)
                else:
                    vec.wait_ge(sem[AC], pa)
                vec.tensor_tensor_reduce(
                    out=aux1k[:], in0=xc[:, c % 2, :], in1=xc[:, c % 2, :],
                    scale=1.0, scalar=0.0, op0=Alu.mult, op1=Alu.add,
                    accum_out=ssq_s[:]).then_inc(sem[DV], 1)
            a2ttr = cnt.add(DV, 1)

            @block.scalar
            def _(scal, c=c, a2ttr=a2ttr, pts=a2ts[c - 1] if c >= 1 else 0):
                scal.wait_ge(sem[DV], a2ttr)
                if pts:
                    scal.wait_ge(sem[DV], pts)
                scal.activation(lsc_s[:], ssq_s[:], Act.Ln,
                                scale=c_i1024[:], bias=c_rmse[:]).then_inc(sem[AC], 1)
                scal.wait_ge(sem[AC], cnt.add(AC, 1))
                scal.activation(inv_n[:], lsc_s[:], Act.Exp,
                                scale=c_nhalf[:]).then_inc(sem[AC], 1)
            a2_ac[c] = cnt.add(AC, 1)

            @block.vector
            def _(vec, c=c, acv=a2_ac[c], pT=a2Tsy[c - 2] if c >= 2 else 0):
                vec.wait_ge(sem[AC], acv)
                if pT:
                    vec.wait_ge(sem[SY], pT)
                vec.tensor_scalar(xnc[:, c % 2, :], xc[:, c % 2, :],
                                  inv_n[:], None, Alu.mult).then_inc(sem[DV], 1)
            a2ts[c] = cnt.add(DV, 1)

            @block.sync
            def _(sync, c=c, pts=a2ts[c]):
                sync.wait_ge(sem[DV], pts)
                sync.dma_start_transpose(
                    out=xnT[:, :, c * 128:(c + 1) * 128],
                    in_=xnc[:, c % 2, :].rearrange("p (a b) -> p a b", a=8)
                ).then_inc(sem[SY], 16)
                a2Tsy[c] = cnt.add(SY, 16)
                if c + 2 < NT:
                    sync.dma_start(out=xc[:, c % 2, :],
                                   in_=x_d[(c + 2) * 128:(c + 3) * 128, :]
                                   ).then_inc(sem[SY], 16)
                    xsy2[c + 2] = cnt.add(SY, 16)

        def emit_a4_chunk(c):
            b = c % 2

            @block.tensor
            def _(ten, c=c, b=b, pT=a2Tsy[c]):
                ten.wait_ge(sem[SY], pT)
                if c == 0:
                    ten.wait_ge(sem[GQ], winkv_gq)
                    ten.wait_ge(sem[SY], d_stored)
                if c >= 2:
                    ten.wait_ge(sem[DV], kvtt[c - 2])
                last = None
                for dc in range(8):
                    st = xnT[:, dc, c * 128:(c + 1) * 128]
                    for n in range(2):
                        last = ten.matmul(k_ps[b][:, n * 512:(n + 1) * 512], st,
                                          win_kv[:, dc, n * 512:(n + 1) * 512],
                                          start=(dc == 0), stop=(dc == 7))
                    for n in range(2):
                        last = ten.matmul(v_ps[b][:, n * 512:(n + 1) * 512], st,
                                          win_kv[:, dc, 1024 + n * 512:1024 + (n + 1) * 512],
                                          start=(dc == 0), stop=(dc == 7))
                last.then_inc(sem[MM], 1)
            mmv = cnt.add(MM, 1)

            @block.scalar
            def _(scal, c=c, b=b, mmv=mmv):
                scal.wait_ge(sem[MM], mmv)
                scal.activation(aux1k[:], k_ps[b][:], Act.Square,
                                accum_out=ssq_b[:]).then_inc(sem[AC], 1)
                scal.wait_ge(sem[AC], cnt.add(AC, 1))
                scal.activation(lsc_b[:], ssq_b[:], Act.Ln,
                                scale=c_i1024[:], bias=c_rmse[:]).then_inc(sem[AC], 1)
                scal.wait_ge(sem[AC], cnt.add(AC, 1))
                scal.activation(inv_b[:], lsc_b[:], Act.Exp, scale=c_nhalf[:]
                                ).then_inc(sem[AC], 1)
                scal.wait_ge(sem[AC], cnt.add(AC, 1))
                scal.activation(wbuf[:, c, :], k_ps[b][:], Act.Exp,
                                scale=inv_b[:]).then_inc(sem[AC], 1)
            a4_ac[c] = cnt.add(AC, 1)

            @block.vector
            def _(vec, c=c, b=b, acv=a4_ac[c]):
                vec.wait_ge(sem[AC], acv)
                vec.tensor_tensor(kvbuf[:, c, :], wbuf[:, c, :], v_ps[b][:],
                                  Alu.mult).then_inc(sem[DV], 1)
            kvtt[c] = cnt.add(DV, 1)

        with ExitStack() as ph:
            k_ps = [ph.enter_context(nc.psum_tensor("k_ps%d" % i, [128, 1024], F32))
                    for i in range(2)]
            v_ps = [ph.enter_context(nc.psum_tensor("v_ps%d" % i, [128, 1024], F32))
                    for i in range(2)]
            for c in range(NT):
                emit_a2_chunk(c)
                if c >= 1:
                    emit_a4_chunk(c - 1)
            emit_a4_chunk(NT - 1)
        a4_all_dv = kvtt[NT - 1]
        a4_all_ac = a4_ac[NT - 1]
        xn_done_sy = a2Tsy[NT - 1]

        @block.gpsimd
        def _(gp):
            gp.wait_ge(sem[SY], g_stored)
            gp.collective_compute("AllGather", Alu.bypass,
                                  replica_groups=[list(range(8))],
                                  ins=[ag_in_g.opt()], outs=[ag_g.opt()]
                                  ).then_inc(sem[CC], 1)
            cnt.add(CC, 1)
            gp.wait_ge(sem[SY], u_stored)
            gp.collective_compute("AllGather", Alu.bypass,
                                  replica_groups=[list(range(8))],
                                  ins=[ag_in_u.opt()], outs=[ag_u.opt()]
                                  ).then_inc(sem[CC], 1)
            cnt.add(CC, 1)
            gp.wait_ge(sem[SY], d_stored)
            gp.collective_compute("AllGather", Alu.bypass,
                                  replica_groups=[list(range(8))],
                                  ins=[ag_in_d.opt()], outs=[ag_d.opt()]
                                  ).then_inc(sem[CC], 1)
            cnt.add(CC, 1)
        cc_done = cnt.val(CC)

        win_q = place("win_q", [128, 8, 1024], BF16)

        @block.gpsimd
        def _(gp):
            gp.wait_ge(sem[SY], xn_done_sy)
            for dc in range(8):
                gp.dma_start(
                    out=win_q[:, dc, :],
                    in_=wqkvT_d.rearrange("(dc p) e -> p dc e", p=128)[:, dc, 0:1024]
                ).then_inc(sem[GQ], 16)
                cnt.add(GQ, 16)
        winq_gq = cnt.val(GQ)


        pf8 = place("pf8", [16, 8, 512], BF16)
        th = place("th", [16, 512], BF16)
        tl = place("tl", [16, 512], BF16)
        e1s = [place("e1s0", [128, 1024], F32), place("e1s1", [128, 1024], F32)]
        sigtmp = place("sigtmp", [128, 1024], BF16)
        lnscr = place("lnscr", [128, 1024], F32)
        ebuf = [place("ebufA", [128, 1024], F32), place("ebufB", [128, 1024], F32)]
        y2c = place("y2c", [128, 2, 1024], BF16)
        y2T = place("y2T", [128, 8, 2048], BF16)

        mmcs = [0] * (NT + 1)
        with ExitStack() as ph:
            with ExitStack() as php:
                tot_ps = php.enter_context(nc.psum_tensor("tot_ps", [16, 512], F32))
                pfx_ps = php.enter_context(nc.psum_tensor("pfx_ps", [16, 512], F32))
                for sl in range(2):
                    for ti, buf in enumerate((wbuf, kvbuf)):
                        j = (sl * 2 + ti) * 2

                        @block.tensor
                        def _(ten, sl=sl, buf=buf):
                            ten.wait_ge(sem[DV], a4_all_dv)
                            ten.wait_ge(sem[AC], a4_all_ac)
                            last = None
                            for c in range(16):
                                last = ten.matmul(tot_ps[:], csel[:, c, :],
                                                  buf[:, c, sl * 512:(sl + 1) * 512],
                                                  start=(c == 0), stop=(c == 15))
                            last.then_inc(sem[MM], 1)
                        mmt = cnt.add(MM, 1)

                        @block.vector
                        def _(vec, mmt=mmt):
                            vec.wait_ge(sem[MM], mmt)
                            vec.tensor_copy(th[:], tot_ps[:]).then_inc(sem[DV], 1)
                            vec.wait_ge(sem[DV], cnt.add(DV, 1))
                            vec.tensor_tensor(tl[:], tot_ps[:], th[:], Alu.subtract
                                              ).then_inc(sem[DV], 1)
                        dvv = cnt.add(DV, 1)

                        @block.tensor
                        def _(ten, dvv=dvv):
                            ten.wait_ge(sem[DV], dvv)
                            ten.matmul(pfx_ps[:], cpfx[:], th[:], start=True, stop=False)
                            ten.matmul(pfx_ps[:], cpfx[:], tl[:], start=False, stop=True
                                       ).then_inc(sem[MM], 1)
                        mmv2 = cnt.add(MM, 1)

                        @block.vector
                        def _(vec, mmv2=mmv2, j=j):
                            vec.wait_ge(sem[MM], mmv2)
                            vec.tensor_copy(pf8[:, j, :], pfx_ps[:]).then_inc(sem[DV], 1)
                            vec.wait_ge(sem[DV], cnt.add(DV, 1))
                            vec.tensor_tensor(pf8[:, j + 1, :], pfx_ps[:], pf8[:, j, :],
                                              Alu.subtract).then_inc(sem[DV], 1)
                        cnt.add(DV, 1)
            pf_dv = cnt.val(DV)

            cw_ps = ph.enter_context(nc.psum_tensor("cw_ps", [128, 1024], F32))
            ckv_ps = ph.enter_context(nc.psum_tensor("ckv_ps", [128, 1024], F32))
            q_ps = ph.enter_context(nc.psum_tensor("q_ps", [128, 1024], F32))

            yr_dv = [0] * (NT + 1)
            fin_dv = [0] * (NT + 1)
            ln_ac = [0] * (NT + 1)
            e1_ac = [0] * (NT + 1)
            sig_ac = [0] * (NT + 1)
            inv_ac = [0] * (NT + 1)
            y2Tsy = [0] * (NT + 1)

            for c in range(NT):
                @block.tensor
                def _(ten, c=c):
                    if c == 0:
                        ten.wait_ge(sem[DV], pf_dv)
                    else:
                        ten.wait_ge(sem[DV], yr_dv[c - 1])
                    for sl in range(2):
                        ten.matmul(cw_ps[:, sl * 512:(sl + 1) * 512], cL[:],
                                   wbuf[:, c, sl * 512:(sl + 1) * 512],
                                   start=True, stop=False)
                        ten.matmul(ckv_ps[:, sl * 512:(sl + 1) * 512], cL[:],
                                   kvbuf[:, c, sl * 512:(sl + 1) * 512],
                                   start=True, stop=False)
                    last = None
                    for sl in range(2):
                        j = (sl * 2) * 2
                        jk = (sl * 2 + 1) * 2
                        ten.matmul(cw_ps[:, sl * 512:(sl + 1) * 512], crow[:, c, :],
                                   pf8[:, j, :], start=False, stop=False)
                        ten.matmul(cw_ps[:, sl * 512:(sl + 1) * 512], crow[:, c, :],
                                   pf8[:, j + 1, :], start=False, stop=True)
                        ten.matmul(ckv_ps[:, sl * 512:(sl + 1) * 512], crow[:, c, :],
                                   pf8[:, jk, :], start=False, stop=False)
                        last = ten.matmul(ckv_ps[:, sl * 512:(sl + 1) * 512],
                                          crow[:, c, :], pf8[:, jk + 1, :],
                                          start=False, stop=True)
                    last.then_inc(sem[MM], 1)
                mmcs[c] = cnt.add(MM, 1)

                @block.tensor
                def _(ten, c=c):
                    if c == 0:
                        ten.wait_ge(sem[GQ], winq_gq)
                    else:
                        ten.wait_ge(sem[AC], e1_ac[c - 1])
                    last = None
                    for dc in range(8):
                        st = xnT[:, dc, c * 128:(c + 1) * 128]
                        for n in range(2):
                            last = ten.matmul(q_ps[:, n * 512:(n + 1) * 512], st,
                                              win_q[:, dc, n * 512:(n + 1) * 512],
                                              start=(dc == 0), stop=(dc == 7))
                    last.then_inc(sem[MM], 1)
                mmq = cnt.add(MM, 1)

                @block.scalar
                def _(scal, c=c, mmq=mmq, yw=yr_dv[c - 2] if c >= 2 else 0,
                      fw=fin_dv[c - 1] if c >= 1 else 0,
                      sw=sig_ac[c - 1] if c >= 1 else 0):
                    scal.wait_ge(sem[MM], mmq)
                    if yw:
                        scal.wait_ge(sem[DV], yw)
                    if fw:
                        scal.wait_ge(sem[DV], fw)
                    if sw:
                        scal.wait_ge(sem[AC], sw)
                    scal.activation(e1s[c % 2][:], q_ps[:], Act.Square,
                                    accum_out=ssq_b[:])
                    scal.activation(lnscr[:], cw_ps[:], Act.Ln, bias=c_eps6[:])
                    scal.activation(ebuf[c % 2][:], lnscr[:], Act.Exp,
                                    scale=c_none[:]).then_inc(sem[AC], 1)
                    cnt.add(AC, 1)
                    scal.activation(lsc_b[:], ssq_b[:], Act.Ln,
                                    scale=c_i1024[:], bias=c_rmse[:]).then_inc(sem[AC], 1)
                    scal.wait_ge(sem[AC], cnt.add(AC, 1))
                    scal.activation(inv_b[:], lsc_b[:], Act.Exp, scale=c_nhalf[:]
                                    ).then_inc(sem[AC], 1)
                    scal.wait_ge(sem[AC], cnt.add(AC, 1))
                    scal.activation(e1s[c % 2][:], q_ps[:], Act.Exp, scale=inv_b[:]
                                    ).then_inc(sem[AC], 1)
                    cnt.add(AC, 1)
                    scal.activation(lnscr[:], e1s[c % 2][:], Act.Ln, bias=c_one[:])
                    scal.activation(e1s[c % 2][:], lnscr[:], Act.Exp, scale=c_none[:]
                                    ).then_inc(sem[AC], 1)
                sig_ac[c] = cnt.add(AC, 1)
                e1_ac[c] = sig_ac[c] - 1
                ln_ac[c] = sig_ac[c] - 4

                @block.vector
                def _(vec, c=c, la=ln_ac[c], pT=y2Tsy[c - 2] if c >= 2 else 0):
                    vec.wait_ge(sem[AC], la)
                    if pT:
                        vec.wait_ge(sem[SY], pT)
                    vec.tensor_tensor(y2c[:, c % 2, :], ckv_ps[:], ebuf[c % 2][:],
                                      Alu.mult).then_inc(sem[DV], 1)
                yr_dv[c] = cnt.add(DV, 1)

                @block.vector
                def _(vec, c=c, sa=sig_ac[c]):
                    vec.wait_ge(sem[AC], sa)
                    vec.tensor_tensor(lnscr[:], y2c[:, c % 2, :], e1s[c % 2][:],
                                      Alu.mult)
                    vec.tensor_tensor(y2c[:, c % 2, :], y2c[:, c % 2, :], lnscr[:],
                                      Alu.subtract).then_inc(sem[DV], 1)
                fin_dv[c] = cnt.add(DV, 1)

                @block.sync
                def _(sync, c=c, fv=fin_dv[c]):
                    sync.wait_ge(sem[DV], fv)
                    sync.dma_start_transpose(
                        out=y2T[:, :, c * 128:(c + 1) * 128],
                        in_=y2c[:, c % 2, :].rearrange("p (a b) -> p a b", a=8)
                    ).then_inc(sem[SY], 16)
                    cnt.add(SY, 16)
                y2Tsy[c] = cnt.val(SY)
        y2_dv = fin_dv[NT - 1]
        a5q_mm_end = cnt.val(MM)

        win_sw = place("win_sw", [128, 8, 2048], BF16)

        @block.gpsimd
        def _(gp):
            for dc in range(8):
                gp.wait_ge(sem[MM], mmcs[2 * dc + 1])
                gp.dma_start(
                    out=win_sw[:, dc, :],
                    in_=wswT_d.rearrange("(dc p) e -> p dc e", p=128)[:, dc, :]
                ).then_inc(sem[GQ], 16)
                cnt.add(GQ, 16)
        winsw_gq = cnt.val(GQ)

        win_o = place("win_o", [128, 8, 1024], BF16)
        xc2 = place("xc2", [128, 2, 1024], F32)

        @block.gpsimd
        def _(gp):
            gp.wait_ge(sem[MM], a5q_mm_end)
            gp.wait_ge(sem[DV], fin_dv[NT - 1])
            for dc in range(8):
                gp.dma_start(
                    out=win_o[:, dc, :],
                    in_=woutT_d.rearrange("(dc p) e -> p dc e", p=128)[:, dc, :]
                ).then_inc(sem[GQ], 16)
                cnt.add(GQ, 16)
        wino_gq = cnt.val(GQ)

        x2gq = [0] * (NT + 1)

        @block.gpsimd
        def _(gp):
            for c in range(2):
                gp.dma_start(out=xc2[:, c, :], in_=x_d[c * 128:(c + 1) * 128, :]
                             ).then_inc(sem[GQ], 16)
                x2gq[c] = cnt.add(GQ, 16)

        @block.gpsimd
        def _(gp):
            gp.wait_ge(sem[CC], cc_done)
            for hs in range(8):
                gp.dma_start(
                    out=ag_d2.rearrange("i (hs hl) -> i hs hl", hs=8)[:, hs, :],
                    in_=ag_d[hs * I_FFN:(hs + 1) * I_FFN, :]).then_inc(sem[GQ], 16)
                cnt.add(GQ, 16)
        agd2_gq = cnt.val(GQ)

        gs_a6 = place("gs_a6", [128, 1024], BF16)
        zc = place("zc", [128, 2, 1024], BF16)
        zT = place("zT", [128, 8, 2048], BF16)

        ztt = [0] * (NT + 1)
        zTsy = [0] * (NT + 1)
        silu6 = [0] * (NT + 1)
        with ExitStack() as ph:
            uv_ps = [ph.enter_context(nc.psum_tensor("uv_ps%d" % i, [128, 2048], F32))
                     for i in range(2)]
            for c in range(NT):
                b = c % 2

                @block.tensor
                def _(ten, c=c, b=b, pT=y2Tsy[c]):
                    ten.wait_ge(sem[SY], pT)
                    if c == 0:
                        ten.wait_ge(sem[DV], y2_dv)
                        ten.wait_ge(sem[GQ], winsw_gq)
                    if c >= 2:
                        ten.wait_ge(sem[DV], ztt[c - 2])
                    last = None
                    for dc in range(8):
                        st = y2T[:, dc, c * 128:(c + 1) * 128]
                        for n in range(4):
                            last = ten.matmul(uv_ps[b][:, n * 512:(n + 1) * 512], st,
                                              win_sw[:, dc, n * 512:(n + 1) * 512],
                                              start=(dc == 0), stop=(dc == 7))
                    last.then_inc(sem[MM], 1)
                mmv = cnt.add(MM, 1)

                @block.scalar
                def _(scal, mmv=mmv, b=b, pz=ztt[c - 1] if c >= 1 else 0):
                    scal.wait_ge(sem[MM], mmv)
                    if pz:
                        scal.wait_ge(sem[DV], pz)
                    scal.activation(gs_a6[:], uv_ps[b][:, 1024:2048], Act.Silu
                                    ).then_inc(sem[AC], 1)
                silu6[c] = cnt.add(AC, 1)

                @block.vector
                def _(vec, c=c, b=b, acv=silu6[c],
                      pT=zTsy[c - 2] if c >= 2 else 0, y15=y2Tsy[NT - 1]):
                    vec.wait_ge(sem[AC], acv)
                    if c <= 1:
                        vec.wait_ge(sem[SY], y15)
                    if pT:
                        vec.wait_ge(sem[SY], pT)
                    vec.tensor_tensor(zc[:, c % 2, :], uv_ps[b][:, 0:1024],
                                      gs_a6[:], Alu.mult).then_inc(sem[DV], 1)
                ztt[c] = cnt.add(DV, 1)

                @block.sync
                def _(sync, c=c, zv=ztt[c]):
                    sync.wait_ge(sem[DV], zv)
                    sync.dma_start_transpose(
                        out=zT[:, :, c * 128:(c + 1) * 128],
                        in_=zc[:, c % 2, :].rearrange("p (a b) -> p a b", a=8)
                    ).then_inc(sem[SY], 16)
                    cnt.add(SY, 16)
                zTsy[c] = cnt.val(SY)
        z_dv = ztt[NT - 1]
        a6_mm_end = cnt.val(MM)

        scrA7 = place("scrA7", [128, 1024], F32)
        x1buf = place("x1buf", [128, 16, 1024], F32)

        x1tt = [0] * (NT + 1)
        with ExitStack() as ph:
            a_ps = [ph.enter_context(nc.psum_tensor("a_ps%d" % i, [128, 1024], F32))
                    for i in range(4)]
            for c in range(NT):
                b = c % 4

                @block.tensor
                def _(ten, c=c, b=b, pT=zTsy[c]):
                    ten.wait_ge(sem[SY], pT)
                    if c == 0:
                        ten.wait_ge(sem[GQ], wino_gq)
                    if c <= 3:
                        ten.wait_ge(sem[DV], z_dv)
                    if c >= 4:
                        ten.wait_ge(sem[DV], x1tt[c - 4])
                    last = None
                    for dc in range(8):
                        st = zT[:, dc, c * 128:(c + 1) * 128]
                        for n in range(2):
                            last = ten.matmul(a_ps[b][:, n * 512:(n + 1) * 512], st,
                                              win_o[:, dc, n * 512:(n + 1) * 512],
                                              start=(dc == 0), stop=(dc == 7))
                    last.then_inc(sem[MM], 1)
                mmv = cnt.add(MM, 1)

                @block.vector
                def _(vec, c=c, b=b, mmv=mmv, xg=x2gq[c]):
                    vec.wait_ge(sem[MM], mmv)
                    vec.wait_ge(sem[GQ], xg)
                    vec.scalar_tensor_tensor(x1buf[:, c, :], xc2[:, c % 2, :],
                                             2.0, a_ps[b][:], Alu.mult, Alu.add
                                             ).then_inc(sem[DV], 1)
                x1tt[c] = cnt.add(DV, 1)

                if c + 2 < NT:
                    @block.gpsimd
                    def _(gp, c=c, px=x1tt[c]):
                        gp.wait_ge(sem[DV], px)
                        gp.dma_start(out=xc2[:, c % 2, :],
                                     in_=x_d[(c + 2) * 128:(c + 3) * 128, :]
                                     ).then_inc(sem[GQ], 16)
                        x2gq[c + 2] = cnt.add(GQ, 16)

                @block.scalar
                def _(scal, c=c, px=x1tt[c]):
                    scal.wait_ge(sem[DV], px)
                    scal.activation(scrA7[:], x1buf[:, c, :], Act.Copy,
                                    accum_out=st_sum[:, c:c + 1]).then_inc(sem[AC], 1)
                cnt.add(AC, 1)

                @block.vector
                def _(vec, c=c, px=x1tt[c]):
                    vec.wait_ge(sem[DV], px)
                    vec.tensor_tensor_reduce(
                        out=scrA7[:], in0=x1buf[:, c, :], in1=x1buf[:, c, :],
                        scale=1.0, scalar=0.0, op0=Alu.mult, op1=Alu.add,
                        accum_out=st_sq[:, c:c + 1]).then_inc(sem[DV], 1)
                cnt.add(DV, 1)
        a7_ac = cnt.val(AC)
        a7_dv2 = cnt.val(DV)
        a7_mm_end = cnt.val(MM)

        def emit_ln_batch(prev_ac_total, prev_dv_total, a=0, b=16):
            @block.vector
            def _(vec):
                vec.wait_ge(sem[AC], prev_ac_total)
                vec.wait_ge(sem[DV], prev_dv_total)
                vec.tensor_scalar(st_mean[:, a:b], st_sum[:, a:b],
                                  1.0 / 1024.0, None, Alu.mult).then_inc(sem[DV], 1)
                vec.wait_ge(sem[DV], cnt.add(DV, 1))
                vec.tensor_scalar(st_var[:, a:b], st_sq[:, a:b],
                                  1.0 / 1024.0, None, Alu.mult).then_inc(sem[DV], 1)
                vec.wait_ge(sem[DV], cnt.add(DV, 1))
                vec.tensor_tensor(st_sq[:, a:b], st_mean[:, a:b], st_mean[:, a:b],
                                  Alu.mult).then_inc(sem[DV], 1)
                vec.wait_ge(sem[DV], cnt.add(DV, 1))
                vec.tensor_tensor(st_var[:, a:b], st_var[:, a:b], st_sq[:, a:b],
                                  Alu.subtract).then_inc(sem[DV], 1)
            dvv = cnt.add(DV, 1)

            @block.scalar
            def _(scal, dvv=dvv):
                scal.wait_ge(sem[DV], dvv)
                scal.activation(st_sq[:, a:b], st_var[:, a:b], Act.Ln, bias=c_lneps[:]
                                ).then_inc(sem[AC], 1)
                scal.wait_ge(sem[AC], cnt.add(AC, 1))
                scal.activation(st_rstd[:, a:b], st_sq[:, a:b], Act.Exp,
                                scale=c_nhalf[:]).then_inc(sem[AC], 1)
            return cnt.add(AC, 1)

        rstd_ac = emit_ln_batch(a7_ac, a7_dv2)

        x1ap = [0] * (NT + 1)

        @block.vector
        def _(vec):
            vec.wait_ge(sem[AC], rstd_ac)
            for c in range(NT):
                vec.tensor_scalar(x1buf[:, c, :], x1buf[:, c, :],
                                  st_mean[:, c:c + 1], st_rstd[:, c:c + 1],
                                  Alu.subtract, Alu.mult).then_inc(sem[DV], 1)
                x1ap[c] = cnt.add(DV, 1)
        x1_dv = x1ap[NT - 1]

        wg_win = place("wg_win", [128, 8, 1408], BF16)
        wu_win = place("wu_win", [128, 8, 1408], BF16)

        @block.gpsimd
        def _(gp):
            gp.wait_ge(sem[CC], cc_done)
            gp.wait_ge(sem[MM], a6_mm_end)
            for dc in range(8):
                gp.dma_start(out=wu_win[:, dc, :],
                             in_=ag_u.rearrange("(dc p) i -> p dc i", p=128)
                             [:, dc, 0:1408]).then_inc(sem[GQ], 16)
                cnt.add(GQ, 16)
            gp.wait_ge(sem[MM], a7_mm_end)
            for dc in range(8):
                gp.dma_start(out=wg_win[:, dc, :],
                             in_=ag_g.rearrange("(dc p) i -> p dc i", p=128)
                             [:, dc, 0:1408]).then_inc(sem[GQ], 16)
                cnt.add(GQ, 16)
        wgwu_gq = cnt.val(GQ)

        xqc = place("xqc", [128, 2, 1024], BF16)
        xqs = [place("xqs0", [128, 8, 128], BF16), place("xqs1", [128, 8, 128], BF16)]
        xqw = place("xqw", [128, 2, 8, 128], BF16)
        wd_win = place("wd_win", [128, 22, 1024], BF16)
        hqw = place("hqw", [128, 2, 22, 128], BF16)
        outb = place("outb", [128, 2, 1024], F32)
        hq_sy = [0] * (NT + 1)
        wd_sy = [0] * 22
        hh = place("hh", [128, 1408], F32)
        gs2 = place("gs2", [128, 1408], BF16)
        gs2b = place("gs2b", [128, 1408], BF16)
        hqs = place("hqs", [128, 11, 128], BF16)
        hqt = place("hqt", [128, 11, 128], BF16)

        f1Tsy = [0] * (NT + 1)
        f1store = [0] * (NT + 1)

        def emit_f1_pair(cp):
            w0 = [(DV, x1ap[cp + 1])]
            gw = [(SY, f1Tsy[cp - 1])] if cp >= 2 else []
            qk, gpq = emit_quant(
                x1buf[:, cp:cp + 2, :].rearrange("p a d -> p (a d)"),
                x1buf[:, cp:cp + 2, :].rearrange("p a (b k) -> p (a b) k", k=16),
                xqc[:, :, :].rearrange("p a d -> p (a d)"), 128, 2048,
                dv_wait=w0, ac_wait=w0, gp_wait=gw, out_on_dve=True)

            @block.sync
            def _(sync, cp=cp, qk=qk, gpq=gpq,
                  ps=f1store[cp - 1] if cp >= 2 else 0):
                sync.wait_ge(sem[qk], gpq)
                if ps:
                    sync.wait_ge(sem[SY], ps)
                for c in (cp, cp + 1):
                    sync.dma_start_transpose(
                        out=xqs[c % 2][:],
                        in_=xqc[:, c % 2, :].rearrange("p (a b) -> p a b", a=8)
                    ).then_inc(sem[SY], 16)
                    f1Tsy[c] = cnt.add(SY, 16)
                    sync.wait_ge(sem[SY], f1Tsy[c])
                    sync.dma_start(
                        out=xqT_dr[c].rearrange("(dc p) t -> p dc t", p=128),
                        in_=xqs[c % 2][:]).then_inc(sem[SY], 16)
                    f1store[c] = cnt.add(SY, 16)

        f2w_sy = [0, 0]
        f2w_s = [0] * 8
        f2w_g = [0] * 8
        SLICES = ((0, 512), (512, 1024), (1024, 1408))

        def emit_f2_chunk(ih, c, S):
            @block.tensor
            def _(ten, c=c, ih=ih, pq=S["xq_sy"][c], pa=S["silu_ac"][c - 1] if c >= 1 else 0,
                  ws=list(f2w_s), wg=list(f2w_g), psil=S.get("prev_sil", 0)):
                ten.wait_ge(sem[SY], pq)
                if c == 0 and ih == 0:
                    ten.wait_ge(sem[GQ], wgwu_gq)
                if c == 0 and psil:
                    ten.wait_ge(sem[AC], psil)
                if c <= 1 and ih == 1:
                    ten.wait_ge(sem[SY], ws[3])
                    ten.wait_ge(sem[GQ], wg[7])
                if pa:
                    ten.wait_ge(sem[AC], pa)
                last = None
                for dc in range(8):
                    st = xqw[:, c % 2, dc, :]
                    for (a, b2) in SLICES:
                        last = ten.matmul(g_ps[:, a:b2], st, wg_win[:, dc, a:b2],
                                          start=(dc == 0), stop=(dc == 7))
                last.then_inc(sem[MM], 1)
            mmg = cnt.add(MM, 1)

            @block.tensor
            def _(ten, c=c, ph_=S["hh_dv"][c - 1] if c >= 1 else 0,
                  phh=S.get("prev_hh", 0)):
                if ph_:
                    ten.wait_ge(sem[DV], ph_)
                if c == 0 and phh:
                    ten.wait_ge(sem[DV], phh)
                last = None
                for dc in range(8):
                    st = xqw[:, c % 2, dc, :]
                    for (a, b2) in SLICES:
                        last = ten.matmul(u_ps[:, a:b2], st, wu_win[:, dc, a:b2],
                                          start=(dc == 0), stop=(dc == 7))
                last.then_inc(sem[MM], 1)
            mmu = cnt.add(MM, 1)

            if c + 2 < NT:
                @block.sync
                def _(sync, c=c, ih=ih, mmu=mmu,
                      fs=f1store[c + 2] if ih == 0 else 0):
                    sync.wait_ge(sem[MM], mmu)
                    if fs:
                        sync.wait_ge(sem[SY], fs)
                    sync.dma_start(
                        out=xqw[:, c % 2, :, :],
                        in_=xqT_dr[c + 2].rearrange("(dc p) t -> p dc t", p=128)
                        ).then_inc(sem[SY], 16)
                    S["xq_sy"][c + 2] = cnt.add(SY, 16)

            gbuf = (gs2, gs2b)[c % 2] if ih == 1 else gs2
            if ih == 1:
                gw_ = S["hh_dv"][c - 2] if c >= 2 else S.get("prev_hh", 0)
            else:
                gw_ = S["hh_dv"][c - 1] if c >= 1 else 0

            @block.scalar
            def _(scal, mmg=mmg, gbuf=gbuf, ph_=gw_):
                scal.wait_ge(sem[MM], mmg)
                if ph_:
                    scal.wait_ge(sem[DV], ph_)
                scal.activation(gbuf[:], g_ps[:], Act.Silu).then_inc(sem[AC], 1)
            S["silu_ac"][c] = cnt.add(AC, 1)

            @block.vector
            def _(vec, mmu=mmu, gbuf=gbuf, psg=S["sgn_ac"][c - 1] if c >= 1 else 0,
                  sa=S["silu_ac"][c]):
                vec.wait_ge(sem[MM], mmu)
                vec.wait_ge(sem[AC], sa)
                if psg:
                    vec.wait_ge(sem[AC], psg)
                vec.tensor_tensor(hh[:], u_ps[:], gbuf[:], Alu.mult
                                  ).then_inc(sem[DV], 1)
            S["hh_dv"][c] = cnt.add(DV, 1)

            S["sgn_ac"][c] = cnt.val(AC) + 1
            qk, gpq = emit_quant(hh[:], hh[:].rearrange("p (b k) -> p b k", k=16),
                                 hqs[:].rearrange("p a b -> p (a b)"), 88, 1408,
                                 dv_wait=[(DV, S["hh_dv"][c])],
                                 ac_wait=[(DV, S["hh_dv"][c])],
                                 gp_wait=[(SY, S["hq_t"][c - 1])] if c >= 1 else [],
                                 m_on_gp=True, gset=c % 2)

            @block.sync
            def _(sync, c=c, ih=ih, qk=qk, gpq=gpq):
                sync.wait_ge(sem[qk], gpq)
                sync.wait_ge(sem[SY], cnt.val(SY))
                sync.dma_start_transpose(out=hqt[:], in_=hqs[:]
                                         ).then_inc(sem[SY], 16)
                S["hq_t"][c] = cnt.add(SY, 16)
                sync.wait_ge(sem[SY], S["hq_t"][c])
                sync.dma_start(
                    out=hqT_dr[c].rearrange("(it p) t -> p it t", p=128)
                    [:, ih * 11:(ih + 1) * 11, :],
                    in_=hqt[:]).then_inc(sem[SY], 16)
                S["hq_st"][c] = cnt.add(SY, 16)

        with ExitStack() as ph:
            g_ps = ph.enter_context(nc.psum_tensor("g_ps", [128, 1408], F32))
            u_ps = ph.enter_context(nc.psum_tensor("u_ps", [128, 1408], F32))

            S0 = {"xq_sy": [0] * (NT + 1), "silu_ac": [0] * (NT + 1),
                  "hh_dv": [0] * (NT + 1), "hq_t": [0] * (NT + 1),
                  "hq_st": [0] * (NT + 1), "sgn_ac": [0] * (NT + 1)}
            for cp in range(0, NT, 2):
                emit_f1_pair(cp)
                if cp == 0:
                    @block.sync
                    def _(sync):
                        for cc in range(2):
                            sync.wait_ge(sem[SY], f1store[cc])
                            sync.dma_start(
                                out=xqw[:, cc % 2, :, :],
                                in_=xqT_dr[cc].rearrange("(dc p) t -> p dc t", p=128)
                                ).then_inc(sem[SY], 16)
                            S0["xq_sy"][cc] = cnt.add(SY, 16)
                if cp >= 2:
                    emit_f2_chunk(0, cp - 2, S0)
                    emit_f2_chunk(0, cp - 1, S0)
            emit_f2_chunk(0, NT - 2, S0)

            pm_ih0 = cnt.val(MM) + 2

            S1 = {"xq_sy": [0] * (NT + 1), "silu_ac": [0] * (NT + 1),
                  "hh_dv": [0] * (NT + 1), "hq_t": [0] * (NT + 1),
                  "hq_st": [0] * (NT + 1), "sgn_ac": [0] * (NT + 1)}

            @block.sync
            def _(sync, pm=pm_ih0):
                sync.wait_ge(sem[MM], pm)
                for cc in range(2):
                    sync.dma_start(
                        out=xqw[:, cc % 2, :, :],
                        in_=xqT_dr[cc].rearrange("(dc p) t -> p dc t", p=128)
                        ).then_inc(sem[SY], 16)
                    S1["xq_sy"][cc] = cnt.add(SY, 16)
                for dc in range(4):
                    sync.dma_start(out=wg_win[:, dc, :],
                                   in_=ag_g.rearrange("(dc p) i -> p dc i", p=128)
                                   [:, dc, 1408:2816]).then_inc(sem[SY], 16)
                    cnt.add(SY, 16)
                    sync.dma_start(out=wu_win[:, dc, :],
                                   in_=ag_u.rearrange("(dc p) i -> p dc i", p=128)
                                   [:, dc, 1408:2816]).then_inc(sem[SY], 16)
                    f2w_s[dc] = cnt.add(SY, 16)

            @block.gpsimd
            def _(gp, pm=pm_ih0):
                gp.wait_ge(sem[MM], pm)
                for dc in range(4, 8):
                    gp.dma_start(out=wg_win[:, dc, :],
                                 in_=ag_g.rearrange("(dc p) i -> p dc i", p=128)
                                 [:, dc, 1408:2816]).then_inc(sem[GQ], 16)
                    cnt.add(GQ, 16)
                    gp.dma_start(out=wu_win[:, dc, :],
                                 in_=ag_u.rearrange("(dc p) i -> p dc i", p=128)
                                 [:, dc, 1408:2816]).then_inc(sem[GQ], 16)
                    f2w_g[dc] = cnt.add(GQ, 16)

            emit_f2_chunk(0, NT - 1, S0)
            f2w_sy[1] = f2w_s[3]
            S1["prev_sil"] = S0["silu_ac"][NT - 1]
            S1["prev_hh"] = S0["hh_dv"][NT - 1]
            for c in range(NT - 1):
                emit_f2_chunk(1, c, S1)

            pm_f2 = cnt.val(MM) + 2

            @block.sync
            def _(sync, pm=pm_f2):
                sync.wait_ge(sem[MM], pm)
                sync.wait_ge(sem[GQ], agd2_gq)
                for c in range(2):
                    sync.wait_ge(sem[SY], S1["hq_st"][c])
                    sync.dma_start(
                        out=hqw[:, c % 2, :, :],
                        in_=hqT_dr[c].rearrange("(it p) t -> p it t", p=128)
                        ).then_inc(sem[SY], 16)
                    hq_sy[c] = cnt.add(SY, 16)
                for ic in range(11):
                    sync.dma_start(
                        out=wd_win[:, ic, :],
                        in_=ag_d2[ic * 128:(ic + 1) * 128, :]).then_inc(sem[SY], 16)
                    wd_sy[ic] = cnt.add(SY, 16)

            emit_f2_chunk(1, NT - 1, S1)
        hq_done_sy = cnt.val(SY)
        f2_gp = cnt.val(GP)

        wd_gq = [0] * 22

        @block.gpsimd
        def _(gp):
            gp.wait_ge(sem[SY], hq_done_sy)
            gp.wait_ge(sem[GQ], agd2_gq)
            for ic in range(11, 22):
                gp.dma_start(
                    out=wd_win[:, ic, :],
                    in_=ag_d2[ic * 128:(ic + 1) * 128, :]).then_inc(sem[GQ], 16)
                wd_gq[ic] = cnt.add(GQ, 16)

        out_gq = [0] * (NT + 1)

        def emit_out_half(a, b, rstd_v):
            for c in range(a, b):
                @block.vector
                def _(vec, c=c, ra=rstd_v, po=out_gq[c - 2] if c - 2 >= 0 else 0):
                    vec.wait_ge(sem[AC], ra)
                    if po:
                        vec.wait_ge(sem[GQ], po)
                    vec.tensor_scalar(outb[:, c % 2, :], x1buf[:, c, :],
                                      st_mean[:, c:c + 1], st_rstd[:, c:c + 1],
                                      Alu.subtract, Alu.mult).then_inc(sem[DV], 1)
                dvv = cnt.add(DV, 1)

                @block.gpsimd
                def _(gp, c=c, dvv=dvv):
                    gp.wait_ge(sem[DV], dvv)
                    gp.dma_start(out=out_d[c * 128:(c + 1) * 128, :],
                                 in_=outb[:, c % 2, :]).then_inc(sem[GQ], 16)
                    out_gq[c] = cnt.add(GQ, 16)

        res_dv = [0] * (NT + 1)
        with ExitStack() as ph:
            d_ps = [ph.enter_context(nc.psum_tensor("d_ps%d" % i, [128, 1024], F32))
                    for i in range(4)]
            for c in range(NT):
                b = c % 4

                @block.tensor
                def _(ten, c=c, b=b, pq=hq_sy[c]):
                    ten.wait_ge(sem[SY], pq)
                    if c >= 4:
                        ten.wait_ge(sem[DV], res_dv[c - 4])
                    if c <= 1:
                        ten.wait_ge(sem[SY], wd_sy[10])
                        ten.wait_ge(sem[GQ], wd_gq[21])
                    last = None
                    for ic in range(22):
                        st = hqw[:, c % 2, ic, :]
                        for n in range(2):
                            last = ten.matmul(d_ps[b][:, n * 512:(n + 1) * 512], st,
                                              wd_win[:, ic, n * 512:(n + 1) * 512],
                                              start=(ic == 0), stop=(ic == 21))
                    last.then_inc(sem[MM], 1)
                mmv = cnt.add(MM, 1)

                @block.vector
                def _(vec, c=c, b=b, mmv=mmv):
                    vec.wait_ge(sem[MM], mmv)
                    vec.tensor_tensor(x1buf[:, c, :], x1buf[:, c, :], d_ps[b][:],
                                      Alu.add).then_inc(sem[DV], 1)
                res_dv[c] = cnt.add(DV, 1)

                if c + 2 < NT:
                    @block.sync
                    def _(sync, c=c, pr=res_dv[c]):
                        sync.wait_ge(sem[DV], pr)
                        sync.dma_start(
                            out=hqw[:, c % 2, :, :],
                            in_=hqT_dr[c + 2].rearrange("(it p) t -> p it t", p=128)
                            ).then_inc(sem[SY], 16)
                        hq_sy[c + 2] = cnt.add(SY, 16)

                @block.scalar
                def _(scal, c=c, pr=res_dv[c]):
                    scal.wait_ge(sem[DV], pr)
                    scal.activation(aux1k[:], x1buf[:, c, :], Act.Copy,
                                    accum_out=st_sum[:, c:c + 1]).then_inc(sem[AC], 1)
                cnt.add(AC, 1)

                @block.vector
                def _(vec, c=c, pr=res_dv[c]):
                    vec.wait_ge(sem[DV], pr)
                    vec.tensor_tensor_reduce(
                        out=aux1k[:], in0=x1buf[:, c, :], in1=x1buf[:, c, :],
                        scale=1.0, scalar=0.0, op0=Alu.mult, op1=Alu.add,
                        accum_out=st_sq[:, c:c + 1]).then_inc(sem[DV], 1)
                cnt.add(DV, 1)

                if c == 7:
                    rstdA = emit_ln_batch(cnt.val(AC), cnt.val(DV), 0, 8)
                    emit_out_half(0, 8, rstdA)
        rstdB = emit_ln_batch(cnt.val(AC), cnt.val(DV), 8, 16)
        emit_out_half(8, 16, rstdB)

        @block.sync
        def _(sync):
            sync.wait_ge(sem[SY], cnt.val(SY))
            sync.wait_ge(sem[GQ], cnt.val(GQ))

    return nc


def kernel(L0_attn_x, L0_attn_w_qkv, L0_attn_w_swiglu, L0_attn_w_out,
           L0_ffn_w_gate, L0_ffn_w_up, L0_ffn_w_down):
    bf16 = ml_dtypes.bfloat16
    x = np.ascontiguousarray(np.asarray(L0_attn_x, np.float32))
    wqkvT = np.ascontiguousarray(np.asarray(L0_attn_w_qkv, np.float32).T)
    wswT = np.ascontiguousarray(np.asarray(L0_attn_w_swiglu, np.float32).T)
    woutT = np.ascontiguousarray(np.asarray(L0_attn_w_out, np.float32).T)
    wg = np.asarray(L0_ffn_w_gate, np.float32)
    wu = np.asarray(L0_ffn_w_up, np.float32)
    wd = np.asarray(L0_ffn_w_down, np.float32)

    L128 = np.triu(np.ones((128, 128), np.float32)).astype(bf16)
    colsel = np.zeros((128, 16, 16), np.float32)
    for c in range(16):
        colsel[:, c, c] = 1.0
    colsel = colsel.astype(bf16)
    rowsel = np.zeros((16, 16, 128), np.float32)
    for c in range(16):
        rowsel[c, c, :] = 1.0
    rowsel = rowsel.astype(bf16)
    pfx16 = np.zeros((16, 16), np.float32)
    for c in range(16):
        pfx16[:c, c] = 1.0
    pfx16 = pfx16.astype(bf16)
    ident = np.eye(128, dtype=np.float32).astype(bf16)

    nc = build_program()
    mybir.codegen_inst_isa_subclasses(nc)

    in_maps = []
    for core in range(8):
        in_maps.append({
            "x": np.ascontiguousarray(x[core]),
            "wqkvT": wqkvT,
            "wswT": wswT,
            "woutT": woutT,
            "wg_slab": np.ascontiguousarray(
                wg[:, core * 128:(core + 1) * 128].reshape(22, 128, 128)
                .transpose(1, 0, 2).reshape(128, I_FFN)),
            "wu_slab": np.ascontiguousarray(
                wu[:, core * 128:(core + 1) * 128].reshape(22, 128, 128)
                .transpose(1, 0, 2).reshape(128, I_FFN)),
            "wd_slab": np.ascontiguousarray(wd[core * 128:(core + 1) * 128, :]),
            "cL128": L128,
            "ccolsel": colsel,
            "crowsel": rowsel,
            "cpfx17": pfx16,
            "cident": ident,
        })
    res = run_bass_kernel_spmd(nc, in_maps, core_ids=list(range(8)))
    out = np.stack([np.asarray(res.results[i]["out"], np.float32) for i in range(8)], 0)
    return out


# version 29
# speedup vs baseline: 1.0178x; 1.0178x over previous
import numpy as np
import ml_dtypes
import concourse.bass as bass
import concourse.mybir as mybir
from concourse.bass_utils import run_bass_kernel_spmd
from contextlib import ExitStack

F32 = mybir.dt.float32
BF16 = mybir.dt.bfloat16
Alu = mybir.AluOpType
Act = mybir.ActivationFunctionType
AxX = mybir.AxisListType.X

import concourse.dve_ops as _dvo
from concourse.dve_op import DveOp
from concourse.dve_spec import (Spec, lower as _dve_lower, Src0, Src1, C0, C1, C2, C3,
                                Zero, maxx, _spill_c3_to_src1)
from concourse.dve_table_gen import dve_ver_for

_VER = dve_ver_for("TRN2")


def _reg_op(name, spec, reference, row):
    if name in _dvo._SUB_OPCODE_FOR_NAME:
        for op in _dvo.OPS:
            if op.name == name:
                return op
    compiled = _dve_lower(spec, ver=_VER, perf_en=False, name=name)
    op = DveOp(name, spec, subdim=False, uops_sha={_VER: compiled.sha(_VER)},
               reference=reference)
    _dvo.OPS.append(op)
    _dvo._SUB_OPCODE_FOR_NAME[name] = row
    _dvo.CUSTOM_DVE_SPECS[name] = op.spec
    _dvo.CUSTOM_DVE_REFS[name] = op.reference
    return op


FP4U = _reg_op("FP4U", Spec(body=maxx(Src0, Zero - Src0) * Src1),
               lambda in0, in1, s0, s1, imm2: np.abs(in0) * in1, 29)
_a1 = Src0 > C0
_a2 = Src0 > C1
_a3 = Src0 > C2
_a4 = Src0 > C3
FP4A = _reg_op("FP4A", Spec(body=_spill_c3_to_src1((_a1 + _a2) + (_a3 + _a4))),
               lambda in0, in1, s0, s1, imm2: (
                   (in0 > s0).astype(in0.dtype) + (in0 > s1).astype(in0.dtype)
                   + (in0 > imm2).astype(in0.dtype) + (in0 > in1[:, :1]).astype(in0.dtype)),
               30)
_b5 = Src0 > C0
_b6 = Src0 > C1
_b7 = Src0 > C2
_bt = (_b5 + _b6) + (_b7 + _b7)
FP4B = _reg_op("FP4B", Spec(body=Src1 + (_bt + _bt)),
               lambda in0, in1, s0, s1, imm2: (
                   in1 + 2.0 * ((in0 > s0).astype(in0.dtype) + (in0 > s1).astype(in0.dtype)
                                + 2.0 * (in0 > imm2).astype(in0.dtype))),
               31)

T, D, NT = 2048, 1024, 16
I_FFN = 2816
FP32_EPS = 1.1920929e-07

OFF = dict(
    cL=0, csel=256, cident=768, crow=1024,
    ssq_b=5120, lsc_b=5152, inv_b=5184, ninv_b=5216,
    sigtmp=5248,
    cpfx=9216,
    c_i1024=9280, c_rmse=9312, c_lneps=9344, c_nhalf=9376, c_none=9408,
    c_one=9440, c_c3a=9472,
    st_sum=9504, st_sq=9568, st_mean=9632, st_var=9696, st_rstd=9760,
    ssq_s=9824, lsc_s=9856, inv_n=9888, ninv_n=9920, c_eps6=9984,
    q_bmax=10048, q_s=10752, q_r=11456, q_sh=12160,
    q_u=12864, q_p4=24128, q_g2=29760, q_sgn=35392, q_m=41024,
    aux1k=12864,
    pf8=12864,
    e1s0=21056, e1s1=25152,
    lnscr=29248,
    ebufA=33344, ebufB=37440,
    th=41536, tl=42560,
    y2c=43584,
    wslab=58368, wdslab=69632, qbf=80896, tstage=86528,
    wbuf=47680,
    kvbuf=80448,
    xnT=113216,
    win_kv=145984,
    y2T=145984,
    xc=178752,
    xnc=186944,
    win_q=178752,
    win_sw=80448,
    zc=43584, gs_a6=41536, zT=47680,
    win_o=12864, xc2=29248, scrA7=168512, x1buf=80448,
    xqc=47680, xqs0=51776, xqs1=53824, wg_win=55872, wu_win=145984,
    xqw=178752, hh=168512, gs2=174144, hqs=182848, hqt=185664,
    q_g2b=188480, q_shb=194112, q_mb=41536,
    wd_win=145984, hqw=47680, outb=58944,
)
ARENA = 196608


class Counts:
    def __init__(self):
        self.c = {}

    def add(self, k, by):
        self.c[k] = self.c.get(k, 0) + by
        return self.c[k]

    def val(self, k):
        return self.c.get(k, 0)


def build_program():
    nc = bass.Bass("TRN2", target_bir_lowering=False, debug=False,
                   detect_race_conditions=False)

    x_d = nc.dram_tensor("x", [T, D], F32, kind="ExternalInput").ap()
    wqkvT_d = nc.dram_tensor("wqkvT", [D, 3 * D], F32, kind="ExternalInput").ap()
    wswT_d = nc.dram_tensor("wswT", [D, 2 * D], F32, kind="ExternalInput").ap()
    woutT_d = nc.dram_tensor("woutT", [D, D], F32, kind="ExternalInput").ap()
    wg_sl_d = nc.dram_tensor("wg_slab", [128, I_FFN], F32, kind="ExternalInput").ap()
    wu_sl_d = nc.dram_tensor("wu_slab", [128, I_FFN], F32, kind="ExternalInput").ap()
    wd_sl_d = nc.dram_tensor("wd_slab", [128, I_FFN], F32, kind="ExternalInput").ap()
    cL_d = nc.dram_tensor("cL128", [128, 128], BF16, kind="ExternalInput").ap()
    csel_d = nc.dram_tensor("ccolsel", [128, 16, 16], BF16, kind="ExternalInput").ap()
    crow_d = nc.dram_tensor("crowsel", [16, 16, 128], BF16, kind="ExternalInput").ap()
    cpfx_d = nc.dram_tensor("cpfx17", [16, 16], BF16, kind="ExternalInput").ap()
    cid_d = nc.dram_tensor("cident", [128, 128], BF16, kind="ExternalInput").ap()
    out_d = nc.dram_tensor("out", [T, D], F32, kind="ExternalOutput").ap()

    ag_in_g = nc.dram_tensor("ag_in_g", [128, I_FFN], BF16).ap()
    ag_in_u = nc.dram_tensor("ag_in_u", [128, I_FFN], BF16).ap()
    ag_in_d = nc.dram_tensor("ag_in_d", [I_FFN, 128], BF16).ap()
    ag_g = nc.dram_tensor("ag_g", [D, I_FFN], BF16).ap()
    ag_u = nc.dram_tensor("ag_u", [D, I_FFN], BF16).ap()
    ag_d = nc.dram_tensor("ag_d", [8 * I_FFN, 128], BF16).ap()
    xqT_dr = nc.dram_tensor("xqT_dr", [NT, D, 128], BF16).ap()
    hqT_dr = nc.dram_tensor("hqT_dr", [NT, I_FFN, 128], BF16).ap()
    ag_d2 = nc.dram_tensor("ag_d2", [I_FFN, 8 * 128], BF16).ap()

    cnt = Counts()
    SY, GQ, MM, DV, AC, GP, CC = "sy", "gq", "mm", "dv", "ac", "gp", "cc"

    with ExitStack() as top:
        sem = {s: top.enter_context(nc.semaphore("sem_" + s))
               for s in (SY, GQ, MM, DV, AC, GP, CC)}
        arena = top.enter_context(nc.sbuf_tensor("arena", [128, ARENA], mybir.dt.uint8))
        base = nc.lookup_mloc(arena).addr

        def place(key, shape, dtype):
            h = nc.virtual_sbuf_tensor("t_" + key, shape, dtype)
            nc._tensor(h.name, h.shape, h.dtype, type="SB")
            mloc = nc.lookup_mloc(h)
            mloc.allocated = True
            mloc.addr = base + OFF[key]
            return h

        cL = place("cL", [128, 128], BF16)
        csel = place("csel", [128, 16, 16], BF16)
        cident = place("cident", [128, 128], BF16)
        crow = place("crow", [16, 16, 128], BF16)
        cpfx = place("cpfx", [16, 16], BF16)
        c_i1024 = place("c_i1024", [128, 1], F32)
        c_rmse = place("c_rmse", [128, 1], F32)
        c_lneps = place("c_lneps", [128, 1], F32)
        c_nhalf = place("c_nhalf", [128, 1], F32)
        c_none = place("c_none", [128, 1], F32)
        c_one = place("c_one", [128, 1], F32)
        c_c3a = place("c_c3a", [128, 1], F32)
        c_eps6 = place("c_eps6", [128, 1], F32)
        st_sum = place("st_sum", [128, 16], F32)
        st_sq = place("st_sq", [128, 16], F32)
        st_mean = place("st_mean", [128, 16], F32)
        st_var = place("st_var", [128, 16], F32)
        st_rstd = place("st_rstd", [128, 16], F32)
        ssq_s = place("ssq_s", [128, 1], F32)
        lsc_s = place("lsc_s", [128, 1], F32)
        inv_n = place("inv_n", [128, 1], F32)
        ninv_n = place("ninv_n", [128, 1], F32)
        ssq_b = place("ssq_b", [128, 1], F32)
        lsc_b = place("lsc_b", [128, 1], F32)
        inv_b = place("inv_b", [128, 1], F32)
        ninv_b = place("ninv_b", [128, 1], F32)
        q_bmax = place("q_bmax", [128, 176], F32)
        q_s = place("q_s", [128, 176], F32)
        q_r = place("q_r", [128, 176], F32)
        q_sh = place("q_sh", [128, 176], F32)
        q_u = place("q_u", [128, 2816], F32)
        q_p4 = place("q_p4", [128, 2816], BF16)
        q_g2 = place("q_g2", [128, 2816], BF16)
        q_sgn = place("q_sgn", [128, 2816], BF16)
        q_m = place("q_m", [128, 2816], BF16)
        q_g2b = place("q_g2b", [128, 2816], BF16)
        q_mb = place("q_mb", [128, 2816], BF16)
        q_shb = place("q_shb", [128, 176], F32)
        aux1k = place("aux1k", [128, 1024], F32)

        block = top.enter_context(nc.Block())

        @block.sync
        def _(sync):
            sync.dma_start(out=cL[:], in_=cL_d).then_inc(sem[SY], 16)
            sync.dma_start(out=csel[:], in_=csel_d).then_inc(sem[SY], 16)
            sync.dma_start(out=crow[:], in_=crow_d).then_inc(sem[SY], 16)
            sync.dma_start(out=cpfx[:], in_=cpfx_d).then_inc(sem[SY], 16)
            sync.dma_start(out=cident[:], in_=cid_d).then_inc(sem[SY], 16)
            cnt.add(SY, 80)

        @block.vector
        def _(vec):
            for buf, v in ((c_i1024, 1.0 / 1024.0), (c_rmse, FP32_EPS),
                           (c_lneps, 1e-5), (c_nhalf, -0.5), (c_none, -1.0),
                           (c_one, 1.0), (c_c3a, 1.75), (c_eps6, 1e-6)):
                vec.memset(buf[:], v)
            vec.memset(inv_n[:], 1.0).then_inc(sem[DV], 1)
        consts_dv = cnt.add(DV, 1)

        xc = place("xc", [128, 2, 1024], F32)
        win_kv = place("win_kv", [128, 8, 2048], BF16)

        @block.gpsimd
        def _(gp):
            for c in range(2):
                gp.dma_start(out=xc[:, c, :], in_=x_d[c * 128:(c + 1) * 128, :]
                             ).then_inc(sem[GQ], 16)
                cnt.add(GQ, 16)
            for dc in range(8):
                gp.dma_start(
                    out=win_kv[:, dc, :],
                    in_=wqkvT_d.rearrange("(dc p) e -> p dc e", p=128)[:, dc, 1024:3072]
                ).then_inc(sem[GQ], 16)
                cnt.add(GQ, 16)
        xgq = [16, 32] + [0] * 16
        winkv_gq = cnt.val(GQ)

        last_m = [(DV, 0)]

        last_gpset = {0: 0, 1: 0}

        def emit_quant(X_ap, X3_ap, out_ap, nb, F, dv_wait=(), ac_wait=(), gp_wait=(),
                       out_on_dve=False, m_on_gp=False, gset=0):
            g2t, mt, sht = ((q_g2, q_m, q_sh), (q_g2b, q_mb, q_shb))[gset]
            prev_gp = last_gpset[gset]
            prev_mk, prev_m = last_m[0]
            r_b = q_r[0:128, 0:nb].unsqueeze(2).broadcast_to([128, nb, 16])
            sh_b = sht[0:128, 0:nb].unsqueeze(2).broadcast_to([128, nb, 16])
            u_f = q_u[0:128, 0:F]
            u_3 = q_u[0:128, 0:F].rearrange("p (b k) -> p b k", k=16)

            @block.scalar
            def _(scal):
                for s, v in ac_wait:
                    scal.wait_ge(sem[s], v)
                if prev_m:
                    scal.wait_ge(sem[prev_mk], prev_m)
                scal.activation(q_sgn[0:128, 0:F], X_ap, Act.Sign).then_inc(sem[AC], 1)
            acv = cnt.add(AC, 1)

            @block.vector
            def _(vec):
                for s, v in dv_wait:
                    vec.wait_ge(sem[s], v)
                vec.tensor_reduce(q_bmax[0:128, 0:nb], X3_ap, AxX, Alu.max,
                                  apply_absolute_value=True).then_inc(sem[DV], 1)
                vec.wait_ge(sem[DV], cnt.add(DV, 1))
                vec.tensor_scalar(q_s[0:128, 0:nb], q_bmax[0:128, 0:nb],
                                  1.0 / 6.0, 1e-12, Alu.mult, Alu.max
                                  ).then_inc(sem[DV], 1)
                vec.wait_ge(sem[DV], cnt.add(DV, 1))
                vec.reciprocal(q_r[0:128, 0:nb], q_s[0:128, 0:nb]).then_inc(sem[DV], 1)
                vec.wait_ge(sem[DV], cnt.add(DV, 1))
                vec._custom_dve(FP4U, out=u_3, in0=X3_ap, in1=r_b)
                vec._custom_dve(FP4A, out=q_p4[0:128, 0:F], in0=u_f, in1=c_c3a[:],
                                s0=0.25, s1=0.75, imm2=1.25)
                if prev_gp:
                    vec.wait_ge(sem[GP], prev_gp)
                vec.tensor_scalar(sht[0:128, 0:nb], q_s[0:128, 0:nb],
                                  0.5, None, Alu.mult)
                m3 = mt[0:128, 0:F].rearrange("p (b k) -> p b k", k=16)
                s3 = q_sgn[0:128, 0:F].rearrange("p (b k) -> p b k", k=16)
                if m_on_gp:
                    vec._custom_dve(FP4B, out=g2t[0:128, 0:F], in0=u_f,
                                    in1=q_p4[0:128, 0:F],
                                    s0=2.5, s1=3.5, imm2=5.0).then_inc(sem[DV], 1)
                elif out_on_dve:
                    vec._custom_dve(FP4B, out=g2t[0:128, 0:F], in0=u_f,
                                    in1=q_p4[0:128, 0:F],
                                    s0=2.5, s1=3.5, imm2=5.0)
                    vec.wait_ge(sem[AC], acv)
                    vec.tensor_tensor(m3, s3, sh_b, Alu.mult)
                    for s, v in gp_wait:
                        vec.wait_ge(sem[s], v)
                    vec.tensor_tensor(out_ap, g2t[0:128, 0:F], mt[0:128, 0:F],
                                      Alu.mult).then_inc(sem[DV], 1)
                else:
                    vec._custom_dve(FP4B, out=g2t[0:128, 0:F], in0=u_f,
                                    in1=q_p4[0:128, 0:F],
                                    s0=2.5, s1=3.5, imm2=5.0)
                    vec.wait_ge(sem[AC], acv)
                    vec.tensor_tensor(m3, s3, sh_b, Alu.mult).then_inc(sem[DV], 1)
            dvv = cnt.add(DV, 1)
            if out_on_dve:
                last_m[0] = (DV, dvv)
                return (DV, dvv)

            if m_on_gp:
                @block.gpsimd
                def _(gp):
                    for s, v in gp_wait:
                        gp.wait_ge(sem[s], v)
                    gp.wait_ge(sem[DV], dvv)
                    gp.wait_ge(sem[AC], acv)
                    gp.tensor_tensor(
                        mt[0:128, 0:F].rearrange("p (b k) -> p b k", k=16),
                        q_sgn[0:128, 0:F].rearrange("p (b k) -> p b k", k=16),
                        sh_b, Alu.mult)
                    gp.tensor_tensor(out_ap, g2t[0:128, 0:F], mt[0:128, 0:F],
                                     Alu.mult).then_inc(sem[GP], 1)
                gpv = cnt.add(GP, 1)
                last_m[0] = (GP, gpv)
                last_gpset[gset] = gpv
                return (GP, gpv)

            last_m[0] = (DV, dvv)

            @block.gpsimd
            def _(gp):
                for s, v in gp_wait:
                    gp.wait_ge(sem[s], v)
                gp.wait_ge(sem[DV], dvv)
                gp.tensor_tensor(out_ap, g2t[0:128, 0:F], mt[0:128, 0:F],
                                 Alu.mult).then_inc(sem[GP], 1)
            gpv2 = cnt.add(GP, 1)
            last_gpset[gset] = gpv2
            return (GP, gpv2)

        wslab = place("wslab", [128, 22, 128], F32)
        wdslab = place("wdslab", [128, 22, 128], F32)
        qbf = place("qbf", [128, 2816], BF16)
        tstage = place("tstage", [128, 22, 128], BF16)

        @block.sync
        def _(sync):
            sync.dma_start(out=wslab[:],
                           in_=wg_sl_d.rearrange("p (it h) -> p it h", it=22)
                           ).then_inc(sem[SY], 16)
            cnt.add(SY, 16)
            sync.dma_start(out=wdslab[:],
                           in_=wu_sl_d.rearrange("p (it h) -> p it h", it=22)
                           ).then_inc(sem[SY], 16)
            cnt.add(SY, 16)
        wu_ld = cnt.val(SY)
        wg_ld = wu_ld - 16

        wsl_f = wslab[:].rearrange("p it h -> p (it h)")
        wsl_3 = wslab[:].rearrange("p it (b k) -> p (it b) k", k=16)
        wdl_f = wdslab[:].rearrange("p it i -> p (it i)")
        wdl_3 = wdslab[:].rearrange("p it (b k) -> p (it b) k", k=16)

        kg, vg = emit_quant(wsl_f, wsl_3, qbf[:], 176, 2816,
                            dv_wait=[(SY, wg_ld), (DV, consts_dv)],
                            ac_wait=[(SY, wg_ld)], out_on_dve=True)

        @block.sync
        def _(sync):
            sync.wait_ge(sem[kg], vg)
            sync.dma_start_transpose(
                out=tstage[:], in_=qbf[:].rearrange("p (it h) -> p it h", it=22)
            ).then_inc(sem[SY], 16)
            cnt.add(SY, 16)
            sync.wait_ge(sem[SY], cnt.val(SY))
            sync.dma_start(out=ag_in_g,
                           in_=tstage[:].rearrange("p it h -> p (it h)")
                           ).then_inc(sem[SY], 16)
            cnt.add(SY, 16)
            sync.dma_start(out=wslab[:],
                           in_=wd_sl_d.rearrange("p (it i) -> p it i", it=22)
                           ).then_inc(sem[SY], 16)
            cnt.add(SY, 16)
        wd_ld = cnt.val(SY)
        g_stored = wd_ld - 16
        gtr_sy = wd_ld - 32

        ku, vu = emit_quant(wdl_f, wdl_3, qbf[:], 176, 2816,
                            dv_wait=[(SY, wu_ld)], ac_wait=[(SY, wu_ld)],
                            gp_wait=[(SY, gtr_sy)], out_on_dve=True)

        @block.sync
        def _(sync):
            sync.wait_ge(sem[ku], vu)
            sync.dma_start_transpose(
                out=tstage[:], in_=qbf[:].rearrange("p (it h) -> p it h", it=22)
            ).then_inc(sem[SY], 16)
            cnt.add(SY, 16)
            sync.wait_ge(sem[SY], cnt.val(SY))
            sync.dma_start(out=ag_in_u,
                           in_=tstage[:].rearrange("p it h -> p (it h)")
                           ).then_inc(sem[SY], 16)
            cnt.add(SY, 16)
        u_stored = cnt.val(SY)
        utr_sy = u_stored - 16

        kd, vd = emit_quant(wsl_f, wsl_3, qbf[:], 176, 2816,
                            dv_wait=[(SY, wd_ld)], ac_wait=[(SY, wd_ld)],
                            gp_wait=[(SY, utr_sy)], out_on_dve=True)

        @block.sync
        def _(sync):
            sync.wait_ge(sem[kd], vd)
            sync.dma_start_transpose(
                out=tstage[:], in_=qbf[:].rearrange("p (it i) -> p it i", it=22)
            ).then_inc(sem[SY], 16)
            cnt.add(SY, 16)
            sync.wait_ge(sem[SY], cnt.val(SY))
            sync.dma_start(out=ag_in_d.rearrange("(it q) h -> q it h", q=128),
                           in_=tstage[:]).then_inc(sem[SY], 16)
            cnt.add(SY, 16)
        d_stored = cnt.val(SY)

        xnc = place("xnc", [128, 2, 1024], BF16)
        xnT = place("xnT", [128, 8, 2048], BF16)
        wbuf = place("wbuf", [128, 16, 1024], BF16)
        kvbuf = place("kvbuf", [128, 16, 1024], BF16)

        a2_ac = [0] * (NT + 1)
        a2ts = [0] * (NT + 1)
        a2Tsy = [0] * (NT + 1)
        xsy2 = [0] * (NT + 2)
        a4_ac = [0] * (NT + 1)
        kvtt = [0] * (NT + 1)

        def emit_a2_chunk(c):
            @block.vector
            def _(vec, c=c, pa=a2_ac[c - 1] if c >= 1 else 0, px=xsy2[c]):
                if c < 2:
                    vec.wait_ge(sem[GQ], xgq[c])
                else:
                    vec.wait_ge(sem[SY], px)
                if c == 0:
                    vec.wait_ge(sem[DV], consts_dv)
                else:
                    vec.wait_ge(sem[AC], pa)
                vec.tensor_tensor_reduce(
                    out=aux1k[:], in0=xc[:, c % 2, :], in1=xc[:, c % 2, :],
                    scale=1.0, scalar=0.0, op0=Alu.mult, op1=Alu.add,
                    accum_out=ssq_s[:]).then_inc(sem[DV], 1)
            a2ttr = cnt.add(DV, 1)

            @block.scalar
            def _(scal, c=c, a2ttr=a2ttr, pts=a2ts[c - 1] if c >= 1 else 0):
                scal.wait_ge(sem[DV], a2ttr)
                if pts:
                    scal.wait_ge(sem[DV], pts)
                scal.activation(lsc_s[:], ssq_s[:], Act.Ln,
                                scale=c_i1024[:], bias=c_rmse[:]).then_inc(sem[AC], 1)
                scal.wait_ge(sem[AC], cnt.add(AC, 1))
                scal.activation(inv_n[:], lsc_s[:], Act.Exp,
                                scale=c_nhalf[:]).then_inc(sem[AC], 1)
            a2_ac[c] = cnt.add(AC, 1)

            @block.vector
            def _(vec, c=c, acv=a2_ac[c], pT=a2Tsy[c - 2] if c >= 2 else 0):
                vec.wait_ge(sem[AC], acv)
                if pT:
                    vec.wait_ge(sem[SY], pT)
                vec.tensor_scalar(xnc[:, c % 2, :], xc[:, c % 2, :],
                                  inv_n[:], None, Alu.mult).then_inc(sem[DV], 1)
            a2ts[c] = cnt.add(DV, 1)

            @block.sync
            def _(sync, c=c, pts=a2ts[c]):
                sync.wait_ge(sem[DV], pts)
                sync.dma_start_transpose(
                    out=xnT[:, :, c * 128:(c + 1) * 128],
                    in_=xnc[:, c % 2, :].rearrange("p (a b) -> p a b", a=8)
                ).then_inc(sem[SY], 16)
                a2Tsy[c] = cnt.add(SY, 16)
                if c + 2 < NT:
                    sync.dma_start(out=xc[:, c % 2, :],
                                   in_=x_d[(c + 2) * 128:(c + 3) * 128, :]
                                   ).then_inc(sem[SY], 16)
                    xsy2[c + 2] = cnt.add(SY, 16)

        def emit_a4_chunk(c):
            b = c % 2

            @block.tensor
            def _(ten, c=c, b=b, pT=a2Tsy[c]):
                ten.wait_ge(sem[SY], pT)
                if c == 0:
                    ten.wait_ge(sem[GQ], winkv_gq)
                    ten.wait_ge(sem[SY], d_stored)
                if c >= 2:
                    ten.wait_ge(sem[DV], kvtt[c - 2])
                last = None
                for dc in range(8):
                    st = xnT[:, dc, c * 128:(c + 1) * 128]
                    for n in range(2):
                        last = ten.matmul(k_ps[b][:, n * 512:(n + 1) * 512], st,
                                          win_kv[:, dc, n * 512:(n + 1) * 512],
                                          start=(dc == 0), stop=(dc == 7))
                    for n in range(2):
                        last = ten.matmul(v_ps[b][:, n * 512:(n + 1) * 512], st,
                                          win_kv[:, dc, 1024 + n * 512:1024 + (n + 1) * 512],
                                          start=(dc == 0), stop=(dc == 7))
                last.then_inc(sem[MM], 1)
            mmv = cnt.add(MM, 1)

            @block.scalar
            def _(scal, c=c, b=b, mmv=mmv):
                scal.wait_ge(sem[MM], mmv)
                scal.activation(aux1k[:], k_ps[b][:], Act.Square,
                                accum_out=ssq_b[:]).then_inc(sem[AC], 1)
                scal.wait_ge(sem[AC], cnt.add(AC, 1))
                scal.activation(lsc_b[:], ssq_b[:], Act.Ln,
                                scale=c_i1024[:], bias=c_rmse[:]).then_inc(sem[AC], 1)
                scal.wait_ge(sem[AC], cnt.add(AC, 1))
                scal.activation(inv_b[:], lsc_b[:], Act.Exp, scale=c_nhalf[:]
                                ).then_inc(sem[AC], 1)
                scal.wait_ge(sem[AC], cnt.add(AC, 1))
                scal.activation(wbuf[:, c, :], k_ps[b][:], Act.Exp,
                                scale=inv_b[:]).then_inc(sem[AC], 1)
            a4_ac[c] = cnt.add(AC, 1)

            @block.vector
            def _(vec, c=c, b=b, acv=a4_ac[c]):
                vec.wait_ge(sem[AC], acv)
                vec.tensor_tensor(kvbuf[:, c, :], wbuf[:, c, :], v_ps[b][:],
                                  Alu.mult).then_inc(sem[DV], 1)
            kvtt[c] = cnt.add(DV, 1)

        with ExitStack() as ph:
            k_ps = [ph.enter_context(nc.psum_tensor("k_ps%d" % i, [128, 1024], F32))
                    for i in range(2)]
            v_ps = [ph.enter_context(nc.psum_tensor("v_ps%d" % i, [128, 1024], F32))
                    for i in range(2)]
            for c in range(NT):
                emit_a2_chunk(c)
                if c >= 1:
                    emit_a4_chunk(c - 1)
            emit_a4_chunk(NT - 1)
        a4_all_dv = kvtt[NT - 1]
        a4_all_ac = a4_ac[NT - 1]
        xn_done_sy = a2Tsy[NT - 1]

        @block.gpsimd
        def _(gp):
            gp.wait_ge(sem[SY], g_stored)
            gp.collective_compute("AllGather", Alu.bypass,
                                  replica_groups=[list(range(8))],
                                  ins=[ag_in_g.opt()], outs=[ag_g.opt()]
                                  ).then_inc(sem[CC], 1)
            cnt.add(CC, 1)
            gp.wait_ge(sem[SY], u_stored)
            gp.collective_compute("AllGather", Alu.bypass,
                                  replica_groups=[list(range(8))],
                                  ins=[ag_in_u.opt()], outs=[ag_u.opt()]
                                  ).then_inc(sem[CC], 1)
            cnt.add(CC, 1)
            gp.wait_ge(sem[SY], d_stored)
            gp.collective_compute("AllGather", Alu.bypass,
                                  replica_groups=[list(range(8))],
                                  ins=[ag_in_d.opt()], outs=[ag_d.opt()]
                                  ).then_inc(sem[CC], 1)
            cnt.add(CC, 1)
        cc_done = cnt.val(CC)

        win_q = place("win_q", [128, 8, 1024], BF16)

        @block.gpsimd
        def _(gp):
            gp.wait_ge(sem[SY], xn_done_sy)
            for dc in range(8):
                gp.dma_start(
                    out=win_q[:, dc, :],
                    in_=wqkvT_d.rearrange("(dc p) e -> p dc e", p=128)[:, dc, 0:1024]
                ).then_inc(sem[GQ], 16)
                cnt.add(GQ, 16)
        winq_gq = cnt.val(GQ)


        pf8 = place("pf8", [16, 8, 512], BF16)
        th = place("th", [16, 512], BF16)
        tl = place("tl", [16, 512], BF16)
        e1s = [place("e1s0", [128, 1024], F32), place("e1s1", [128, 1024], F32)]
        sigtmp = place("sigtmp", [128, 1024], BF16)
        lnscr = place("lnscr", [128, 1024], F32)
        ebuf = [place("ebufA", [128, 1024], F32), place("ebufB", [128, 1024], F32)]
        y2c = place("y2c", [128, 2, 1024], BF16)
        y2T = place("y2T", [128, 8, 2048], BF16)

        mmcs = [0] * (NT + 1)
        with ExitStack() as ph:
            with ExitStack() as php:
                tot_ps = php.enter_context(nc.psum_tensor("tot_ps", [16, 512], F32))
                pfx_ps = php.enter_context(nc.psum_tensor("pfx_ps", [16, 512], F32))
                for sl in range(2):
                    for ti, buf in enumerate((wbuf, kvbuf)):
                        j = (sl * 2 + ti) * 2

                        @block.tensor
                        def _(ten, sl=sl, buf=buf):
                            ten.wait_ge(sem[DV], a4_all_dv)
                            ten.wait_ge(sem[AC], a4_all_ac)
                            last = None
                            for c in range(16):
                                last = ten.matmul(tot_ps[:], csel[:, c, :],
                                                  buf[:, c, sl * 512:(sl + 1) * 512],
                                                  start=(c == 0), stop=(c == 15))
                            last.then_inc(sem[MM], 1)
                        mmt = cnt.add(MM, 1)

                        @block.vector
                        def _(vec, mmt=mmt):
                            vec.wait_ge(sem[MM], mmt)
                            vec.tensor_copy(th[:], tot_ps[:]).then_inc(sem[DV], 1)
                            vec.wait_ge(sem[DV], cnt.add(DV, 1))
                            vec.tensor_tensor(tl[:], tot_ps[:], th[:], Alu.subtract
                                              ).then_inc(sem[DV], 1)
                        dvv = cnt.add(DV, 1)

                        @block.tensor
                        def _(ten, dvv=dvv):
                            ten.wait_ge(sem[DV], dvv)
                            ten.matmul(pfx_ps[:], cpfx[:], th[:], start=True, stop=False)
                            ten.matmul(pfx_ps[:], cpfx[:], tl[:], start=False, stop=True
                                       ).then_inc(sem[MM], 1)
                        mmv2 = cnt.add(MM, 1)

                        @block.vector
                        def _(vec, mmv2=mmv2, j=j):
                            vec.wait_ge(sem[MM], mmv2)
                            vec.tensor_copy(pf8[:, j, :], pfx_ps[:]).then_inc(sem[DV], 1)
                            vec.wait_ge(sem[DV], cnt.add(DV, 1))
                            vec.tensor_tensor(pf8[:, j + 1, :], pfx_ps[:], pf8[:, j, :],
                                              Alu.subtract).then_inc(sem[DV], 1)
                        cnt.add(DV, 1)
            pf_dv = cnt.val(DV)

            cw_ps = ph.enter_context(nc.psum_tensor("cw_ps", [128, 1024], F32))
            ckv_ps = ph.enter_context(nc.psum_tensor("ckv_ps", [128, 1024], F32))
            q_ps = ph.enter_context(nc.psum_tensor("q_ps", [128, 1024], F32))

            yr_dv = [0] * (NT + 1)
            fin_dv = [0] * (NT + 1)
            ln_ac = [0] * (NT + 1)
            e1_ac = [0] * (NT + 1)
            sig_ac = [0] * (NT + 1)
            inv_ac = [0] * (NT + 1)
            y2Tsy = [0] * (NT + 1)

            for c in range(NT):
                @block.tensor
                def _(ten, c=c):
                    if c == 0:
                        ten.wait_ge(sem[DV], pf_dv)
                    else:
                        ten.wait_ge(sem[DV], yr_dv[c - 1])
                    for sl in range(2):
                        ten.matmul(cw_ps[:, sl * 512:(sl + 1) * 512], cL[:],
                                   wbuf[:, c, sl * 512:(sl + 1) * 512],
                                   start=True, stop=False)
                        ten.matmul(ckv_ps[:, sl * 512:(sl + 1) * 512], cL[:],
                                   kvbuf[:, c, sl * 512:(sl + 1) * 512],
                                   start=True, stop=False)
                    last = None
                    for sl in range(2):
                        j = (sl * 2) * 2
                        jk = (sl * 2 + 1) * 2
                        ten.matmul(cw_ps[:, sl * 512:(sl + 1) * 512], crow[:, c, :],
                                   pf8[:, j, :], start=False, stop=False)
                        ten.matmul(cw_ps[:, sl * 512:(sl + 1) * 512], crow[:, c, :],
                                   pf8[:, j + 1, :], start=False, stop=True)
                        ten.matmul(ckv_ps[:, sl * 512:(sl + 1) * 512], crow[:, c, :],
                                   pf8[:, jk, :], start=False, stop=False)
                        last = ten.matmul(ckv_ps[:, sl * 512:(sl + 1) * 512],
                                          crow[:, c, :], pf8[:, jk + 1, :],
                                          start=False, stop=True)
                    last.then_inc(sem[MM], 1)
                mmcs[c] = cnt.add(MM, 1)

                @block.tensor
                def _(ten, c=c):
                    if c == 0:
                        ten.wait_ge(sem[GQ], winq_gq)
                    else:
                        ten.wait_ge(sem[AC], e1_ac[c - 1])
                    last = None
                    for dc in range(8):
                        st = xnT[:, dc, c * 128:(c + 1) * 128]
                        for n in range(2):
                            last = ten.matmul(q_ps[:, n * 512:(n + 1) * 512], st,
                                              win_q[:, dc, n * 512:(n + 1) * 512],
                                              start=(dc == 0), stop=(dc == 7))
                    last.then_inc(sem[MM], 1)
                mmq = cnt.add(MM, 1)

                @block.scalar
                def _(scal, c=c, mmq=mmq, yw=yr_dv[c - 2] if c >= 2 else 0,
                      fw=fin_dv[c - 1] if c >= 1 else 0,
                      sw=sig_ac[c - 1] if c >= 1 else 0):
                    scal.wait_ge(sem[MM], mmq)
                    if yw:
                        scal.wait_ge(sem[DV], yw)
                    if fw:
                        scal.wait_ge(sem[DV], fw)
                    if sw:
                        scal.wait_ge(sem[AC], sw)
                    scal.activation(e1s[c % 2][:], q_ps[:], Act.Square,
                                    accum_out=ssq_b[:])
                    scal.activation(lnscr[:], cw_ps[:], Act.Ln, bias=c_eps6[:])
                    scal.activation(ebuf[c % 2][:], lnscr[:], Act.Exp,
                                    scale=c_none[:]).then_inc(sem[AC], 1)
                    cnt.add(AC, 1)
                    scal.activation(lsc_b[:], ssq_b[:], Act.Ln,
                                    scale=c_i1024[:], bias=c_rmse[:]).then_inc(sem[AC], 1)
                    scal.wait_ge(sem[AC], cnt.add(AC, 1))
                    scal.activation(inv_b[:], lsc_b[:], Act.Exp, scale=c_nhalf[:]
                                    ).then_inc(sem[AC], 1)
                    scal.wait_ge(sem[AC], cnt.add(AC, 1))
                    scal.activation(e1s[c % 2][:], q_ps[:], Act.Exp, scale=inv_b[:]
                                    ).then_inc(sem[AC], 1)
                    cnt.add(AC, 1)
                    scal.activation(lnscr[:], e1s[c % 2][:], Act.Ln, bias=c_one[:])
                    scal.activation(e1s[c % 2][:], lnscr[:], Act.Exp, scale=c_none[:]
                                    ).then_inc(sem[AC], 1)
                sig_ac[c] = cnt.add(AC, 1)
                e1_ac[c] = sig_ac[c] - 1
                ln_ac[c] = sig_ac[c] - 4

                @block.vector
                def _(vec, c=c, la=ln_ac[c], pT=y2Tsy[c - 2] if c >= 2 else 0):
                    vec.wait_ge(sem[AC], la)
                    if pT:
                        vec.wait_ge(sem[SY], pT)
                    vec.tensor_tensor(y2c[:, c % 2, :], ckv_ps[:], ebuf[c % 2][:],
                                      Alu.mult).then_inc(sem[DV], 1)
                yr_dv[c] = cnt.add(DV, 1)

                @block.vector
                def _(vec, c=c, sa=sig_ac[c]):
                    vec.wait_ge(sem[AC], sa)
                    vec.tensor_tensor(lnscr[:], y2c[:, c % 2, :], e1s[c % 2][:],
                                      Alu.mult)
                    vec.tensor_tensor(y2c[:, c % 2, :], y2c[:, c % 2, :], lnscr[:],
                                      Alu.subtract).then_inc(sem[DV], 1)
                fin_dv[c] = cnt.add(DV, 1)

                @block.sync
                def _(sync, c=c, fv=fin_dv[c]):
                    sync.wait_ge(sem[DV], fv)
                    sync.dma_start_transpose(
                        out=y2T[:, :, c * 128:(c + 1) * 128],
                        in_=y2c[:, c % 2, :].rearrange("p (a b) -> p a b", a=8)
                    ).then_inc(sem[SY], 16)
                    cnt.add(SY, 16)
                y2Tsy[c] = cnt.val(SY)
        y2_dv = fin_dv[NT - 1]
        a5q_mm_end = cnt.val(MM)

        win_sw = place("win_sw", [128, 8, 2048], BF16)

        @block.gpsimd
        def _(gp):
            for dc in range(8):
                gp.wait_ge(sem[MM], mmcs[2 * dc + 1])
                gp.dma_start(
                    out=win_sw[:, dc, :],
                    in_=wswT_d.rearrange("(dc p) e -> p dc e", p=128)[:, dc, :]
                ).then_inc(sem[GQ], 16)
                cnt.add(GQ, 16)
        winsw_gq = cnt.val(GQ)

        win_o = place("win_o", [128, 8, 1024], BF16)
        xc2 = place("xc2", [128, 2, 1024], F32)

        @block.gpsimd
        def _(gp):
            gp.wait_ge(sem[MM], a5q_mm_end)
            gp.wait_ge(sem[DV], fin_dv[NT - 1])
            for dc in range(8):
                gp.dma_start(
                    out=win_o[:, dc, :],
                    in_=woutT_d.rearrange("(dc p) e -> p dc e", p=128)[:, dc, :]
                ).then_inc(sem[GQ], 16)
                cnt.add(GQ, 16)
        wino_gq = cnt.val(GQ)

        x2gq = [0] * (NT + 1)

        @block.gpsimd
        def _(gp):
            for c in range(2):
                gp.dma_start(out=xc2[:, c, :], in_=x_d[c * 128:(c + 1) * 128, :]
                             ).then_inc(sem[GQ], 16)
                x2gq[c] = cnt.add(GQ, 16)

        @block.gpsimd
        def _(gp):
            gp.wait_ge(sem[CC], cc_done)
            for hs in range(8):
                gp.dma_start(
                    out=ag_d2.rearrange("i (hs hl) -> i hs hl", hs=8)[:, hs, :],
                    in_=ag_d[hs * I_FFN:(hs + 1) * I_FFN, :]).then_inc(sem[GQ], 16)
                cnt.add(GQ, 16)
        agd2_gq = cnt.val(GQ)

        gs_a6 = place("gs_a6", [128, 1024], BF16)
        zc = place("zc", [128, 2, 1024], BF16)
        zT = place("zT", [128, 8, 2048], BF16)

        ztt = [0] * (NT + 1)
        zTsy = [0] * (NT + 1)
        silu6 = [0] * (NT + 1)
        with ExitStack() as ph:
            uv_ps = [ph.enter_context(nc.psum_tensor("uv_ps%d" % i, [128, 2048], F32))
                     for i in range(2)]
            for c in range(NT):
                b = c % 2

                @block.tensor
                def _(ten, c=c, b=b, pT=y2Tsy[c]):
                    ten.wait_ge(sem[SY], pT)
                    if c == 0:
                        ten.wait_ge(sem[DV], y2_dv)
                        ten.wait_ge(sem[GQ], winsw_gq)
                    if c >= 2:
                        ten.wait_ge(sem[DV], ztt[c - 2])
                    last = None
                    for dc in range(8):
                        st = y2T[:, dc, c * 128:(c + 1) * 128]
                        for n in range(4):
                            last = ten.matmul(uv_ps[b][:, n * 512:(n + 1) * 512], st,
                                              win_sw[:, dc, n * 512:(n + 1) * 512],
                                              start=(dc == 0), stop=(dc == 7))
                    last.then_inc(sem[MM], 1)
                mmv = cnt.add(MM, 1)

                @block.scalar
                def _(scal, mmv=mmv, b=b, pz=ztt[c - 1] if c >= 1 else 0):
                    scal.wait_ge(sem[MM], mmv)
                    if pz:
                        scal.wait_ge(sem[DV], pz)
                    scal.activation(gs_a6[:], uv_ps[b][:, 1024:2048], Act.Silu
                                    ).then_inc(sem[AC], 1)
                silu6[c] = cnt.add(AC, 1)

                @block.vector
                def _(vec, c=c, b=b, acv=silu6[c],
                      pT=zTsy[c - 2] if c >= 2 else 0, y15=y2Tsy[NT - 1]):
                    vec.wait_ge(sem[AC], acv)
                    if c <= 1:
                        vec.wait_ge(sem[SY], y15)
                    if pT:
                        vec.wait_ge(sem[SY], pT)
                    vec.tensor_tensor(zc[:, c % 2, :], uv_ps[b][:, 0:1024],
                                      gs_a6[:], Alu.mult).then_inc(sem[DV], 1)
                ztt[c] = cnt.add(DV, 1)

                @block.sync
                def _(sync, c=c, zv=ztt[c]):
                    sync.wait_ge(sem[DV], zv)
                    sync.dma_start_transpose(
                        out=zT[:, :, c * 128:(c + 1) * 128],
                        in_=zc[:, c % 2, :].rearrange("p (a b) -> p a b", a=8)
                    ).then_inc(sem[SY], 16)
                    cnt.add(SY, 16)
                zTsy[c] = cnt.val(SY)
        z_dv = ztt[NT - 1]
        a6_mm_end = cnt.val(MM)

        scrA7 = place("scrA7", [128, 1024], F32)
        x1buf = place("x1buf", [128, 16, 1024], F32)

        x1tt = [0] * (NT + 1)
        with ExitStack() as ph:
            a_ps = [ph.enter_context(nc.psum_tensor("a_ps%d" % i, [128, 1024], F32))
                    for i in range(4)]
            for c in range(NT):
                b = c % 4

                @block.tensor
                def _(ten, c=c, b=b, pT=zTsy[c]):
                    ten.wait_ge(sem[SY], pT)
                    if c == 0:
                        ten.wait_ge(sem[GQ], wino_gq)
                    if c <= 3:
                        ten.wait_ge(sem[DV], z_dv)
                    if c >= 4:
                        ten.wait_ge(sem[DV], x1tt[c - 4])
                    last = None
                    for dc in range(8):
                        st = zT[:, dc, c * 128:(c + 1) * 128]
                        for n in range(2):
                            last = ten.matmul(a_ps[b][:, n * 512:(n + 1) * 512], st,
                                              win_o[:, dc, n * 512:(n + 1) * 512],
                                              start=(dc == 0), stop=(dc == 7))
                    last.then_inc(sem[MM], 1)
                mmv = cnt.add(MM, 1)

                @block.vector
                def _(vec, c=c, b=b, mmv=mmv, xg=x2gq[c]):
                    vec.wait_ge(sem[MM], mmv)
                    vec.wait_ge(sem[GQ], xg)
                    vec.scalar_tensor_tensor(x1buf[:, c, :], xc2[:, c % 2, :],
                                             2.0, a_ps[b][:], Alu.mult, Alu.add
                                             ).then_inc(sem[DV], 1)
                x1tt[c] = cnt.add(DV, 1)

                if c + 2 < NT:
                    @block.gpsimd
                    def _(gp, c=c, px=x1tt[c]):
                        gp.wait_ge(sem[DV], px)
                        gp.dma_start(out=xc2[:, c % 2, :],
                                     in_=x_d[(c + 2) * 128:(c + 3) * 128, :]
                                     ).then_inc(sem[GQ], 16)
                        x2gq[c + 2] = cnt.add(GQ, 16)

                @block.scalar
                def _(scal, c=c, px=x1tt[c]):
                    scal.wait_ge(sem[DV], px)
                    scal.activation(scrA7[:], x1buf[:, c, :], Act.Copy,
                                    accum_out=st_sum[:, c:c + 1]).then_inc(sem[AC], 1)
                cnt.add(AC, 1)

                @block.vector
                def _(vec, c=c, px=x1tt[c]):
                    vec.wait_ge(sem[DV], px)
                    vec.tensor_tensor_reduce(
                        out=scrA7[:], in0=x1buf[:, c, :], in1=x1buf[:, c, :],
                        scale=1.0, scalar=0.0, op0=Alu.mult, op1=Alu.add,
                        accum_out=st_sq[:, c:c + 1]).then_inc(sem[DV], 1)
                cnt.add(DV, 1)
        a7_ac = cnt.val(AC)
        a7_dv2 = cnt.val(DV)
        a7_mm_end = cnt.val(MM)

        def emit_ln_batch(prev_ac_total, prev_dv_total, a=0, b=16):
            @block.vector
            def _(vec):
                vec.wait_ge(sem[AC], prev_ac_total)
                vec.wait_ge(sem[DV], prev_dv_total)
                vec.tensor_scalar(st_mean[:, a:b], st_sum[:, a:b],
                                  1.0 / 1024.0, None, Alu.mult).then_inc(sem[DV], 1)
                vec.wait_ge(sem[DV], cnt.add(DV, 1))
                vec.tensor_scalar(st_var[:, a:b], st_sq[:, a:b],
                                  1.0 / 1024.0, None, Alu.mult).then_inc(sem[DV], 1)
                vec.wait_ge(sem[DV], cnt.add(DV, 1))
                vec.tensor_tensor(st_sq[:, a:b], st_mean[:, a:b], st_mean[:, a:b],
                                  Alu.mult).then_inc(sem[DV], 1)
                vec.wait_ge(sem[DV], cnt.add(DV, 1))
                vec.tensor_tensor(st_var[:, a:b], st_var[:, a:b], st_sq[:, a:b],
                                  Alu.subtract).then_inc(sem[DV], 1)
            dvv = cnt.add(DV, 1)

            @block.scalar
            def _(scal, dvv=dvv):
                scal.wait_ge(sem[DV], dvv)
                scal.activation(st_sq[:, a:b], st_var[:, a:b], Act.Ln, bias=c_lneps[:]
                                ).then_inc(sem[AC], 1)
                scal.wait_ge(sem[AC], cnt.add(AC, 1))
                scal.activation(st_rstd[:, a:b], st_sq[:, a:b], Act.Exp,
                                scale=c_nhalf[:]).then_inc(sem[AC], 1)
            return cnt.add(AC, 1)

        rstd_ac = emit_ln_batch(a7_ac, a7_dv2)

        x1ap = [0] * (NT + 1)

        @block.vector
        def _(vec):
            vec.wait_ge(sem[AC], rstd_ac)
            for c in range(NT):
                vec.tensor_scalar(x1buf[:, c, :], x1buf[:, c, :],
                                  st_mean[:, c:c + 1], st_rstd[:, c:c + 1],
                                  Alu.subtract, Alu.mult).then_inc(sem[DV], 1)
                x1ap[c] = cnt.add(DV, 1)
        x1_dv = x1ap[NT - 1]

        wg_win = place("wg_win", [128, 8, 1408], BF16)
        wu_win = place("wu_win", [128, 8, 1408], BF16)

        @block.gpsimd
        def _(gp):
            gp.wait_ge(sem[CC], cc_done)
            gp.wait_ge(sem[MM], a6_mm_end)
            for dc in range(8):
                gp.dma_start(out=wu_win[:, dc, :],
                             in_=ag_u.rearrange("(dc p) i -> p dc i", p=128)
                             [:, dc, 0:1408]).then_inc(sem[GQ], 16)
                cnt.add(GQ, 16)
            gp.wait_ge(sem[MM], a7_mm_end)
            for dc in range(8):
                gp.dma_start(out=wg_win[:, dc, :],
                             in_=ag_g.rearrange("(dc p) i -> p dc i", p=128)
                             [:, dc, 0:1408]).then_inc(sem[GQ], 16)
                cnt.add(GQ, 16)
        wgwu_gq = cnt.val(GQ)

        xqc = place("xqc", [128, 2, 1024], BF16)
        xqs = [place("xqs0", [128, 8, 128], BF16), place("xqs1", [128, 8, 128], BF16)]
        xqw = place("xqw", [128, 2, 8, 128], BF16)
        wd_win = place("wd_win", [128, 22, 1024], BF16)
        hqw = place("hqw", [128, 2, 22, 128], BF16)
        outb = place("outb", [128, 2, 1024], F32)
        hq_sy = [0] * (NT + 1)
        wd_sy = [0] * 22
        hh = place("hh", [128, 1408], F32)
        gs2 = place("gs2", [128, 1408], BF16)
        hqs = place("hqs", [128, 11, 128], BF16)
        hqt = place("hqt", [128, 11, 128], BF16)

        f1Tsy = [0] * (NT + 1)
        f1store = [0] * (NT + 1)

        def emit_f1_pair(cp):
            w0 = [(DV, x1ap[cp + 1])]
            gw = [(SY, f1Tsy[cp - 1])] if cp >= 2 else []
            qk, gpq = emit_quant(
                x1buf[:, cp:cp + 2, :].rearrange("p a d -> p (a d)"),
                x1buf[:, cp:cp + 2, :].rearrange("p a (b k) -> p (a b) k", k=16),
                xqc[:, :, :].rearrange("p a d -> p (a d)"), 128, 2048,
                dv_wait=w0, ac_wait=w0, gp_wait=gw, out_on_dve=True)

            @block.sync
            def _(sync, cp=cp, qk=qk, gpq=gpq,
                  ps=f1store[cp - 1] if cp >= 2 else 0):
                sync.wait_ge(sem[qk], gpq)
                if ps:
                    sync.wait_ge(sem[SY], ps)
                for c in (cp, cp + 1):
                    sync.dma_start_transpose(
                        out=xqs[c % 2][:],
                        in_=xqc[:, c % 2, :].rearrange("p (a b) -> p a b", a=8)
                    ).then_inc(sem[SY], 16)
                    f1Tsy[c] = cnt.add(SY, 16)
                    sync.wait_ge(sem[SY], f1Tsy[c])
                    sync.dma_start(
                        out=xqT_dr[c].rearrange("(dc p) t -> p dc t", p=128),
                        in_=xqs[c % 2][:]).then_inc(sem[SY], 16)
                    f1store[c] = cnt.add(SY, 16)

        f2w_sy = [0, 0]
        f2w_s = [0] * 8
        f2w_g = [0] * 8
        SLICES = ((0, 512), (512, 1024), (1024, 1408))

        def emit_f2_chunk(ih, c, S):
            @block.tensor
            def _(ten, c=c, ih=ih, pq=S["xq_sy"][c], pa=S["silu_ac"][c - 1] if c >= 1 else 0,
                  ws=list(f2w_s), wg=list(f2w_g), psil=S.get("prev_sil", 0)):
                ten.wait_ge(sem[SY], pq)
                if c == 0 and ih == 0:
                    ten.wait_ge(sem[GQ], wgwu_gq)
                if c == 0 and psil:
                    ten.wait_ge(sem[AC], psil)
                if c <= 1 and ih == 1:
                    ten.wait_ge(sem[SY], ws[3])
                    ten.wait_ge(sem[GQ], wg[7])
                if pa:
                    ten.wait_ge(sem[AC], pa)
                last = None
                for dc in range(8):
                    st = xqw[:, c % 2, dc, :]
                    for (a, b2) in SLICES:
                        last = ten.matmul(g_ps[:, a:b2], st, wg_win[:, dc, a:b2],
                                          start=(dc == 0), stop=(dc == 7))
                last.then_inc(sem[MM], 1)
            mmg = cnt.add(MM, 1)

            @block.tensor
            def _(ten, c=c, ph_=S["hh_dv"][c - 1] if c >= 1 else 0,
                  phh=S.get("prev_hh", 0)):
                if ph_:
                    ten.wait_ge(sem[DV], ph_)
                if c == 0 and phh:
                    ten.wait_ge(sem[DV], phh)
                last = None
                for dc in range(8):
                    st = xqw[:, c % 2, dc, :]
                    for (a, b2) in SLICES:
                        last = ten.matmul(u_ps[:, a:b2], st, wu_win[:, dc, a:b2],
                                          start=(dc == 0), stop=(dc == 7))
                last.then_inc(sem[MM], 1)
            mmu = cnt.add(MM, 1)

            if c + 2 < NT:
                @block.sync
                def _(sync, c=c, ih=ih, mmu=mmu,
                      fs=f1store[c + 2] if ih == 0 else 0):
                    sync.wait_ge(sem[MM], mmu)
                    if fs:
                        sync.wait_ge(sem[SY], fs)
                    sync.dma_start(
                        out=xqw[:, c % 2, :, :],
                        in_=xqT_dr[c + 2].rearrange("(dc p) t -> p dc t", p=128)
                        ).then_inc(sem[SY], 16)
                    S["xq_sy"][c + 2] = cnt.add(SY, 16)

            @block.scalar
            def _(scal, mmg=mmg, ph_=S["hh_dv"][c - 1] if c >= 1 else 0):
                scal.wait_ge(sem[MM], mmg)
                if ph_:
                    scal.wait_ge(sem[DV], ph_)
                scal.activation(gs2[:], g_ps[:], Act.Silu).then_inc(sem[AC], 1)
            S["silu_ac"][c] = cnt.add(AC, 1)

            @block.vector
            def _(vec, mmu=mmu, psg=S["sgn_ac"][c - 1] if c >= 1 else 0,
                  sa=S["silu_ac"][c]):
                vec.wait_ge(sem[MM], mmu)
                vec.wait_ge(sem[AC], sa)
                if psg:
                    vec.wait_ge(sem[AC], psg)
                vec.tensor_tensor(hh[:], u_ps[:], gs2[:], Alu.mult
                                  ).then_inc(sem[DV], 1)
            S["hh_dv"][c] = cnt.add(DV, 1)

            S["sgn_ac"][c] = cnt.val(AC) + 1
            qk, gpq = emit_quant(hh[:], hh[:].rearrange("p (b k) -> p b k", k=16),
                                 hqs[:].rearrange("p a b -> p (a b)"), 88, 1408,
                                 dv_wait=[(DV, S["hh_dv"][c])],
                                 ac_wait=[(DV, S["hh_dv"][c])],
                                 gp_wait=[(SY, S["hq_t"][c - 1])] if c >= 1 else [],
                                 m_on_gp=True, gset=c % 2)

            @block.sync
            def _(sync, c=c, ih=ih, qk=qk, gpq=gpq,
                  ps=S["hq_st"][c - 1] if c >= 1 else S.get("prev_hst", 0)):
                sync.wait_ge(sem[qk], gpq)
                if ps:
                    sync.wait_ge(sem[SY], ps)
                sync.dma_start_transpose(out=hqt[:], in_=hqs[:]
                                         ).then_inc(sem[SY], 16)
                S["hq_t"][c] = cnt.add(SY, 16)
                sync.wait_ge(sem[SY], S["hq_t"][c])
                sync.dma_start(
                    out=hqT_dr[c].rearrange("(it p) t -> p it t", p=128)
                    [:, ih * 11:(ih + 1) * 11, :],
                    in_=hqt[:]).then_inc(sem[SY], 16)
                S["hq_st"][c] = cnt.add(SY, 16)

        with ExitStack() as ph:
            g_ps = ph.enter_context(nc.psum_tensor("g_ps", [128, 1408], F32))
            u_ps = ph.enter_context(nc.psum_tensor("u_ps", [128, 1408], F32))

            S0 = {"xq_sy": [0] * (NT + 1), "silu_ac": [0] * (NT + 1),
                  "hh_dv": [0] * (NT + 1), "hq_t": [0] * (NT + 1),
                  "hq_st": [0] * (NT + 1), "sgn_ac": [0] * (NT + 1)}
            for cp in range(0, NT, 2):
                emit_f1_pair(cp)
                if cp == 0:
                    @block.sync
                    def _(sync):
                        for cc in range(2):
                            sync.wait_ge(sem[SY], f1store[cc])
                            sync.dma_start(
                                out=xqw[:, cc % 2, :, :],
                                in_=xqT_dr[cc].rearrange("(dc p) t -> p dc t", p=128)
                                ).then_inc(sem[SY], 16)
                            S0["xq_sy"][cc] = cnt.add(SY, 16)
                if cp >= 2:
                    emit_f2_chunk(0, cp - 2, S0)
                    emit_f2_chunk(0, cp - 1, S0)
            emit_f2_chunk(0, NT - 2, S0)

            pm_ih0 = cnt.val(MM) + 2

            S1 = {"xq_sy": [0] * (NT + 1), "silu_ac": [0] * (NT + 1),
                  "hh_dv": [0] * (NT + 1), "hq_t": [0] * (NT + 1),
                  "hq_st": [0] * (NT + 1), "sgn_ac": [0] * (NT + 1)}

            @block.sync
            def _(sync, pm=pm_ih0):
                sync.wait_ge(sem[MM], pm)
                for cc in range(2):
                    sync.dma_start(
                        out=xqw[:, cc % 2, :, :],
                        in_=xqT_dr[cc].rearrange("(dc p) t -> p dc t", p=128)
                        ).then_inc(sem[SY], 16)
                    S1["xq_sy"][cc] = cnt.add(SY, 16)
                for dc in range(4):
                    sync.dma_start(out=wg_win[:, dc, :],
                                   in_=ag_g.rearrange("(dc p) i -> p dc i", p=128)
                                   [:, dc, 1408:2816]).then_inc(sem[SY], 16)
                    cnt.add(SY, 16)
                    sync.dma_start(out=wu_win[:, dc, :],
                                   in_=ag_u.rearrange("(dc p) i -> p dc i", p=128)
                                   [:, dc, 1408:2816]).then_inc(sem[SY], 16)
                    f2w_s[dc] = cnt.add(SY, 16)

            @block.gpsimd
            def _(gp, pm=pm_ih0):
                gp.wait_ge(sem[MM], pm)
                for dc in range(4, 8):
                    gp.dma_start(out=wg_win[:, dc, :],
                                 in_=ag_g.rearrange("(dc p) i -> p dc i", p=128)
                                 [:, dc, 1408:2816]).then_inc(sem[GQ], 16)
                    cnt.add(GQ, 16)
                    gp.dma_start(out=wu_win[:, dc, :],
                                 in_=ag_u.rearrange("(dc p) i -> p dc i", p=128)
                                 [:, dc, 1408:2816]).then_inc(sem[GQ], 16)
                    f2w_g[dc] = cnt.add(GQ, 16)

            emit_f2_chunk(0, NT - 1, S0)
            f2w_sy[1] = f2w_s[3]
            S1["prev_sil"] = S0["silu_ac"][NT - 1]
            S1["prev_hst"] = S0["hq_st"][NT - 1]
            S1["prev_hh"] = S0["hh_dv"][NT - 1]
            for c in range(NT - 1):
                emit_f2_chunk(1, c, S1)

            pm_f2 = cnt.val(MM) + 2

            @block.sync
            def _(sync, pm=pm_f2):
                sync.wait_ge(sem[MM], pm)
                sync.wait_ge(sem[GQ], agd2_gq)
                for c in range(2):
                    sync.wait_ge(sem[SY], S1["hq_st"][c])
                    sync.dma_start(
                        out=hqw[:, c % 2, :, :],
                        in_=hqT_dr[c].rearrange("(it p) t -> p it t", p=128)
                        ).then_inc(sem[SY], 16)
                    hq_sy[c] = cnt.add(SY, 16)
                for ic in range(11):
                    sync.dma_start(
                        out=wd_win[:, ic, :],
                        in_=ag_d2[ic * 128:(ic + 1) * 128, :]).then_inc(sem[SY], 16)
                    wd_sy[ic] = cnt.add(SY, 16)

            emit_f2_chunk(1, NT - 1, S1)
        hq_done_sy = cnt.val(SY)
        f2_gp = cnt.val(GP)

        wd_gq = [0] * 22

        @block.gpsimd
        def _(gp):
            gp.wait_ge(sem[SY], hq_done_sy)
            gp.wait_ge(sem[GQ], agd2_gq)
            for ic in range(11, 22):
                gp.dma_start(
                    out=wd_win[:, ic, :],
                    in_=ag_d2[ic * 128:(ic + 1) * 128, :]).then_inc(sem[GQ], 16)
                wd_gq[ic] = cnt.add(GQ, 16)

        out_gq = [0] * (NT + 1)

        def emit_out_half(a, b, rstd_v):
            for c in range(a, b):
                @block.vector
                def _(vec, c=c, ra=rstd_v, po=out_gq[c - 2] if c - 2 >= 0 else 0):
                    vec.wait_ge(sem[AC], ra)
                    if po:
                        vec.wait_ge(sem[GQ], po)
                    vec.tensor_scalar(outb[:, c % 2, :], x1buf[:, c, :],
                                      st_mean[:, c:c + 1], st_rstd[:, c:c + 1],
                                      Alu.subtract, Alu.mult).then_inc(sem[DV], 1)
                dvv = cnt.add(DV, 1)

                @block.gpsimd
                def _(gp, c=c, dvv=dvv):
                    gp.wait_ge(sem[DV], dvv)
                    gp.dma_start(out=out_d[c * 128:(c + 1) * 128, :],
                                 in_=outb[:, c % 2, :]).then_inc(sem[GQ], 16)
                    out_gq[c] = cnt.add(GQ, 16)

        res_dv = [0] * (NT + 1)
        with ExitStack() as ph:
            d_ps = [ph.enter_context(nc.psum_tensor("d_ps%d" % i, [128, 1024], F32))
                    for i in range(2)]
            for c in range(NT):
                b = c % 2

                @block.tensor
                def _(ten, c=c, b=b, pq=hq_sy[c]):
                    ten.wait_ge(sem[SY], pq)
                    if c >= 2:
                        ten.wait_ge(sem[DV], res_dv[c - 2])
                    if c <= 1:
                        ten.wait_ge(sem[SY], wd_sy[10])
                        ten.wait_ge(sem[GQ], wd_gq[21])
                    last = None
                    for ic in range(22):
                        st = hqw[:, c % 2, ic, :]
                        for n in range(2):
                            last = ten.matmul(d_ps[b][:, n * 512:(n + 1) * 512], st,
                                              wd_win[:, ic, n * 512:(n + 1) * 512],
                                              start=(ic == 0), stop=(ic == 21))
                    last.then_inc(sem[MM], 1)
                mmv = cnt.add(MM, 1)

                @block.vector
                def _(vec, c=c, b=b, mmv=mmv):
                    vec.wait_ge(sem[MM], mmv)
                    vec.tensor_tensor(x1buf[:, c, :], x1buf[:, c, :], d_ps[b][:],
                                      Alu.add).then_inc(sem[DV], 1)
                res_dv[c] = cnt.add(DV, 1)

                if c + 2 < NT:
                    @block.sync
                    def _(sync, c=c, pr=res_dv[c]):
                        sync.wait_ge(sem[DV], pr)
                        sync.dma_start(
                            out=hqw[:, c % 2, :, :],
                            in_=hqT_dr[c + 2].rearrange("(it p) t -> p it t", p=128)
                            ).then_inc(sem[SY], 16)
                        hq_sy[c + 2] = cnt.add(SY, 16)

                @block.scalar
                def _(scal, c=c, pr=res_dv[c]):
                    scal.wait_ge(sem[DV], pr)
                    scal.activation(aux1k[:], x1buf[:, c, :], Act.Copy,
                                    accum_out=st_sum[:, c:c + 1]).then_inc(sem[AC], 1)
                cnt.add(AC, 1)

                @block.vector
                def _(vec, c=c, pr=res_dv[c]):
                    vec.wait_ge(sem[DV], pr)
                    vec.tensor_tensor_reduce(
                        out=aux1k[:], in0=x1buf[:, c, :], in1=x1buf[:, c, :],
                        scale=1.0, scalar=0.0, op0=Alu.mult, op1=Alu.add,
                        accum_out=st_sq[:, c:c + 1]).then_inc(sem[DV], 1)
                cnt.add(DV, 1)

                if c == 7:
                    rstdA = emit_ln_batch(cnt.val(AC), cnt.val(DV), 0, 8)
                    emit_out_half(0, 8, rstdA)
        rstdB = emit_ln_batch(cnt.val(AC), cnt.val(DV), 8, 16)
        emit_out_half(8, 16, rstdB)

        @block.sync
        def _(sync):
            sync.wait_ge(sem[SY], cnt.val(SY))
            sync.wait_ge(sem[GQ], cnt.val(GQ))

    return nc


def kernel(L0_attn_x, L0_attn_w_qkv, L0_attn_w_swiglu, L0_attn_w_out,
           L0_ffn_w_gate, L0_ffn_w_up, L0_ffn_w_down):
    bf16 = ml_dtypes.bfloat16
    x = np.ascontiguousarray(np.asarray(L0_attn_x, np.float32))
    wqkvT = np.ascontiguousarray(np.asarray(L0_attn_w_qkv, np.float32).T)
    wswT = np.ascontiguousarray(np.asarray(L0_attn_w_swiglu, np.float32).T)
    woutT = np.ascontiguousarray(np.asarray(L0_attn_w_out, np.float32).T)
    wg = np.asarray(L0_ffn_w_gate, np.float32)
    wu = np.asarray(L0_ffn_w_up, np.float32)
    wd = np.asarray(L0_ffn_w_down, np.float32)

    L128 = np.triu(np.ones((128, 128), np.float32)).astype(bf16)
    colsel = np.zeros((128, 16, 16), np.float32)
    for c in range(16):
        colsel[:, c, c] = 1.0
    colsel = colsel.astype(bf16)
    rowsel = np.zeros((16, 16, 128), np.float32)
    for c in range(16):
        rowsel[c, c, :] = 1.0
    rowsel = rowsel.astype(bf16)
    pfx16 = np.zeros((16, 16), np.float32)
    for c in range(16):
        pfx16[:c, c] = 1.0
    pfx16 = pfx16.astype(bf16)
    ident = np.eye(128, dtype=np.float32).astype(bf16)

    nc = build_program()
    mybir.codegen_inst_isa_subclasses(nc)

    in_maps = []
    for core in range(8):
        in_maps.append({
            "x": np.ascontiguousarray(x[core]),
            "wqkvT": wqkvT,
            "wswT": wswT,
            "woutT": woutT,
            "wg_slab": np.ascontiguousarray(
                wg[:, core * 128:(core + 1) * 128].reshape(22, 128, 128)
                .transpose(1, 0, 2).reshape(128, I_FFN)),
            "wu_slab": np.ascontiguousarray(
                wu[:, core * 128:(core + 1) * 128].reshape(22, 128, 128)
                .transpose(1, 0, 2).reshape(128, I_FFN)),
            "wd_slab": np.ascontiguousarray(wd[core * 128:(core + 1) * 128, :]),
            "cL128": L128,
            "ccolsel": colsel,
            "crowsel": rowsel,
            "cpfx17": pfx16,
            "cident": ident,
        })
    res = run_bass_kernel_spmd(nc, in_maps, core_ids=list(range(8)))
    out = np.stack([np.asarray(res.results[i]["out"], np.float32) for i in range(8)], 0)
    return out


# version 30 (final)
# speedup vs baseline: 1.0293x; 1.0112x over previous
import numpy as np
import ml_dtypes
import concourse.bass as bass
import concourse.mybir as mybir
from concourse.bass_utils import run_bass_kernel_spmd
from contextlib import ExitStack

F32 = mybir.dt.float32
BF16 = mybir.dt.bfloat16
Alu = mybir.AluOpType
Act = mybir.ActivationFunctionType
AxX = mybir.AxisListType.X

import concourse.dve_ops as _dvo
from concourse.dve_op import DveOp
from concourse.dve_spec import (Spec, lower as _dve_lower, Src0, Src1, C0, C1, C2, C3,
                                Zero, maxx, _spill_c3_to_src1)
from concourse.dve_table_gen import dve_ver_for

_VER = dve_ver_for("TRN2")


def _reg_op(name, spec, reference, row):
    if name in _dvo._SUB_OPCODE_FOR_NAME:
        for op in _dvo.OPS:
            if op.name == name:
                return op
    compiled = _dve_lower(spec, ver=_VER, perf_en=False, name=name)
    op = DveOp(name, spec, subdim=False, uops_sha={_VER: compiled.sha(_VER)},
               reference=reference)
    _dvo.OPS.append(op)
    _dvo._SUB_OPCODE_FOR_NAME[name] = row
    _dvo.CUSTOM_DVE_SPECS[name] = op.spec
    _dvo.CUSTOM_DVE_REFS[name] = op.reference
    return op


FP4U = _reg_op("FP4U", Spec(body=maxx(Src0, Zero - Src0) * Src1),
               lambda in0, in1, s0, s1, imm2: np.abs(in0) * in1, 29)
_a1 = Src0 > C0
_a2 = Src0 > C1
_a3 = Src0 > C2
_a4 = Src0 > C3
FP4A = _reg_op("FP4A", Spec(body=_spill_c3_to_src1((_a1 + _a2) + (_a3 + _a4))),
               lambda in0, in1, s0, s1, imm2: (
                   (in0 > s0).astype(in0.dtype) + (in0 > s1).astype(in0.dtype)
                   + (in0 > imm2).astype(in0.dtype) + (in0 > in1[:, :1]).astype(in0.dtype)),
               30)
_b5 = Src0 > C0
_b6 = Src0 > C1
_b7 = Src0 > C2
_bt = (_b5 + _b6) + (_b7 + _b7)
FP4B = _reg_op("FP4B", Spec(body=Src1 + (_bt + _bt)),
               lambda in0, in1, s0, s1, imm2: (
                   in1 + 2.0 * ((in0 > s0).astype(in0.dtype) + (in0 > s1).astype(in0.dtype)
                                + 2.0 * (in0 > imm2).astype(in0.dtype))),
               31)

T, D, NT = 2048, 1024, 16
I_FFN = 2816
FP32_EPS = 1.1920929e-07

OFF = dict(
    cL=0, csel=256, cident=768, crow=1024,
    ssq_b=5120, lsc_b=5152, inv_b=5184, ninv_b=5216,
    sigtmp=5248,
    cpfx=9216,
    c_i1024=9280, c_rmse=9312, c_lneps=9344, c_nhalf=9376, c_none=9408,
    c_one=9440, c_c3a=9472,
    st_sum=9504, st_sq=9568, st_mean=9632, st_var=9696, st_rstd=9760,
    ssq_s=9824, lsc_s=9856, inv_n=9888, ninv_n=9920, c_eps6=9984,
    q_bmax=10048, q_s=10752, q_r=11456, q_sh=12160,
    q_u=12864, q_p4=24128, q_g2=29760, q_sgn=35392, q_m=41024,
    aux1k=12864,
    pf8=12864,
    e1s0=21056, e1s1=25152,
    lnscr=29248,
    ebufA=33344, ebufB=37440,
    th=41536, tl=42560,
    y2c=43584,
    wslab=58368, wdslab=69632, qbf=80896, tstage=86528,
    wbuf=47680,
    kvbuf=80448,
    xnT=113216,
    win_kv=145984,
    y2T=145984,
    xc=178752,
    xnc=186944,
    win_q=178752,
    win_sw=80448,
    zc=43584, gs_a6=41536, zT=47680,
    win_o=12864, xc2=29248, scrA7=168512, x1buf=80448,
    xqc=47680, xqs0=51776, xqs1=53824, wg_win=55872, wu_win=145984,
    xqw=178752, hh=168512, gs2=174144, hqs=182848, hqt=185664,
    q_g2b=188480, q_shb=194112, q_mb=41536,
    wd_win=145984, hqw=47680, outb=58944,
)
ARENA = 196608


class Counts:
    def __init__(self):
        self.c = {}

    def add(self, k, by):
        self.c[k] = self.c.get(k, 0) + by
        return self.c[k]

    def val(self, k):
        return self.c.get(k, 0)


def build_program():
    nc = bass.Bass("TRN2", target_bir_lowering=False, debug=False,
                   detect_race_conditions=False)

    x_d = nc.dram_tensor("x", [T, D], F32, kind="ExternalInput").ap()
    wqkvT_d = nc.dram_tensor("wqkvT", [D, 3 * D], F32, kind="ExternalInput").ap()
    wswT_d = nc.dram_tensor("wswT", [D, 2 * D], F32, kind="ExternalInput").ap()
    woutT_d = nc.dram_tensor("woutT", [D, D], F32, kind="ExternalInput").ap()
    wg_sl_d = nc.dram_tensor("wg_slab", [128, I_FFN], F32, kind="ExternalInput").ap()
    wu_sl_d = nc.dram_tensor("wu_slab", [128, I_FFN], F32, kind="ExternalInput").ap()
    wd_sl_d = nc.dram_tensor("wd_slab", [128, I_FFN], F32, kind="ExternalInput").ap()
    cL_d = nc.dram_tensor("cL128", [128, 128], BF16, kind="ExternalInput").ap()
    csel_d = nc.dram_tensor("ccolsel", [128, 16, 16], BF16, kind="ExternalInput").ap()
    crow_d = nc.dram_tensor("crowsel", [16, 16, 128], BF16, kind="ExternalInput").ap()
    cpfx_d = nc.dram_tensor("cpfx17", [16, 16], BF16, kind="ExternalInput").ap()
    cid_d = nc.dram_tensor("cident", [128, 128], BF16, kind="ExternalInput").ap()
    out_d = nc.dram_tensor("out", [T, D], F32, kind="ExternalOutput").ap()

    ag_in_g = nc.dram_tensor("ag_in_g", [128, I_FFN], BF16).ap()
    ag_in_u = nc.dram_tensor("ag_in_u", [128, I_FFN], BF16).ap()
    ag_in_d = nc.dram_tensor("ag_in_d", [I_FFN, 128], BF16).ap()
    ag_g = nc.dram_tensor("ag_g", [D, I_FFN], BF16).ap()
    ag_u = nc.dram_tensor("ag_u", [D, I_FFN], BF16).ap()
    ag_d = nc.dram_tensor("ag_d", [8 * I_FFN, 128], BF16).ap()
    xqT_dr = nc.dram_tensor("xqT_dr", [NT, D, 128], BF16).ap()
    hqT_dr = nc.dram_tensor("hqT_dr", [NT, I_FFN, 128], BF16).ap()
    ag_d2 = nc.dram_tensor("ag_d2", [I_FFN, 8 * 128], BF16).ap()

    cnt = Counts()
    SY, GQ, MM, DV, AC, GP, CC = "sy", "gq", "mm", "dv", "ac", "gp", "cc"

    with ExitStack() as top:
        sem = {s: top.enter_context(nc.semaphore("sem_" + s))
               for s in (SY, GQ, MM, DV, AC, GP, CC)}
        arena = top.enter_context(nc.sbuf_tensor("arena", [128, ARENA], mybir.dt.uint8))
        base = nc.lookup_mloc(arena).addr

        def place(key, shape, dtype):
            h = nc.virtual_sbuf_tensor("t_" + key, shape, dtype)
            nc._tensor(h.name, h.shape, h.dtype, type="SB")
            mloc = nc.lookup_mloc(h)
            mloc.allocated = True
            mloc.addr = base + OFF[key]
            return h

        cL = place("cL", [128, 128], BF16)
        csel = place("csel", [128, 16, 16], BF16)
        cident = place("cident", [128, 128], BF16)
        crow = place("crow", [16, 16, 128], BF16)
        cpfx = place("cpfx", [16, 16], BF16)
        c_i1024 = place("c_i1024", [128, 1], F32)
        c_rmse = place("c_rmse", [128, 1], F32)
        c_lneps = place("c_lneps", [128, 1], F32)
        c_nhalf = place("c_nhalf", [128, 1], F32)
        c_none = place("c_none", [128, 1], F32)
        c_one = place("c_one", [128, 1], F32)
        c_c3a = place("c_c3a", [128, 1], F32)
        c_eps6 = place("c_eps6", [128, 1], F32)
        st_sum = place("st_sum", [128, 16], F32)
        st_sq = place("st_sq", [128, 16], F32)
        st_mean = place("st_mean", [128, 16], F32)
        st_var = place("st_var", [128, 16], F32)
        st_rstd = place("st_rstd", [128, 16], F32)
        ssq_s = place("ssq_s", [128, 1], F32)
        lsc_s = place("lsc_s", [128, 1], F32)
        inv_n = place("inv_n", [128, 1], F32)
        ninv_n = place("ninv_n", [128, 1], F32)
        ssq_b = place("ssq_b", [128, 1], F32)
        lsc_b = place("lsc_b", [128, 1], F32)
        inv_b = place("inv_b", [128, 1], F32)
        ninv_b = place("ninv_b", [128, 1], F32)
        q_bmax = place("q_bmax", [128, 176], F32)
        q_s = place("q_s", [128, 176], F32)
        q_r = place("q_r", [128, 176], F32)
        q_sh = place("q_sh", [128, 176], F32)
        q_u = place("q_u", [128, 2816], F32)
        q_p4 = place("q_p4", [128, 2816], BF16)
        q_g2 = place("q_g2", [128, 2816], BF16)
        q_sgn = place("q_sgn", [128, 2816], BF16)
        q_m = place("q_m", [128, 2816], BF16)
        q_g2b = place("q_g2b", [128, 2816], BF16)
        q_mb = place("q_mb", [128, 2816], BF16)
        q_shb = place("q_shb", [128, 176], F32)
        aux1k = place("aux1k", [128, 1024], F32)

        block = top.enter_context(nc.Block())

        @block.sync
        def _(sync):
            sync.dma_start(out=cL[:], in_=cL_d).then_inc(sem[SY], 16)
            sync.dma_start(out=csel[:], in_=csel_d).then_inc(sem[SY], 16)
            sync.dma_start(out=crow[:], in_=crow_d).then_inc(sem[SY], 16)
            sync.dma_start(out=cpfx[:], in_=cpfx_d).then_inc(sem[SY], 16)
            sync.dma_start(out=cident[:], in_=cid_d).then_inc(sem[SY], 16)
            cnt.add(SY, 80)

        @block.vector
        def _(vec):
            for buf, v in ((c_i1024, 1.0 / 1024.0), (c_rmse, FP32_EPS),
                           (c_lneps, 1e-5), (c_nhalf, -0.5), (c_none, -1.0),
                           (c_one, 1.0), (c_c3a, 1.75), (c_eps6, 1e-6)):
                vec.memset(buf[:], v)
            vec.memset(inv_n[:], 1.0).then_inc(sem[DV], 1)
        consts_dv = cnt.add(DV, 1)

        xc = place("xc", [128, 2, 1024], F32)
        win_kv = place("win_kv", [128, 8, 2048], BF16)

        @block.gpsimd
        def _(gp):
            for c in range(2):
                gp.dma_start(out=xc[:, c, :], in_=x_d[c * 128:(c + 1) * 128, :]
                             ).then_inc(sem[GQ], 16)
                cnt.add(GQ, 16)
            for dc in range(8):
                gp.dma_start(
                    out=win_kv[:, dc, :],
                    in_=wqkvT_d.rearrange("(dc p) e -> p dc e", p=128)[:, dc, 1024:3072]
                ).then_inc(sem[GQ], 16)
                cnt.add(GQ, 16)
        xgq = [16, 32] + [0] * 16
        winkv_gq = cnt.val(GQ)

        last_m = [(DV, 0)]

        last_gpset = {0: 0, 1: 0}

        def emit_quant(X_ap, X3_ap, out_ap, nb, F, dv_wait=(), ac_wait=(), gp_wait=(),
                       out_on_dve=False, m_on_gp=False, gset=0):
            g2t, mt, sht = ((q_g2, q_m, q_sh), (q_g2b, q_mb, q_shb))[gset]
            prev_gp = last_gpset[gset]
            prev_mk, prev_m = last_m[0]
            r_b = q_r[0:128, 0:nb].unsqueeze(2).broadcast_to([128, nb, 16])
            sh_b = sht[0:128, 0:nb].unsqueeze(2).broadcast_to([128, nb, 16])
            u_f = q_u[0:128, 0:F]
            u_3 = q_u[0:128, 0:F].rearrange("p (b k) -> p b k", k=16)

            @block.scalar
            def _(scal):
                for s, v in ac_wait:
                    scal.wait_ge(sem[s], v)
                if prev_m:
                    scal.wait_ge(sem[prev_mk], prev_m)
                scal.activation(q_sgn[0:128, 0:F], X_ap, Act.Sign).then_inc(sem[AC], 1)
            acv = cnt.add(AC, 1)

            @block.vector
            def _(vec):
                for s, v in dv_wait:
                    vec.wait_ge(sem[s], v)
                vec.tensor_reduce(q_bmax[0:128, 0:nb], X3_ap, AxX, Alu.max,
                                  apply_absolute_value=True).then_inc(sem[DV], 1)
                vec.wait_ge(sem[DV], cnt.add(DV, 1))
                vec.tensor_scalar(q_s[0:128, 0:nb], q_bmax[0:128, 0:nb],
                                  1.0 / 6.0, 1e-12, Alu.mult, Alu.max
                                  ).then_inc(sem[DV], 1)
                vec.wait_ge(sem[DV], cnt.add(DV, 1))
                vec.reciprocal(q_r[0:128, 0:nb], q_s[0:128, 0:nb]).then_inc(sem[DV], 1)
                vec.wait_ge(sem[DV], cnt.add(DV, 1))
                vec._custom_dve(FP4U, out=u_3, in0=X3_ap, in1=r_b)
                vec._custom_dve(FP4A, out=q_p4[0:128, 0:F], in0=u_f, in1=c_c3a[:],
                                s0=0.25, s1=0.75, imm2=1.25)
                if prev_gp:
                    vec.wait_ge(sem[GP], prev_gp)
                vec.tensor_scalar(sht[0:128, 0:nb], q_s[0:128, 0:nb],
                                  0.5, None, Alu.mult)
                m3 = mt[0:128, 0:F].rearrange("p (b k) -> p b k", k=16)
                s3 = q_sgn[0:128, 0:F].rearrange("p (b k) -> p b k", k=16)
                if m_on_gp:
                    vec._custom_dve(FP4B, out=g2t[0:128, 0:F], in0=u_f,
                                    in1=q_p4[0:128, 0:F],
                                    s0=2.5, s1=3.5, imm2=5.0).then_inc(sem[DV], 1)
                elif out_on_dve:
                    vec._custom_dve(FP4B, out=g2t[0:128, 0:F], in0=u_f,
                                    in1=q_p4[0:128, 0:F],
                                    s0=2.5, s1=3.5, imm2=5.0)
                    vec.wait_ge(sem[AC], acv)
                    vec.tensor_tensor(m3, s3, sh_b, Alu.mult)
                    for s, v in gp_wait:
                        vec.wait_ge(sem[s], v)
                    vec.tensor_tensor(out_ap, g2t[0:128, 0:F], mt[0:128, 0:F],
                                      Alu.mult).then_inc(sem[DV], 1)
                else:
                    vec._custom_dve(FP4B, out=g2t[0:128, 0:F], in0=u_f,
                                    in1=q_p4[0:128, 0:F],
                                    s0=2.5, s1=3.5, imm2=5.0)
                    vec.wait_ge(sem[AC], acv)
                    vec.tensor_tensor(m3, s3, sh_b, Alu.mult).then_inc(sem[DV], 1)
            dvv = cnt.add(DV, 1)
            if out_on_dve:
                last_m[0] = (DV, dvv)
                return (DV, dvv)

            if m_on_gp:
                @block.gpsimd
                def _(gp):
                    for s, v in gp_wait:
                        gp.wait_ge(sem[s], v)
                    gp.wait_ge(sem[DV], dvv)
                    gp.wait_ge(sem[AC], acv)
                    gp.tensor_tensor(
                        mt[0:128, 0:F].rearrange("p (b k) -> p b k", k=16),
                        q_sgn[0:128, 0:F].rearrange("p (b k) -> p b k", k=16),
                        sh_b, Alu.mult)
                    gp.tensor_tensor(out_ap, g2t[0:128, 0:F], mt[0:128, 0:F],
                                     Alu.mult).then_inc(sem[GP], 1)
                gpv = cnt.add(GP, 1)
                last_m[0] = (GP, gpv)
                last_gpset[gset] = gpv
                return (GP, gpv)

            last_m[0] = (DV, dvv)

            @block.gpsimd
            def _(gp):
                for s, v in gp_wait:
                    gp.wait_ge(sem[s], v)
                gp.wait_ge(sem[DV], dvv)
                gp.tensor_tensor(out_ap, g2t[0:128, 0:F], mt[0:128, 0:F],
                                 Alu.mult).then_inc(sem[GP], 1)
            gpv2 = cnt.add(GP, 1)
            last_gpset[gset] = gpv2
            return (GP, gpv2)

        xnc = place("xnc", [128, 2, 1024], BF16)
        xnT = place("xnT", [128, 8, 2048], BF16)
        wbuf = place("wbuf", [128, 16, 1024], BF16)
        kvbuf = place("kvbuf", [128, 16, 1024], BF16)

        a2_ac = [0] * (NT + 1)
        a2ts = [0] * (NT + 1)
        a2Tsy = [0] * (NT + 1)
        xsy2 = [0] * (NT + 2)
        a4_ac = [0] * (NT + 1)
        kvtt = [0] * (NT + 1)

        def emit_a2_chunk(c):
            @block.vector
            def _(vec, c=c, pa=a2_ac[c - 1] if c >= 1 else 0, px=xsy2[c]):
                if c < 2:
                    vec.wait_ge(sem[GQ], xgq[c])
                else:
                    vec.wait_ge(sem[SY], px)
                if c == 0:
                    vec.wait_ge(sem[DV], consts_dv)
                else:
                    vec.wait_ge(sem[AC], pa)
                vec.tensor_tensor_reduce(
                    out=aux1k[:], in0=xc[:, c % 2, :], in1=xc[:, c % 2, :],
                    scale=1.0, scalar=0.0, op0=Alu.mult, op1=Alu.add,
                    accum_out=ssq_s[:]).then_inc(sem[DV], 1)
            a2ttr = cnt.add(DV, 1)

            @block.scalar
            def _(scal, c=c, a2ttr=a2ttr, pts=a2ts[c - 1] if c >= 1 else 0):
                scal.wait_ge(sem[DV], a2ttr)
                if pts:
                    scal.wait_ge(sem[DV], pts)
                scal.activation(lsc_s[:], ssq_s[:], Act.Ln,
                                scale=c_i1024[:], bias=c_rmse[:]).then_inc(sem[AC], 1)
                scal.wait_ge(sem[AC], cnt.add(AC, 1))
                scal.activation(inv_n[:], lsc_s[:], Act.Exp,
                                scale=c_nhalf[:]).then_inc(sem[AC], 1)
            a2_ac[c] = cnt.add(AC, 1)

            @block.vector
            def _(vec, c=c, acv=a2_ac[c], pT=a2Tsy[c - 2] if c >= 2 else 0):
                vec.wait_ge(sem[AC], acv)
                if pT:
                    vec.wait_ge(sem[SY], pT)
                vec.tensor_scalar(xnc[:, c % 2, :], xc[:, c % 2, :],
                                  inv_n[:], None, Alu.mult).then_inc(sem[DV], 1)
            a2ts[c] = cnt.add(DV, 1)

            @block.sync
            def _(sync, c=c, pts=a2ts[c]):
                sync.wait_ge(sem[DV], pts)
                sync.dma_start_transpose(
                    out=xnT[:, :, c * 128:(c + 1) * 128],
                    in_=xnc[:, c % 2, :].rearrange("p (a b) -> p a b", a=8)
                ).then_inc(sem[SY], 16)
                a2Tsy[c] = cnt.add(SY, 16)
                if c + 2 < NT:
                    sync.dma_start(out=xc[:, c % 2, :],
                                   in_=x_d[(c + 2) * 128:(c + 3) * 128, :]
                                   ).then_inc(sem[SY], 16)
                    xsy2[c + 2] = cnt.add(SY, 16)

        def emit_a4_chunk(c):
            b = c % 2

            @block.tensor
            def _(ten, c=c, b=b, pT=a2Tsy[c]):
                ten.wait_ge(sem[SY], pT)
                if c == 0:
                    ten.wait_ge(sem[GQ], winkv_gq)
                    ten.wait_ge(sem[SY], d_stored)
                if c >= 2:
                    ten.wait_ge(sem[DV], kvtt[c - 2])
                last = None
                for dc in range(8):
                    st = xnT[:, dc, c * 128:(c + 1) * 128]
                    for n in range(2):
                        last = ten.matmul(k_ps[b][:, n * 512:(n + 1) * 512], st,
                                          win_kv[:, dc, n * 512:(n + 1) * 512],
                                          start=(dc == 0), stop=(dc == 7))
                    for n in range(2):
                        last = ten.matmul(v_ps[b][:, n * 512:(n + 1) * 512], st,
                                          win_kv[:, dc, 1024 + n * 512:1024 + (n + 1) * 512],
                                          start=(dc == 0), stop=(dc == 7))
                last.then_inc(sem[MM], 1)
            mmv = cnt.add(MM, 1)

            @block.scalar
            def _(scal, c=c, b=b, mmv=mmv):
                scal.wait_ge(sem[MM], mmv)
                scal.activation(aux1k[:], k_ps[b][:], Act.Square,
                                accum_out=ssq_b[:]).then_inc(sem[AC], 1)
                scal.wait_ge(sem[AC], cnt.add(AC, 1))
                scal.activation(lsc_b[:], ssq_b[:], Act.Ln,
                                scale=c_i1024[:], bias=c_rmse[:]).then_inc(sem[AC], 1)
                scal.wait_ge(sem[AC], cnt.add(AC, 1))
                scal.activation(inv_b[:], lsc_b[:], Act.Exp, scale=c_nhalf[:]
                                ).then_inc(sem[AC], 1)
                scal.wait_ge(sem[AC], cnt.add(AC, 1))
                scal.activation(wbuf[:, c, :], k_ps[b][:], Act.Exp,
                                scale=inv_b[:]).then_inc(sem[AC], 1)
            a4_ac[c] = cnt.add(AC, 1)

            @block.vector
            def _(vec, c=c, b=b, acv=a4_ac[c]):
                vec.wait_ge(sem[AC], acv)
                vec.tensor_tensor(kvbuf[:, c, :], wbuf[:, c, :], v_ps[b][:],
                                  Alu.mult).then_inc(sem[DV], 1)
            kvtt[c] = cnt.add(DV, 1)

        emit_a2_chunk(0)
        emit_a2_chunk(1)

        wslab = place("wslab", [128, 22, 128], F32)
        wdslab = place("wdslab", [128, 22, 128], F32)
        qbf = place("qbf", [128, 2816], BF16)
        tstage = place("tstage", [128, 22, 128], BF16)

        @block.sync
        def _(sync):
            sync.dma_start(out=wslab[:],
                           in_=wg_sl_d.rearrange("p (it h) -> p it h", it=22)
                           ).then_inc(sem[SY], 16)
            cnt.add(SY, 16)
            sync.dma_start(out=wdslab[:],
                           in_=wu_sl_d.rearrange("p (it h) -> p it h", it=22)
                           ).then_inc(sem[SY], 16)
            cnt.add(SY, 16)
        wu_ld = cnt.val(SY)
        wg_ld = wu_ld - 16

        wsl_f = wslab[:].rearrange("p it h -> p (it h)")
        wsl_3 = wslab[:].rearrange("p it (b k) -> p (it b) k", k=16)
        wdl_f = wdslab[:].rearrange("p it i -> p (it i)")
        wdl_3 = wdslab[:].rearrange("p it (b k) -> p (it b) k", k=16)

        kg, vg = emit_quant(wsl_f, wsl_3, qbf[:], 176, 2816,
                            dv_wait=[(SY, wg_ld), (DV, consts_dv)],
                            ac_wait=[(SY, wg_ld)], out_on_dve=True)

        @block.sync
        def _(sync):
            sync.wait_ge(sem[kg], vg)
            sync.dma_start_transpose(
                out=tstage[:], in_=qbf[:].rearrange("p (it h) -> p it h", it=22)
            ).then_inc(sem[SY], 16)
            cnt.add(SY, 16)
            sync.wait_ge(sem[SY], cnt.val(SY))
            sync.dma_start(out=ag_in_g,
                           in_=tstage[:].rearrange("p it h -> p (it h)")
                           ).then_inc(sem[SY], 16)
            cnt.add(SY, 16)
            sync.dma_start(out=wslab[:],
                           in_=wd_sl_d.rearrange("p (it i) -> p it i", it=22)
                           ).then_inc(sem[SY], 16)
            cnt.add(SY, 16)
        wd_ld = cnt.val(SY)
        g_stored = wd_ld - 16
        gtr_sy = wd_ld - 32

        ku, vu = emit_quant(wdl_f, wdl_3, qbf[:], 176, 2816,
                            dv_wait=[(SY, wu_ld)], ac_wait=[(SY, wu_ld)],
                            gp_wait=[(SY, gtr_sy)], out_on_dve=True)

        @block.sync
        def _(sync):
            sync.wait_ge(sem[ku], vu)
            sync.dma_start_transpose(
                out=tstage[:], in_=qbf[:].rearrange("p (it h) -> p it h", it=22)
            ).then_inc(sem[SY], 16)
            cnt.add(SY, 16)
            sync.wait_ge(sem[SY], cnt.val(SY))
            sync.dma_start(out=ag_in_u,
                           in_=tstage[:].rearrange("p it h -> p (it h)")
                           ).then_inc(sem[SY], 16)
            cnt.add(SY, 16)
        u_stored = cnt.val(SY)
        utr_sy = u_stored - 16

        kd, vd = emit_quant(wsl_f, wsl_3, qbf[:], 176, 2816,
                            dv_wait=[(SY, wd_ld)], ac_wait=[(SY, wd_ld)],
                            gp_wait=[(SY, utr_sy)], out_on_dve=True)

        @block.sync
        def _(sync):
            sync.wait_ge(sem[kd], vd)
            sync.dma_start_transpose(
                out=tstage[:], in_=qbf[:].rearrange("p (it i) -> p it i", it=22)
            ).then_inc(sem[SY], 16)
            cnt.add(SY, 16)
            sync.wait_ge(sem[SY], cnt.val(SY))
            sync.dma_start(out=ag_in_d.rearrange("(it q) h -> q it h", q=128),
                           in_=tstage[:]).then_inc(sem[SY], 16)
            cnt.add(SY, 16)
        d_stored = cnt.val(SY)

        with ExitStack() as ph:
            k_ps = [ph.enter_context(nc.psum_tensor("k_ps%d" % i, [128, 1024], F32))
                    for i in range(2)]
            v_ps = [ph.enter_context(nc.psum_tensor("v_ps%d" % i, [128, 1024], F32))
                    for i in range(2)]
            for c in range(2, NT):
                emit_a2_chunk(c)
                emit_a4_chunk(c - 2)
            emit_a4_chunk(NT - 2)
            emit_a4_chunk(NT - 1)
        a4_all_dv = kvtt[NT - 1]
        a4_all_ac = a4_ac[NT - 1]
        xn_done_sy = a2Tsy[NT - 1]

        @block.gpsimd
        def _(gp):
            gp.wait_ge(sem[SY], g_stored)
            gp.collective_compute("AllGather", Alu.bypass,
                                  replica_groups=[list(range(8))],
                                  ins=[ag_in_g.opt()], outs=[ag_g.opt()]
                                  ).then_inc(sem[CC], 1)
            cnt.add(CC, 1)
            gp.wait_ge(sem[SY], u_stored)
            gp.collective_compute("AllGather", Alu.bypass,
                                  replica_groups=[list(range(8))],
                                  ins=[ag_in_u.opt()], outs=[ag_u.opt()]
                                  ).then_inc(sem[CC], 1)
            cnt.add(CC, 1)
            gp.wait_ge(sem[SY], d_stored)
            gp.collective_compute("AllGather", Alu.bypass,
                                  replica_groups=[list(range(8))],
                                  ins=[ag_in_d.opt()], outs=[ag_d.opt()]
                                  ).then_inc(sem[CC], 1)
            cnt.add(CC, 1)
        cc_done = cnt.val(CC)

        win_q = place("win_q", [128, 8, 1024], BF16)

        @block.gpsimd
        def _(gp):
            gp.wait_ge(sem[SY], xn_done_sy)
            for dc in range(8):
                gp.dma_start(
                    out=win_q[:, dc, :],
                    in_=wqkvT_d.rearrange("(dc p) e -> p dc e", p=128)[:, dc, 0:1024]
                ).then_inc(sem[GQ], 16)
                cnt.add(GQ, 16)
        winq_gq = cnt.val(GQ)


        pf8 = place("pf8", [16, 8, 512], BF16)
        th = place("th", [16, 512], BF16)
        tl = place("tl", [16, 512], BF16)
        e1s = [place("e1s0", [128, 1024], F32), place("e1s1", [128, 1024], F32)]
        sigtmp = place("sigtmp", [128, 1024], BF16)
        lnscr = place("lnscr", [128, 1024], F32)
        ebuf = [place("ebufA", [128, 1024], F32), place("ebufB", [128, 1024], F32)]
        y2c = place("y2c", [128, 2, 1024], BF16)
        y2T = place("y2T", [128, 8, 2048], BF16)

        mmcs = [0] * (NT + 1)
        with ExitStack() as ph:
            with ExitStack() as php:
                tot_ps = php.enter_context(nc.psum_tensor("tot_ps", [16, 512], F32))
                pfx_ps = php.enter_context(nc.psum_tensor("pfx_ps", [16, 512], F32))
                for sl in range(2):
                    for ti, buf in enumerate((wbuf, kvbuf)):
                        j = (sl * 2 + ti) * 2

                        @block.tensor
                        def _(ten, sl=sl, buf=buf):
                            ten.wait_ge(sem[DV], a4_all_dv)
                            ten.wait_ge(sem[AC], a4_all_ac)
                            last = None
                            for c in range(16):
                                last = ten.matmul(tot_ps[:], csel[:, c, :],
                                                  buf[:, c, sl * 512:(sl + 1) * 512],
                                                  start=(c == 0), stop=(c == 15))
                            last.then_inc(sem[MM], 1)
                        mmt = cnt.add(MM, 1)

                        @block.vector
                        def _(vec, mmt=mmt):
                            vec.wait_ge(sem[MM], mmt)
                            vec.tensor_copy(th[:], tot_ps[:]).then_inc(sem[DV], 1)
                            vec.wait_ge(sem[DV], cnt.add(DV, 1))
                            vec.tensor_tensor(tl[:], tot_ps[:], th[:], Alu.subtract
                                              ).then_inc(sem[DV], 1)
                        dvv = cnt.add(DV, 1)

                        @block.tensor
                        def _(ten, dvv=dvv):
                            ten.wait_ge(sem[DV], dvv)
                            ten.matmul(pfx_ps[:], cpfx[:], th[:], start=True, stop=False)
                            ten.matmul(pfx_ps[:], cpfx[:], tl[:], start=False, stop=True
                                       ).then_inc(sem[MM], 1)
                        mmv2 = cnt.add(MM, 1)

                        @block.vector
                        def _(vec, mmv2=mmv2, j=j):
                            vec.wait_ge(sem[MM], mmv2)
                            vec.tensor_copy(pf8[:, j, :], pfx_ps[:]).then_inc(sem[DV], 1)
                            vec.wait_ge(sem[DV], cnt.add(DV, 1))
                            vec.tensor_tensor(pf8[:, j + 1, :], pfx_ps[:], pf8[:, j, :],
                                              Alu.subtract).then_inc(sem[DV], 1)
                        cnt.add(DV, 1)
            pf_dv = cnt.val(DV)

            cw_ps = ph.enter_context(nc.psum_tensor("cw_ps", [128, 1024], F32))
            ckv_ps = ph.enter_context(nc.psum_tensor("ckv_ps", [128, 1024], F32))
            q_ps = ph.enter_context(nc.psum_tensor("q_ps", [128, 1024], F32))

            yr_dv = [0] * (NT + 1)
            fin_dv = [0] * (NT + 1)
            ln_ac = [0] * (NT + 1)
            e1_ac = [0] * (NT + 1)
            sig_ac = [0] * (NT + 1)
            inv_ac = [0] * (NT + 1)
            y2Tsy = [0] * (NT + 1)

            for c in range(NT):
                @block.tensor
                def _(ten, c=c):
                    if c == 0:
                        ten.wait_ge(sem[DV], pf_dv)
                    else:
                        ten.wait_ge(sem[DV], yr_dv[c - 1])
                    for sl in range(2):
                        ten.matmul(cw_ps[:, sl * 512:(sl + 1) * 512], cL[:],
                                   wbuf[:, c, sl * 512:(sl + 1) * 512],
                                   start=True, stop=False)
                        ten.matmul(ckv_ps[:, sl * 512:(sl + 1) * 512], cL[:],
                                   kvbuf[:, c, sl * 512:(sl + 1) * 512],
                                   start=True, stop=False)
                    last = None
                    for sl in range(2):
                        j = (sl * 2) * 2
                        jk = (sl * 2 + 1) * 2
                        ten.matmul(cw_ps[:, sl * 512:(sl + 1) * 512], crow[:, c, :],
                                   pf8[:, j, :], start=False, stop=False)
                        ten.matmul(cw_ps[:, sl * 512:(sl + 1) * 512], crow[:, c, :],
                                   pf8[:, j + 1, :], start=False, stop=True)
                        ten.matmul(ckv_ps[:, sl * 512:(sl + 1) * 512], crow[:, c, :],
                                   pf8[:, jk, :], start=False, stop=False)
                        last = ten.matmul(ckv_ps[:, sl * 512:(sl + 1) * 512],
                                          crow[:, c, :], pf8[:, jk + 1, :],
                                          start=False, stop=True)
                    last.then_inc(sem[MM], 1)
                mmcs[c] = cnt.add(MM, 1)

                @block.tensor
                def _(ten, c=c):
                    if c == 0:
                        ten.wait_ge(sem[GQ], winq_gq)
                    else:
                        ten.wait_ge(sem[AC], e1_ac[c - 1])
                    last = None
                    for dc in range(8):
                        st = xnT[:, dc, c * 128:(c + 1) * 128]
                        for n in range(2):
                            last = ten.matmul(q_ps[:, n * 512:(n + 1) * 512], st,
                                              win_q[:, dc, n * 512:(n + 1) * 512],
                                              start=(dc == 0), stop=(dc == 7))
                    last.then_inc(sem[MM], 1)
                mmq = cnt.add(MM, 1)

                @block.scalar
                def _(scal, c=c, mmq=mmq, yw=yr_dv[c - 2] if c >= 2 else 0,
                      fw=fin_dv[c - 1] if c >= 1 else 0,
                      sw=sig_ac[c - 1] if c >= 1 else 0):
                    scal.wait_ge(sem[MM], mmq)
                    if yw:
                        scal.wait_ge(sem[DV], yw)
                    if fw:
                        scal.wait_ge(sem[DV], fw)
                    if sw:
                        scal.wait_ge(sem[AC], sw)
                    scal.activation(e1s[c % 2][:], q_ps[:], Act.Square,
                                    accum_out=ssq_b[:])
                    scal.activation(lnscr[:], cw_ps[:], Act.Ln, bias=c_eps6[:])
                    scal.activation(ebuf[c % 2][:], lnscr[:], Act.Exp,
                                    scale=c_none[:]).then_inc(sem[AC], 1)
                    cnt.add(AC, 1)
                    scal.activation(lsc_b[:], ssq_b[:], Act.Ln,
                                    scale=c_i1024[:], bias=c_rmse[:]).then_inc(sem[AC], 1)
                    scal.wait_ge(sem[AC], cnt.add(AC, 1))
                    scal.activation(inv_b[:], lsc_b[:], Act.Exp, scale=c_nhalf[:]
                                    ).then_inc(sem[AC], 1)
                    scal.wait_ge(sem[AC], cnt.add(AC, 1))
                    scal.activation(e1s[c % 2][:], q_ps[:], Act.Exp, scale=inv_b[:]
                                    ).then_inc(sem[AC], 1)
                    cnt.add(AC, 1)
                    scal.activation(lnscr[:], e1s[c % 2][:], Act.Ln, bias=c_one[:])
                    scal.activation(e1s[c % 2][:], lnscr[:], Act.Exp, scale=c_none[:]
                                    ).then_inc(sem[AC], 1)
                sig_ac[c] = cnt.add(AC, 1)
                e1_ac[c] = sig_ac[c] - 1
                ln_ac[c] = sig_ac[c] - 4

                @block.vector
                def _(vec, c=c, la=ln_ac[c], pT=y2Tsy[c - 2] if c >= 2 else 0):
                    vec.wait_ge(sem[AC], la)
                    if pT:
                        vec.wait_ge(sem[SY], pT)
                    vec.tensor_tensor(y2c[:, c % 2, :], ckv_ps[:], ebuf[c % 2][:],
                                      Alu.mult).then_inc(sem[DV], 1)
                yr_dv[c] = cnt.add(DV, 1)

                @block.vector
                def _(vec, c=c, sa=sig_ac[c]):
                    vec.wait_ge(sem[AC], sa)
                    vec.tensor_tensor(lnscr[:], y2c[:, c % 2, :], e1s[c % 2][:],
                                      Alu.mult)
                    vec.tensor_tensor(y2c[:, c % 2, :], y2c[:, c % 2, :], lnscr[:],
                                      Alu.subtract).then_inc(sem[DV], 1)
                fin_dv[c] = cnt.add(DV, 1)

                @block.sync
                def _(sync, c=c, fv=fin_dv[c]):
                    sync.wait_ge(sem[DV], fv)
                    sync.dma_start_transpose(
                        out=y2T[:, :, c * 128:(c + 1) * 128],
                        in_=y2c[:, c % 2, :].rearrange("p (a b) -> p a b", a=8)
                    ).then_inc(sem[SY], 16)
                    cnt.add(SY, 16)
                y2Tsy[c] = cnt.val(SY)
        y2_dv = fin_dv[NT - 1]
        a5q_mm_end = cnt.val(MM)

        win_sw = place("win_sw", [128, 8, 2048], BF16)

        @block.gpsimd
        def _(gp):
            for dc in range(8):
                gp.wait_ge(sem[MM], mmcs[2 * dc + 1])
                gp.dma_start(
                    out=win_sw[:, dc, :],
                    in_=wswT_d.rearrange("(dc p) e -> p dc e", p=128)[:, dc, :]
                ).then_inc(sem[GQ], 16)
                cnt.add(GQ, 16)
        winsw_gq = cnt.val(GQ)

        win_o = place("win_o", [128, 8, 1024], BF16)
        xc2 = place("xc2", [128, 2, 1024], F32)

        @block.gpsimd
        def _(gp):
            gp.wait_ge(sem[MM], a5q_mm_end)
            gp.wait_ge(sem[DV], fin_dv[NT - 1])
            for dc in range(8):
                gp.dma_start(
                    out=win_o[:, dc, :],
                    in_=woutT_d.rearrange("(dc p) e -> p dc e", p=128)[:, dc, :]
                ).then_inc(sem[GQ], 16)
                cnt.add(GQ, 16)
        wino_gq = cnt.val(GQ)

        x2gq = [0] * (NT + 1)

        @block.gpsimd
        def _(gp):
            for c in range(2):
                gp.dma_start(out=xc2[:, c, :], in_=x_d[c * 128:(c + 1) * 128, :]
                             ).then_inc(sem[GQ], 16)
                x2gq[c] = cnt.add(GQ, 16)

        @block.gpsimd
        def _(gp):
            gp.wait_ge(sem[CC], cc_done)
            for hs in range(8):
                gp.dma_start(
                    out=ag_d2.rearrange("i (hs hl) -> i hs hl", hs=8)[:, hs, :],
                    in_=ag_d[hs * I_FFN:(hs + 1) * I_FFN, :]).then_inc(sem[GQ], 16)
                cnt.add(GQ, 16)
        agd2_gq = cnt.val(GQ)

        gs_a6 = place("gs_a6", [128, 1024], BF16)
        zc = place("zc", [128, 2, 1024], BF16)
        zT = place("zT", [128, 8, 2048], BF16)

        ztt = [0] * (NT + 1)
        zTsy = [0] * (NT + 1)
        silu6 = [0] * (NT + 1)
        with ExitStack() as ph:
            uv_ps = [ph.enter_context(nc.psum_tensor("uv_ps%d" % i, [128, 2048], F32))
                     for i in range(2)]
            for c in range(NT):
                b = c % 2

                @block.tensor
                def _(ten, c=c, b=b, pT=y2Tsy[c]):
                    ten.wait_ge(sem[SY], pT)
                    if c == 0:
                        ten.wait_ge(sem[DV], y2_dv)
                        ten.wait_ge(sem[GQ], winsw_gq)
                    if c >= 2:
                        ten.wait_ge(sem[DV], ztt[c - 2])
                    last = None
                    for dc in range(8):
                        st = y2T[:, dc, c * 128:(c + 1) * 128]
                        for n in range(4):
                            last = ten.matmul(uv_ps[b][:, n * 512:(n + 1) * 512], st,
                                              win_sw[:, dc, n * 512:(n + 1) * 512],
                                              start=(dc == 0), stop=(dc == 7))
                    last.then_inc(sem[MM], 1)
                mmv = cnt.add(MM, 1)

                @block.scalar
                def _(scal, mmv=mmv, b=b, pz=ztt[c - 1] if c >= 1 else 0):
                    scal.wait_ge(sem[MM], mmv)
                    if pz:
                        scal.wait_ge(sem[DV], pz)
                    scal.activation(gs_a6[:], uv_ps[b][:, 1024:2048], Act.Silu
                                    ).then_inc(sem[AC], 1)
                silu6[c] = cnt.add(AC, 1)

                @block.vector
                def _(vec, c=c, b=b, acv=silu6[c],
                      pT=zTsy[c - 2] if c >= 2 else 0, y15=y2Tsy[NT - 1]):
                    vec.wait_ge(sem[AC], acv)
                    if c <= 1:
                        vec.wait_ge(sem[SY], y15)
                    if pT:
                        vec.wait_ge(sem[SY], pT)
                    vec.tensor_tensor(zc[:, c % 2, :], uv_ps[b][:, 0:1024],
                                      gs_a6[:], Alu.mult).then_inc(sem[DV], 1)
                ztt[c] = cnt.add(DV, 1)

                @block.sync
                def _(sync, c=c, zv=ztt[c]):
                    sync.wait_ge(sem[DV], zv)
                    sync.dma_start_transpose(
                        out=zT[:, :, c * 128:(c + 1) * 128],
                        in_=zc[:, c % 2, :].rearrange("p (a b) -> p a b", a=8)
                    ).then_inc(sem[SY], 16)
                    cnt.add(SY, 16)
                zTsy[c] = cnt.val(SY)
        z_dv = ztt[NT - 1]
        a6_mm_end = cnt.val(MM)

        scrA7 = place("scrA7", [128, 1024], F32)
        x1buf = place("x1buf", [128, 16, 1024], F32)

        x1tt = [0] * (NT + 1)
        with ExitStack() as ph:
            a_ps = [ph.enter_context(nc.psum_tensor("a_ps%d" % i, [128, 1024], F32))
                    for i in range(4)]
            for c in range(NT):
                b = c % 4

                @block.tensor
                def _(ten, c=c, b=b, pT=zTsy[c]):
                    ten.wait_ge(sem[SY], pT)
                    if c == 0:
                        ten.wait_ge(sem[GQ], wino_gq)
                    if c <= 3:
                        ten.wait_ge(sem[DV], z_dv)
                    if c >= 4:
                        ten.wait_ge(sem[DV], x1tt[c - 4])
                    last = None
                    for dc in range(8):
                        st = zT[:, dc, c * 128:(c + 1) * 128]
                        for n in range(2):
                            last = ten.matmul(a_ps[b][:, n * 512:(n + 1) * 512], st,
                                              win_o[:, dc, n * 512:(n + 1) * 512],
                                              start=(dc == 0), stop=(dc == 7))
                    last.then_inc(sem[MM], 1)
                mmv = cnt.add(MM, 1)

                @block.vector
                def _(vec, c=c, b=b, mmv=mmv, xg=x2gq[c]):
                    vec.wait_ge(sem[MM], mmv)
                    vec.wait_ge(sem[GQ], xg)
                    vec.scalar_tensor_tensor(x1buf[:, c, :], xc2[:, c % 2, :],
                                             2.0, a_ps[b][:], Alu.mult, Alu.add
                                             ).then_inc(sem[DV], 1)
                x1tt[c] = cnt.add(DV, 1)

                if c + 2 < NT:
                    @block.gpsimd
                    def _(gp, c=c, px=x1tt[c]):
                        gp.wait_ge(sem[DV], px)
                        gp.dma_start(out=xc2[:, c % 2, :],
                                     in_=x_d[(c + 2) * 128:(c + 3) * 128, :]
                                     ).then_inc(sem[GQ], 16)
                        x2gq[c + 2] = cnt.add(GQ, 16)

                @block.scalar
                def _(scal, c=c, px=x1tt[c]):
                    scal.wait_ge(sem[DV], px)
                    scal.activation(scrA7[:], x1buf[:, c, :], Act.Copy,
                                    accum_out=st_sum[:, c:c + 1]).then_inc(sem[AC], 1)
                cnt.add(AC, 1)

                @block.vector
                def _(vec, c=c, px=x1tt[c]):
                    vec.wait_ge(sem[DV], px)
                    vec.tensor_tensor_reduce(
                        out=scrA7[:], in0=x1buf[:, c, :], in1=x1buf[:, c, :],
                        scale=1.0, scalar=0.0, op0=Alu.mult, op1=Alu.add,
                        accum_out=st_sq[:, c:c + 1]).then_inc(sem[DV], 1)
                cnt.add(DV, 1)
        a7_ac = cnt.val(AC)
        a7_dv2 = cnt.val(DV)
        a7_mm_end = cnt.val(MM)

        def emit_ln_batch(prev_ac_total, prev_dv_total, a=0, b=16):
            @block.vector
            def _(vec):
                vec.wait_ge(sem[AC], prev_ac_total)
                vec.wait_ge(sem[DV], prev_dv_total)
                vec.tensor_scalar(st_mean[:, a:b], st_sum[:, a:b],
                                  1.0 / 1024.0, None, Alu.mult).then_inc(sem[DV], 1)
                vec.wait_ge(sem[DV], cnt.add(DV, 1))
                vec.tensor_scalar(st_var[:, a:b], st_sq[:, a:b],
                                  1.0 / 1024.0, None, Alu.mult).then_inc(sem[DV], 1)
                vec.wait_ge(sem[DV], cnt.add(DV, 1))
                vec.tensor_tensor(st_sq[:, a:b], st_mean[:, a:b], st_mean[:, a:b],
                                  Alu.mult).then_inc(sem[DV], 1)
                vec.wait_ge(sem[DV], cnt.add(DV, 1))
                vec.tensor_tensor(st_var[:, a:b], st_var[:, a:b], st_sq[:, a:b],
                                  Alu.subtract).then_inc(sem[DV], 1)
            dvv = cnt.add(DV, 1)

            @block.scalar
            def _(scal, dvv=dvv):
                scal.wait_ge(sem[DV], dvv)
                scal.activation(st_sq[:, a:b], st_var[:, a:b], Act.Ln, bias=c_lneps[:]
                                ).then_inc(sem[AC], 1)
                scal.wait_ge(sem[AC], cnt.add(AC, 1))
                scal.activation(st_rstd[:, a:b], st_sq[:, a:b], Act.Exp,
                                scale=c_nhalf[:]).then_inc(sem[AC], 1)
            return cnt.add(AC, 1)

        rstd_ac = emit_ln_batch(a7_ac, a7_dv2)

        x1ap = [0] * (NT + 1)

        @block.vector
        def _(vec):
            vec.wait_ge(sem[AC], rstd_ac)
            for c in range(NT):
                vec.tensor_scalar(x1buf[:, c, :], x1buf[:, c, :],
                                  st_mean[:, c:c + 1], st_rstd[:, c:c + 1],
                                  Alu.subtract, Alu.mult).then_inc(sem[DV], 1)
                x1ap[c] = cnt.add(DV, 1)
        x1_dv = x1ap[NT - 1]

        wg_win = place("wg_win", [128, 8, 1408], BF16)
        wu_win = place("wu_win", [128, 8, 1408], BF16)

        @block.gpsimd
        def _(gp):
            gp.wait_ge(sem[CC], cc_done)
            gp.wait_ge(sem[MM], a6_mm_end)
            for dc in range(8):
                gp.dma_start(out=wu_win[:, dc, :],
                             in_=ag_u.rearrange("(dc p) i -> p dc i", p=128)
                             [:, dc, 0:1408]).then_inc(sem[GQ], 16)
                cnt.add(GQ, 16)
            gp.wait_ge(sem[MM], a7_mm_end)
            for dc in range(8):
                gp.dma_start(out=wg_win[:, dc, :],
                             in_=ag_g.rearrange("(dc p) i -> p dc i", p=128)
                             [:, dc, 0:1408]).then_inc(sem[GQ], 16)
                cnt.add(GQ, 16)
        wgwu_gq = cnt.val(GQ)

        xqc = place("xqc", [128, 2, 1024], BF16)
        xqs = [place("xqs0", [128, 8, 128], BF16), place("xqs1", [128, 8, 128], BF16)]
        xqw = place("xqw", [128, 2, 8, 128], BF16)
        wd_win = place("wd_win", [128, 22, 1024], BF16)
        hqw = place("hqw", [128, 2, 22, 128], BF16)
        outb = place("outb", [128, 2, 1024], F32)
        hq_sy = [0] * (NT + 1)
        wd_sy = [0] * 22
        hh = place("hh", [128, 1408], F32)
        gs2 = place("gs2", [128, 1408], BF16)
        hqs = place("hqs", [128, 11, 128], BF16)
        hqt = place("hqt", [128, 11, 128], BF16)

        f1Tsy = [0] * (NT + 1)
        f1store = [0] * (NT + 1)

        def emit_f1_pair(cp):
            w0 = [(DV, x1ap[cp + 1])]
            gw = [(SY, f1Tsy[cp - 1])] if cp >= 2 else []
            qk, gpq = emit_quant(
                x1buf[:, cp:cp + 2, :].rearrange("p a d -> p (a d)"),
                x1buf[:, cp:cp + 2, :].rearrange("p a (b k) -> p (a b) k", k=16),
                xqc[:, :, :].rearrange("p a d -> p (a d)"), 128, 2048,
                dv_wait=w0, ac_wait=w0, gp_wait=gw, out_on_dve=True)

            @block.sync
            def _(sync, cp=cp, qk=qk, gpq=gpq,
                  ps=f1store[cp - 1] if cp >= 2 else 0):
                sync.wait_ge(sem[qk], gpq)
                if ps:
                    sync.wait_ge(sem[SY], ps)
                for c in (cp, cp + 1):
                    sync.dma_start_transpose(
                        out=xqs[c % 2][:],
                        in_=xqc[:, c % 2, :].rearrange("p (a b) -> p a b", a=8)
                    ).then_inc(sem[SY], 16)
                    f1Tsy[c] = cnt.add(SY, 16)
                    sync.wait_ge(sem[SY], f1Tsy[c])
                    sync.dma_start(
                        out=xqT_dr[c].rearrange("(dc p) t -> p dc t", p=128),
                        in_=xqs[c % 2][:]).then_inc(sem[SY], 16)
                    f1store[c] = cnt.add(SY, 16)

        f2w_sy = [0, 0]
        f2w_s = [0] * 8
        f2w_g = [0] * 8
        SLICES = ((0, 512), (512, 1024), (1024, 1408))

        def emit_f2_chunk(ih, c, S):
            @block.tensor
            def _(ten, c=c, ih=ih, pq=S["xq_sy"][c], pa=S["silu_ac"][c - 1] if c >= 1 else 0,
                  ws=list(f2w_s), wg=list(f2w_g), psil=S.get("prev_sil", 0)):
                ten.wait_ge(sem[SY], pq)
                if c == 0 and ih == 0:
                    ten.wait_ge(sem[GQ], wgwu_gq)
                if c == 0 and psil:
                    ten.wait_ge(sem[AC], psil)
                if c <= 1 and ih == 1:
                    ten.wait_ge(sem[SY], ws[3])
                    ten.wait_ge(sem[GQ], wg[7])
                if pa:
                    ten.wait_ge(sem[AC], pa)
                last = None
                for dc in range(8):
                    st = xqw[:, c % 2, dc, :]
                    for (a, b2) in SLICES:
                        last = ten.matmul(g_ps[:, a:b2], st, wg_win[:, dc, a:b2],
                                          start=(dc == 0), stop=(dc == 7))
                last.then_inc(sem[MM], 1)
            mmg = cnt.add(MM, 1)

            @block.tensor
            def _(ten, c=c, ph_=S["hh_dv"][c - 1] if c >= 1 else 0,
                  phh=S.get("prev_hh", 0)):
                if ph_:
                    ten.wait_ge(sem[DV], ph_)
                if c == 0 and phh:
                    ten.wait_ge(sem[DV], phh)
                last = None
                for dc in range(8):
                    st = xqw[:, c % 2, dc, :]
                    for (a, b2) in SLICES:
                        last = ten.matmul(u_ps[:, a:b2], st, wu_win[:, dc, a:b2],
                                          start=(dc == 0), stop=(dc == 7))
                last.then_inc(sem[MM], 1)
            mmu = cnt.add(MM, 1)

            if c + 2 < NT:
                @block.sync
                def _(sync, c=c, ih=ih, mmu=mmu,
                      fs=f1store[c + 2] if ih == 0 else 0):
                    sync.wait_ge(sem[MM], mmu)
                    if fs:
                        sync.wait_ge(sem[SY], fs)
                    sync.dma_start(
                        out=xqw[:, c % 2, :, :],
                        in_=xqT_dr[c + 2].rearrange("(dc p) t -> p dc t", p=128)
                        ).then_inc(sem[SY], 16)
                    S["xq_sy"][c + 2] = cnt.add(SY, 16)

            @block.scalar
            def _(scal, mmg=mmg, ph_=S["hh_dv"][c - 1] if c >= 1 else 0):
                scal.wait_ge(sem[MM], mmg)
                if ph_:
                    scal.wait_ge(sem[DV], ph_)
                scal.activation(gs2[:], g_ps[:], Act.Silu).then_inc(sem[AC], 1)
            S["silu_ac"][c] = cnt.add(AC, 1)

            @block.vector
            def _(vec, mmu=mmu, psg=S["sgn_ac"][c - 1] if c >= 1 else 0,
                  sa=S["silu_ac"][c]):
                vec.wait_ge(sem[MM], mmu)
                vec.wait_ge(sem[AC], sa)
                if psg:
                    vec.wait_ge(sem[AC], psg)
                vec.tensor_tensor(hh[:], u_ps[:], gs2[:], Alu.mult
                                  ).then_inc(sem[DV], 1)
            S["hh_dv"][c] = cnt.add(DV, 1)

            S["sgn_ac"][c] = cnt.val(AC) + 1
            qk, gpq = emit_quant(hh[:], hh[:].rearrange("p (b k) -> p b k", k=16),
                                 hqs[:].rearrange("p a b -> p (a b)"), 88, 1408,
                                 dv_wait=[(DV, S["hh_dv"][c])],
                                 ac_wait=[(DV, S["hh_dv"][c])],
                                 gp_wait=[(SY, S["hq_t"][c - 1])] if c >= 1 else [],
                                 m_on_gp=True, gset=c % 2)

            @block.sync
            def _(sync, c=c, ih=ih, qk=qk, gpq=gpq):
                sync.wait_ge(sem[qk], gpq)
                sync.wait_ge(sem[SY], cnt.val(SY))
                sync.dma_start_transpose(out=hqt[:], in_=hqs[:]
                                         ).then_inc(sem[SY], 16)
                S["hq_t"][c] = cnt.add(SY, 16)
                sync.wait_ge(sem[SY], S["hq_t"][c])
                sync.dma_start(
                    out=hqT_dr[c].rearrange("(it p) t -> p it t", p=128)
                    [:, ih * 11:(ih + 1) * 11, :],
                    in_=hqt[:]).then_inc(sem[SY], 16)
                S["hq_st"][c] = cnt.add(SY, 16)

        with ExitStack() as ph:
            g_ps = ph.enter_context(nc.psum_tensor("g_ps", [128, 1408], F32))
            u_ps = ph.enter_context(nc.psum_tensor("u_ps", [128, 1408], F32))

            S0 = {"xq_sy": [0] * (NT + 1), "silu_ac": [0] * (NT + 1),
                  "hh_dv": [0] * (NT + 1), "hq_t": [0] * (NT + 1),
                  "hq_st": [0] * (NT + 1), "sgn_ac": [0] * (NT + 1)}
            for cp in range(0, NT, 2):
                emit_f1_pair(cp)
                if cp == 0:
                    @block.sync
                    def _(sync):
                        for cc in range(2):
                            sync.wait_ge(sem[SY], f1store[cc])
                            sync.dma_start(
                                out=xqw[:, cc % 2, :, :],
                                in_=xqT_dr[cc].rearrange("(dc p) t -> p dc t", p=128)
                                ).then_inc(sem[SY], 16)
                            S0["xq_sy"][cc] = cnt.add(SY, 16)
                if cp >= 2:
                    emit_f2_chunk(0, cp - 2, S0)
                    emit_f2_chunk(0, cp - 1, S0)
            emit_f2_chunk(0, NT - 2, S0)

            pm_ih0 = cnt.val(MM) + 2

            S1 = {"xq_sy": [0] * (NT + 1), "silu_ac": [0] * (NT + 1),
                  "hh_dv": [0] * (NT + 1), "hq_t": [0] * (NT + 1),
                  "hq_st": [0] * (NT + 1), "sgn_ac": [0] * (NT + 1)}

            @block.sync
            def _(sync, pm=pm_ih0):
                sync.wait_ge(sem[MM], pm)
                for cc in range(2):
                    sync.dma_start(
                        out=xqw[:, cc % 2, :, :],
                        in_=xqT_dr[cc].rearrange("(dc p) t -> p dc t", p=128)
                        ).then_inc(sem[SY], 16)
                    S1["xq_sy"][cc] = cnt.add(SY, 16)
                for dc in range(4):
                    sync.dma_start(out=wg_win[:, dc, :],
                                   in_=ag_g.rearrange("(dc p) i -> p dc i", p=128)
                                   [:, dc, 1408:2816]).then_inc(sem[SY], 16)
                    cnt.add(SY, 16)
                    sync.dma_start(out=wu_win[:, dc, :],
                                   in_=ag_u.rearrange("(dc p) i -> p dc i", p=128)
                                   [:, dc, 1408:2816]).then_inc(sem[SY], 16)
                    f2w_s[dc] = cnt.add(SY, 16)

            @block.gpsimd
            def _(gp, pm=pm_ih0):
                gp.wait_ge(sem[MM], pm)
                for dc in range(4, 8):
                    gp.dma_start(out=wg_win[:, dc, :],
                                 in_=ag_g.rearrange("(dc p) i -> p dc i", p=128)
                                 [:, dc, 1408:2816]).then_inc(sem[GQ], 16)
                    cnt.add(GQ, 16)
                    gp.dma_start(out=wu_win[:, dc, :],
                                 in_=ag_u.rearrange("(dc p) i -> p dc i", p=128)
                                 [:, dc, 1408:2816]).then_inc(sem[GQ], 16)
                    f2w_g[dc] = cnt.add(GQ, 16)

            emit_f2_chunk(0, NT - 1, S0)
            f2w_sy[1] = f2w_s[3]
            S1["prev_sil"] = S0["silu_ac"][NT - 1]
            S1["prev_hh"] = S0["hh_dv"][NT - 1]
            for c in range(NT - 1):
                emit_f2_chunk(1, c, S1)

            pm_f2 = cnt.val(MM) + 2

            @block.sync
            def _(sync, pm=pm_f2):
                sync.wait_ge(sem[MM], pm)
                sync.wait_ge(sem[GQ], agd2_gq)
                for c in range(2):
                    sync.wait_ge(sem[SY], S1["hq_st"][c])
                    sync.dma_start(
                        out=hqw[:, c % 2, :, :],
                        in_=hqT_dr[c].rearrange("(it p) t -> p it t", p=128)
                        ).then_inc(sem[SY], 16)
                    hq_sy[c] = cnt.add(SY, 16)
                for ic in range(11):
                    sync.dma_start(
                        out=wd_win[:, ic, :],
                        in_=ag_d2[ic * 128:(ic + 1) * 128, :]).then_inc(sem[SY], 16)
                    wd_sy[ic] = cnt.add(SY, 16)

            emit_f2_chunk(1, NT - 1, S1)
        hq_done_sy = cnt.val(SY)
        f2_gp = cnt.val(GP)

        wd_gq = [0] * 22

        @block.gpsimd
        def _(gp):
            gp.wait_ge(sem[SY], hq_done_sy)
            gp.wait_ge(sem[GQ], agd2_gq)
            for ic in range(11, 22):
                gp.dma_start(
                    out=wd_win[:, ic, :],
                    in_=ag_d2[ic * 128:(ic + 1) * 128, :]).then_inc(sem[GQ], 16)
                wd_gq[ic] = cnt.add(GQ, 16)

        out_gq = [0] * (NT + 1)

        def emit_out_half(a, b, rstd_v):
            for c in range(a, b):
                @block.vector
                def _(vec, c=c, ra=rstd_v, po=out_gq[c - 2] if c - 2 >= 0 else 0):
                    vec.wait_ge(sem[AC], ra)
                    if po:
                        vec.wait_ge(sem[GQ], po)
                    vec.tensor_scalar(outb[:, c % 2, :], x1buf[:, c, :],
                                      st_mean[:, c:c + 1], st_rstd[:, c:c + 1],
                                      Alu.subtract, Alu.mult).then_inc(sem[DV], 1)
                dvv = cnt.add(DV, 1)

                @block.gpsimd
                def _(gp, c=c, dvv=dvv):
                    gp.wait_ge(sem[DV], dvv)
                    gp.dma_start(out=out_d[c * 128:(c + 1) * 128, :],
                                 in_=outb[:, c % 2, :]).then_inc(sem[GQ], 16)
                    out_gq[c] = cnt.add(GQ, 16)

        res_dv = [0] * (NT + 1)
        with ExitStack() as ph:
            d_ps = [ph.enter_context(nc.psum_tensor("d_ps%d" % i, [128, 1024], F32))
                    for i in range(2)]
            for c in range(NT):
                b = c % 2

                @block.tensor
                def _(ten, c=c, b=b, pq=hq_sy[c]):
                    ten.wait_ge(sem[SY], pq)
                    if c >= 2:
                        ten.wait_ge(sem[DV], res_dv[c - 2])
                    if c <= 1:
                        ten.wait_ge(sem[SY], wd_sy[10])
                        ten.wait_ge(sem[GQ], wd_gq[21])
                    last = None
                    for ic in range(22):
                        st = hqw[:, c % 2, ic, :]
                        for n in range(2):
                            last = ten.matmul(d_ps[b][:, n * 512:(n + 1) * 512], st,
                                              wd_win[:, ic, n * 512:(n + 1) * 512],
                                              start=(ic == 0), stop=(ic == 21))
                    last.then_inc(sem[MM], 1)
                mmv = cnt.add(MM, 1)

                @block.vector
                def _(vec, c=c, b=b, mmv=mmv):
                    vec.wait_ge(sem[MM], mmv)
                    vec.tensor_tensor(x1buf[:, c, :], x1buf[:, c, :], d_ps[b][:],
                                      Alu.add).then_inc(sem[DV], 1)
                res_dv[c] = cnt.add(DV, 1)

                if c + 2 < NT:
                    @block.sync
                    def _(sync, c=c, pr=res_dv[c]):
                        sync.wait_ge(sem[DV], pr)
                        sync.dma_start(
                            out=hqw[:, c % 2, :, :],
                            in_=hqT_dr[c + 2].rearrange("(it p) t -> p it t", p=128)
                            ).then_inc(sem[SY], 16)
                        hq_sy[c + 2] = cnt.add(SY, 16)

                @block.scalar
                def _(scal, c=c, pr=res_dv[c]):
                    scal.wait_ge(sem[DV], pr)
                    scal.activation(aux1k[:], x1buf[:, c, :], Act.Copy,
                                    accum_out=st_sum[:, c:c + 1]).then_inc(sem[AC], 1)
                cnt.add(AC, 1)

                @block.vector
                def _(vec, c=c, pr=res_dv[c]):
                    vec.wait_ge(sem[DV], pr)
                    vec.tensor_tensor_reduce(
                        out=aux1k[:], in0=x1buf[:, c, :], in1=x1buf[:, c, :],
                        scale=1.0, scalar=0.0, op0=Alu.mult, op1=Alu.add,
                        accum_out=st_sq[:, c:c + 1]).then_inc(sem[DV], 1)
                cnt.add(DV, 1)

                if c == 7:
                    rstdA = emit_ln_batch(cnt.val(AC), cnt.val(DV), 0, 8)
                    emit_out_half(0, 8, rstdA)
        rstdB = emit_ln_batch(cnt.val(AC), cnt.val(DV), 8, 16)
        emit_out_half(8, 16, rstdB)

        @block.sync
        def _(sync):
            sync.wait_ge(sem[SY], cnt.val(SY))
            sync.wait_ge(sem[GQ], cnt.val(GQ))

    return nc


def kernel(L0_attn_x, L0_attn_w_qkv, L0_attn_w_swiglu, L0_attn_w_out,
           L0_ffn_w_gate, L0_ffn_w_up, L0_ffn_w_down):
    bf16 = ml_dtypes.bfloat16
    x = np.ascontiguousarray(np.asarray(L0_attn_x, np.float32))
    wqkvT = np.ascontiguousarray(np.asarray(L0_attn_w_qkv, np.float32).T)
    wswT = np.ascontiguousarray(np.asarray(L0_attn_w_swiglu, np.float32).T)
    woutT = np.ascontiguousarray(np.asarray(L0_attn_w_out, np.float32).T)
    wg = np.asarray(L0_ffn_w_gate, np.float32)
    wu = np.asarray(L0_ffn_w_up, np.float32)
    wd = np.asarray(L0_ffn_w_down, np.float32)

    L128 = np.triu(np.ones((128, 128), np.float32)).astype(bf16)
    colsel = np.zeros((128, 16, 16), np.float32)
    for c in range(16):
        colsel[:, c, c] = 1.0
    colsel = colsel.astype(bf16)
    rowsel = np.zeros((16, 16, 128), np.float32)
    for c in range(16):
        rowsel[c, c, :] = 1.0
    rowsel = rowsel.astype(bf16)
    pfx16 = np.zeros((16, 16), np.float32)
    for c in range(16):
        pfx16[:c, c] = 1.0
    pfx16 = pfx16.astype(bf16)
    ident = np.eye(128, dtype=np.float32).astype(bf16)

    nc = build_program()
    mybir.codegen_inst_isa_subclasses(nc)

    in_maps = []
    for core in range(8):
        in_maps.append({
            "x": np.ascontiguousarray(x[core]),
            "wqkvT": wqkvT,
            "wswT": wswT,
            "woutT": woutT,
            "wg_slab": np.ascontiguousarray(
                wg[:, core * 128:(core + 1) * 128].reshape(22, 128, 128)
                .transpose(1, 0, 2).reshape(128, I_FFN)),
            "wu_slab": np.ascontiguousarray(
                wu[:, core * 128:(core + 1) * 128].reshape(22, 128, 128)
                .transpose(1, 0, 2).reshape(128, I_FFN)),
            "wd_slab": np.ascontiguousarray(wd[core * 128:(core + 1) * 128, :]),
            "cL128": L128,
            "ccolsel": colsel,
            "crowsel": rowsel,
            "cpfx17": pfx16,
            "cident": ident,
        })
    res = run_bass_kernel_spmd(nc, in_maps, core_ids=list(range(8)))
    out = np.stack([np.asarray(res.results[i]["out"], np.float32) for i in range(8)], 0)
    return out
